# version 05
# baseline (speedup 1.0000x reference)
import os
import numpy as np
import concourse.bass as bass
import concourse.mybir as mybir
from concourse.alu_op_type import AluOpType as OP
from concourse.bass_utils import run_bass_kernel_spmd

F32 = mybir.dt.float32
BF16 = mybir.dt.bfloat16
AF = mybir.ActivationFunctionType
AX = mybir.AxisListType.X

S, D, E, HP, DH = 4096, 1024, 64, 4, 4096
T = S // 2
EPS = 1e-5
MAGIC = float(np.float32(1.5 * 2 ** 23))
NT = S // 128
FFN_CHUNKS = [(i * 254, 254) for i in range(8)] + [(2032, 16)]
ENGS = ("tensor", "vector", "scalar", "sync", "gpsimd")
GROUPS = [[0, 1], [2, 3], [4, 5], [6, 7]]

_NC_CACHE = [None]


def build_program(upto=99, ksub=10**9, dbg=False):
    nc = bass.Bass("TRN2", target_bir_lowering=False)
    x_in = nc.dram_tensor("x", [S, D], F32, kind="ExternalInput").ap()
    wqp_in = nc.dram_tensor("wqp", [128, HP, 8, 128], F32, kind="ExternalInput").ap()
    wkp_in = nc.dram_tensor("wkp", [128, HP, 8, 128], F32, kind="ExternalInput").ap()
    wv_in = nc.dram_tensor("wv8", [128, 8, 512], F32, kind="ExternalInput").ap()
    wo_in = nc.dram_tensor("wo8", [128, HP, 1024], F32, kind="ExternalInput").ap()
    lnw_in = nc.dram_tensor("lnw_bc", [128, D], F32, kind="ExternalInput").ap()
    lnb_in = nc.dram_tensor("lnb_bc", [128, D], F32, kind="ExternalInput").ap()
    w1T_in = nc.dram_tensor("w1T", [D, DH], F32, kind="ExternalInput").ap()
    w2T_in = nc.dram_tensor("w2T", [DH, D], F32, kind="ExternalInput").ap()
    diag_in = nc.dram_tensor("diag_in", [128, 32, 4, 64], F32, kind="ExternalInput").ap()
    wsel_in = nc.dram_tensor("wsel", [4, 2], F32, kind="ExternalInput").ap()
    out = nc.dram_tensor("out", [T, D], F32, kind="ExternalOutput").ap()
    cin = nc.dram_tensor("cin", [S, D], F32, kind="Internal").ap()
    rs_pad = nc.dram_tensor("rs_pad", [T + 2, D], F32, kind="Internal", addr_space="Local").ap()
    ag_in = nc.dram_tensor("ag_in", [2, D], F32, kind="Internal").ap()
    ag_out = nc.dram_tensor("ag_out", [4, D], F32, kind="Internal", addr_space="Local").ap()
    w1q_h = nc.dram_tensor("w1q_h", [D, DH], BF16, kind="Internal").ap()
    w2q_h = nc.dram_tensor("w2q_h", [DH, D], BF16, kind="Internal").ap()
    if dbg:
        dbg_y = nc.dram_tensor("dbg_y", [128, 8, 512], BF16, kind="ExternalOutput").ap()
        dbg_y2 = nc.dram_tensor("dbg_y2", [128, 8, 512], BF16, kind="ExternalOutput").ap()
        dbg_kq = nc.dram_tensor("dbg_kq", [128, 2, S], BF16, kind="ExternalOutput").ap()
        dbg_at = nc.dram_tensor("dbg_at", [128, HP, S], BF16, kind="ExternalOutput").ap()
        dbg_cin = nc.dram_tensor("dbg_cin", [S, D], F32, kind="ExternalOutput").ap()
        dbg_rs = nc.dram_tensor("dbg_rs", [T + 2, D], F32, kind="ExternalOutput").ap()

    cnt = {}

    def bump(sem, k=1):
        cnt[sem] = cnt.get(sem, 0) + k
        return cnt[sem]

    def val(sem):
        return cnt.get(sem, 0)

    from contextlib import ExitStack
    with ExitStack() as top:
        et = top.enter_context
        sems = {n: et(nc.semaphore(n)) for n in
                ("sd", "sg", "so", "pe", "ve", "ac", "gp", "co",
                 "sdx0", "sdx1", "swq0", "swq1", "sst0", "sst1",
                 "sci0", "sci1", "sgk0", "sgk1")}

        ops = {e: [] for e in ENGS}
        mute = {"on": False}

        def phase_mark(n):
            if n > upto:
                mute["on"] = True

        def submark(k):
            if k > ksub:
                mute["on"] = True

        BLANKET = {"vector": "ve", "scalar": "ac", "gpsimd": "gp"}

        def emit(eng, meth, *args, wait=(), inc=None, inck=1, **kw):
            if mute["on"]:
                return cnt.get(inc, 0)
            if meth == "dma_start":
                inck = 16
            bsem = BLANKET.get(eng)
            if bsem is not None and meth not in ("dma_start", "collective_compute"):
                wait = tuple(wait) + ((bsem, val(bsem)),)
                if inc is None:
                    inc = bsem
            v = bump(inc, inck) if inc else None
            wmax = {}
            for s, w in wait:
                if w > wmax.get(s, 0):
                    wmax[s] = w

            def fn(E, meth=meth, args=args, kw=kw, wait=tuple(wmax.items()),
                   inc=inc, inck=inck):
                for s, w in wait:
                    E.wait_ge(sems[s], w)
                ins = getattr(E, meth)(*args, **kw)
                if inc:
                    ins.then_inc(sems[inc], inck)
            ops[eng].append(fn)
            return v

        def waitonly(eng, *pairs):
            if mute["on"]:
                return
            def fn(E, pairs=tuple(pairs)):
                for s, w in pairs:
                    if w > 0:
                        E.wait_ge(sems[s], w)
            ops[eng].append(fn)

        def finalize_debug(buf):
            if not mute["on"] or mute.get("done"):
                return
            mute["on"] = False
            for sname in ("sd", "sg", "so", "pe", "ve", "ac", "gp", "co", "sdx0", "sdx1", "swq0", "swq1", "sst0", "sst1", "sci0", "sci1", "sgk0", "sgk1"):
                if val(sname) > 0:
                    waitonly("sync", (sname, val(sname)))
            emit("sync", "dma_start", out[0:128, :], buf, inc="so")
            waitonly("sync", ("so", val("so")))
            mute["on"] = True
            mute["done"] = True

        def run_block():
            with nc.Block() as block:
                @block.tensor
                def _(t):
                    for fn in ops["tensor"]:
                        fn(t)

                @block.vector
                def _(v):
                    for fn in ops["vector"]:
                        fn(v)

                @block.scalar
                def _(sc):
                    for fn in ops["scalar"]:
                        fn(sc)

                @block.sync
                def _(sy):
                    for fn in ops["sync"]:
                        fn(sy)

                @block.gpsimd
                def _(g):
                    for fn in ops["gpsimd"]:
                        fn(g)
            for e in ENGS:
                ops[e] = []

        stat_iv = et(nc.sbuf_tensor("stat_iv", [128, NT], F32))
        stat_b1 = et(nc.sbuf_tensor("stat_b1", [128, NT], F32))
        stat_m = et(nc.sbuf_tensor("stat_m", [128, NT], F32))
        stat_R = et(nc.sbuf_tensor("stat_R", [128, NT], F32))
        stat_t1 = et(nc.sbuf_tensor("stat_t1", [128, NT], F32))
        stat_t2 = et(nc.sbuf_tensor("stat_t2", [128, NT], F32))
        stat_t3 = et(nc.sbuf_tensor("stat_t3", [128, NT], F32))
        mbm = et(nc.sbuf_tensor("mbm", [128, 896], BF16))
        idb = et(nc.sbuf_tensor("idb", [128, 128], BF16))
        idf = et(nc.sbuf_tensor("idf", [128, 128], F32))
        esh = et(nc.sbuf_tensor("esh", [128, 128], BF16))
        ones2 = et(nc.sbuf_tensor("ones2", [128, 2], BF16))
        onesr = et(nc.sbuf_tensor("onesr", [1, 128], F32))
        onesr2 = et(nc.sbuf_tensor("onesr2", [65, 128], F32))
        onesc = et(nc.sbuf_tensor("onesc", [128, 1], F32))
        onesb = et(nc.sbuf_tensor("onesb", [128, 256], BF16))
        wselb = et(nc.sbuf_tensor("wselb", [4, 2], F32))
        rowb = et(nc.sbuf_tensor("rowb", [1, 16], F32))
        sw_bc = et(nc.sbuf_tensor("sw_bc", [128, 2], F32))
        swr_bc = et(nc.sbuf_tensor("swr_bc", [128, 2], F32))
        negc = et(nc.sbuf_tensor("negc", [128, 2], F32))
        magicc = et(nc.sbuf_tensor("magicc", [128, 1], F32))
        stgm = et(nc.sbuf_tensor("stgm", [128, 2], F32))
        stgr = et(nc.sbuf_tensor("stgr", [128, 2], F32))
        stgiv = et(nc.sbuf_tensor("stgiv", [128, 2], F32))
        stgb1 = et(nc.sbuf_tensor("stgb1", [128, 2], F32))
        wsums = et(nc.sbuf_tensor("wsums", [128, 49], F32))
        wcol = et(nc.sbuf_tensor("wcol", [128, 2], F32))

        pp_a = et(nc.psum_tensor("pp_a", [128, 1024], F32))
        pp_b = et(nc.psum_tensor("pp_b", [128, 1024], F32))
        pp4 = et(nc.psum_tensor("pp4", [128, 512], F32))
        pp5 = et(nc.psum_tensor("pp5", [128, 512], F32))
        pp6 = et(nc.psum_tensor("pp6", [128, 512], F32))
        pp7 = et(nc.psum_tensor("pp7", [128, 512], F32))

        with ExitStack() as actx:
            ea = actx.enter_context
            y_fm = ea(nc.sbuf_tensor("y_fm", [128, 8, S], BF16))
            v_sb = ea(nc.sbuf_tensor("v_sb", [128, 32, 520], BF16))
            k_hp = ea(nc.sbuf_tensor("k_hp", [128, S], BF16))
            q_hp = ea(nc.sbuf_tensor("q_hp", [128, S], BF16))
            attn = ea(nc.sbuf_tensor("attn", [128, HP, S], BF16))
            wqkp = ea(nc.sbuf_tensor("wqkp_s", [128, 2, 2, 8, 128], BF16))
            x0t = [ea(nc.sbuf_tensor("x0t%d" % i, [128, D], F32)) for i in range(2)]
            s1c = ea(nc.sbuf_tensor("s1c", [128, NT], F32))
            s2c = ea(nc.sbuf_tensor("s2c", [128, NT], F32))
            ptile = ea(nc.sbuf_tensor("ptile", [128, 2, 2, 512], BF16))
            accs = ea(nc.sbuf_tensor("accs", [65, 2, 512], F32))
            kq2 = ea(nc.sbuf_tensor("kq2", [128, 2], F32))
            scr = ea(nc.sbuf_tensor("scr", [128, 8192], F32))
            scr_ap = scr.ap()
            scrb = scr_ap.bitcast(BF16)
            xsq = [scr_ap[:, 0:1024], scr_ap[:, 1024:2048]]
            ytmp = scr_ap[:, 0:1024]
            ytk = [scrb[:, 4096:5120], scrb[:, 5120:6144]]
            lnw = scr_ap[:, 6144:7168]
            lnb = scr_ap[:, 7168:8192]
            wv8 = scrb[:, 8192:12288].rearrange("p (c n) -> p c n", c=8)
            sqs_nb = [scrb[:, 12288:12800], scrb[:, 12800:13312]]
            wbuf = [scr_ap[:, 0:2048], scr_ap[:, 2048:4096]]
            wqo = [scrb[:, 8192:10240], scrb[:, 10240:12288]]
            rlA64 = scr_ap[64:65, 6656:7168]
            rlB64 = scr_ap[64:65, 7680:8192]
            atmp = scrb[0:64, 14336:14848]
            yv = y_fm.ap().rearrange("p a b -> p (a b)")
            yvf = yv.bitcast(F32)
            wo8 = yv[:, 0:4096].rearrange("p (a b) -> p a b", a=HP)
            u25 = yvf[:, 2048:3072]
            n25 = yvf[:, 3072:4096]
            x1p = [yvf[:, 4096:5120], yvf[:, 5120:6144]]
            x1halo = scr_ap[0:2, 0:1024]
            agsb = scr_ap[0:4, 1024:2048]

            emit("sync", "dma_start", lnw[:], lnw_in, inc="sd")
            emit("sync", "dma_start", lnb[:], lnb_in, inc="sd")
            
            emit("sync", "dma_start", wselb[:], wsel_in, inc="sd")
            d_w0 = val("sd")
            kq_load = {}
            for h0 in range(2):
                sk = "sgk%d" % h0
                emit("gpsimd", "dma_start", wqkp[:, h0, 0], wkp_in[:, h0], inc=sk)
                emit("gpsimd", "dma_start", wqkp[:, h0, 1], wqp_in[:, h0], inc=sk)
                kq_load[h0] = val(sk)
            emit("gpsimd", "dma_start", wv8[:, :, :], wv_in, inc="sg")
            d_wg = val("sg")
            emit("gpsimd", "memset", mbm[:], 1.0, inc="gp")
            emit("gpsimd", "affine_select", out=mbm[:], in_=mbm[:], compare_op=OP.is_ge,
                 fill=0.0, base=-384, pattern=[[1, 896]], channel_multiplier=-1,
                 wait=[("gp", val("gp"))])
            emit("gpsimd", "memset", idb[:], 0.0, inc="gp")
            emit("gpsimd", "affine_select", out=idb[:], in_=idb[:], compare_op=OP.not_equal,
                 fill=1.0, base=0, pattern=[[-1, 128]], channel_multiplier=1,
                 wait=[("gp", val("gp"))])
            emit("gpsimd", "memset", idf[:], 0.0, inc="gp")
            emit("gpsimd", "affine_select", out=idf[:], in_=idf[:], compare_op=OP.not_equal,
                 fill=1.0, base=0, pattern=[[-1, 128]], channel_multiplier=1,
                 wait=[("gp", val("gp"))])
            emit("gpsimd", "memset", esh[:], 0.0, inc="gp")
            emit("gpsimd", "affine_select", out=esh[:], in_=esh[:], compare_op=OP.not_equal,
                 fill=1.0, base=64, pattern=[[-1, 128]], channel_multiplier=1,
                 wait=[("gp", val("gp"))])
            emit("gpsimd", "memset", ones2[:], 0.0, inc="gp")
            emit("gpsimd", "memset", ones2[0:64, 0:1], 1.0,
                 wait=[("gp", val("gp"))], inc="gp")
            emit("gpsimd", "memset", ones2[64:128, 1:2], 1.0,
                 wait=[("gp", val("gp"))])
            emit("gpsimd", "memset", onesr[:], 1.0)
            emit("gpsimd", "memset", onesr2[:], 1.0)
            emit("gpsimd", "memset", magicc[:], MAGIC)
            emit("gpsimd", "memset", onesb[:], 1.0)
            emit("gpsimd", "memset", onesc[:], 1.0)
            emit("gpsimd", "memset", v_sb[:], 1.0, inc="gp")
            emit("gpsimd", "memset", kq2[:], 0.0, inc="gp")
            emit("gpsimd", "memset", wsums[:], 0.0, inc="gp")
            gp_consts = val("gp")

            ve_p1a = {}
            for j in range(NT):
                bi = j % 2
                sx = "sdx%d" % bi
                w = [("ve", ve_p1a.get(j - 2, 0))]
                dv = emit("sync", "dma_start", x0t[bi][:], x_in[j * 128:(j + 1) * 128, :],
                          wait=w, inc=sx)
                emit("scalar", "square", xsq[bi][:], x0t[bi][:],
                     wait=[(sx, dv), ("ve", ve_p1a.get(j - 2, 0))], inc="ac")
                a1 = val("ac")
                emit("vector", "tensor_reduce", s1c[:, j:j + 1], x0t[bi][:], AX, OP.add,
                     wait=[(sx, dv)])
                emit("vector", "tensor_reduce", s2c[:, j:j + 1], xsq[bi][:], AX, OP.add,
                     wait=[("ac", a1)], inc="ve")
                ve_p1a[j] = val("ve")

            def newton_rsqrt(dst, src, tmp_a, tmp_b):
                emit("scalar", "sqrt", tmp_a[:], src[:],
                     wait=[("ve", val("ve"))], inc="ac")
                a0 = val("ac")
                emit("vector", "reciprocal", dst[:], tmp_a[:], wait=[("ac", a0)])
                for _ in range(2):
                    emit("vector", "tensor_mul", tmp_b[:], dst[:], dst[:])
                    emit("vector", "tensor_mul", tmp_b[:], tmp_b[:], src[:])
                    emit("vector", "tensor_scalar", tmp_b[:], tmp_b[:], -0.5, 1.5,
                         OP.mult, OP.add)
                    emit("vector", "tensor_mul", dst[:], dst[:], tmp_b[:])

            waitonly("vector", ("ve", ve_p1a[NT - 1]))
            emit("vector", "tensor_scalar", stat_m[:], s1c[:], 1.0 / D, None, OP.mult)
            emit("vector", "tensor_scalar", stat_t1[:], s2c[:], 1.0 / D, None, OP.mult)
            emit("vector", "tensor_mul", stat_t2[:], stat_m[:], stat_m[:])
            emit("vector", "tensor_sub", stat_t1[:], stat_t1[:], stat_t2[:])
            emit("vector", "tensor_scalar", stat_t2[:], stat_t1[:], EPS, None, OP.add,
                 inc="ve")
            newton_rsqrt(stat_iv, stat_t2, stat_t3, s1c)
            emit("vector", "scalar_tensor_tensor", stat_b1[:], stat_m[:], -1.0,
                 stat_iv[:], OP.mult, OP.mult)
            emit("vector", "scalar_tensor_tensor", stat_t2[:], stat_t2[:], EPS,
                 stat_t1[:], OP.mult, OP.add, inc="ve")
            newton_rsqrt(stat_R, stat_t2, stat_t3, s1c)
            emit("vector", "memset", rowb[:], 0.0, inc="ve")
            ve_stats = val("ve")

            phase_mark(2)
            waitonly("sync", ("ve", ve_stats), ("sg", d_wg), ("gp", gp_consts),
                     ("sd", d_w0))
            pe_tp = {}
            ve_cp = {}
            for j in range(NT):
                bi = j % 2
                ve_prev_y = cnt.get("ve_y_%d" % (j - 2), 0)
                sx = "sdx%d" % bi
                dv = emit("sync", "dma_start", x0t[bi][:], x_in[j * 128:(j + 1) * 128, :],
                          wait=[("ve", ve_prev_y)], inc=sx)
                wv_ = [(sx, dv), ("ve", ve_stats)]
                if (j - 2, 7) in pe_tp:
                    wv_.append(("pe", pe_tp[(j - 2, 7)]))
                emit("vector", "tensor_copy", stgm[:, bi:bi + 1], stat_m[:, j:j + 1],
                     wait=wv_)
                emit("vector", "tensor_copy", stgr[:, bi:bi + 1], stat_R[:, j:j + 1])
                emit("vector", "scalar_tensor_tensor", ytmp[:], x0t[bi][:],
                     stgm[:, bi:bi + 1], lnw[:], OP.subtract, OP.mult)
                emit("vector", "scalar_tensor_tensor", ytk[bi][:], ytmp[:],
                     stgr[:, bi:bi + 1], lnb[:], OP.mult, OP.add, inc="ve")
                cnt["ve_y_%d" % j] = val("ve")
                for c in range(8):
                    pv8 = pp4 if c % 2 == 0 else pp5
                    w = [("ve", cnt["ve_y_%d" % j])]
                    key = (j, c - 2) if c >= 2 else ((j - 1, c + 6) if j >= 1 else None)
                    if key is not None and key in ve_cp:
                        w.append(("ve", ve_cp[key]))
                    emit("tensor", "matmul", pv8[:, 0:128],
                         ytk[bi][:, c * 128:(c + 1) * 128], idb[:],
                         start=True, stop=True, wait=w, inc="pe")
                    pe_tp[(j, c)] = val("pe")
                    emit("vector", "tensor_copy", y_fm[:, c, j * 128:(j + 1) * 128],
                         pv8[:, 0:128], wait=[("pe", pe_tp[(j, c)])], inc="ve")
                    ve_cp[(j, c)] = val("ve")

            if dbg and os.environ.get("KERNEL_DBG_EARLY", "") == "2" and not mute["on"]:
                emit("sync", "dma_start", dbg_y, y_fm[:, :, 1536:2048],
                     wait=[("pe", val("pe")), ("ve", val("ve")), ("ac", val("ac"))],
                     inc="so")
                emit("sync", "dma_start", dbg_y2, y_fm[:, :, 1536:2048], inc="so")
                waitonly("sync", ("so", val("so")))
            phase_mark(3)
            ve_all_cp = val("ve")
            waitonly("tensor", ("ve", ve_all_cp), ("sg", d_wg), ("gp", gp_consts))
            ve_vd = {}
            for kt in range(NT):
                ppv = pp6 if kt % 2 == 0 else pp7
                w = [("ve", ve_vd.get(kt - 2, 0))]
                for ch in range(8):
                    emit("tensor", "matmul", ppv[:, 0:512],
                         y_fm[:, ch, kt * 128:(kt + 1) * 128], wv8[:, ch, 0:512],
                         start=(ch == 0), stop=(ch == 7), wait=(w if ch == 0 else ()),
                         inc=("pe" if ch == 7 else None))
                pv_ = val("pe")
                if os.environ.get("KERNEL_NOVEVAC", "") == "1":
                    emit("vector", "tensor_copy", v_sb[:, kt, 0:512], ppv[:, 0:512],
                         wait=[("pe", pv_)], inc="ve")
                else:
                    src = ppv[:, 0:512].rearrange("p (h e) -> p h e", h=8)
                    dst = v_sb[:, kt, 0:520].rearrange("p (h e) -> p h e", e=65)[:, :, 0:64]
                    emit("vector", "tensor_copy", dst, src, wait=[("pe", pv_)], inc="ve")
                ve_vd[kt] = val("ve")

            NOWQ = os.environ.get("KERNEL_NOWQ", "") == "1"
            wq_chunks = ([("w1", (j, h)) for j in range(8) for h in range(2)]
                         + [("w2", j) for j in range(32)])
            wq = {"li": 0, "ci": 0, "ldv": [], "use": [], "st": [], "mode": "red",
                  "gate": [("pe", val("pe")), ("ve", val("ve"))]}

            def wq_step(n):
                if NOWQ:
                    return
                for _ in range(n):
                    while wq["li"] < len(wq_chunks) and wq["li"] < wq["ci"] + 2:
                        i = wq["li"]
                        kind, j = wq_chunks[i]
                        if kind == "w1":
                            jj, hh = j
                            width = 2048
                            src = w1T_in[jj * 128:(jj + 1) * 128,
                                         hh * 2048:(hh + 1) * 2048]
                        else:
                            width = D
                            src = w2T_in[j * 128:(j + 1) * 128, :]
                        w = [("ve", wq["use"][i - 2])] if i >= 2 else list(wq["gate"])
                        if wq["mode"] == "quant" and i >= 2:
                            w.append(("sst%d" % (i % 2), wq["st"][i - 2]))
                        dv = emit("sync", "dma_start", wbuf[i % 2][:, 0:width], src,
                                  wait=w, inc="swq%d" % (i % 2))
                        wq["ldv"].append(dv)
                        wq["li"] += 1
                    i = wq["ci"]
                    if i >= len(wq_chunks):
                        return
                    wq["ci"] += 1
                    kind, j = wq_chunks[i]
                    width = 2048 if kind == "w1" else D
                    buf = wbuf[i % 2]
                    dv = wq["ldv"][i]
                    if wq["mode"] == "red":
                        ci_col = (2 * j[0] + j[1]) if kind == "w1" else 16 + j
                        emit("vector", "tensor_reduce", wsums[:, ci_col:ci_col + 1],
                             buf[:, 0:width], AX, OP.add,
                             apply_absolute_value=True,
                             wait=[("swq%d" % (i % 2), dv)] +
                                  ([("gp", gp_consts)] if i == 0 else []),
                             inc="ve")
                        wq["use"].append(val("ve"))
                    else:
                        swcol = sw_bc[:, 0:1] if kind == "w1" else sw_bc[:, 1:2]
                        emit("scalar", "activation", buf[:, 0:width], buf[:, 0:width],
                             AF.Identity, bias=magicc[:], scale=swcol,
                             wait=[("swq%d" % (i % 2), dv), ("ve", wq["sw_ready"])] +
                                  ([("sst%d" % (i % 2), wq["st"][i - 2])] if i >= 2 else []),
                             inc="ac")
                        av = val("ac")
                        emit("vector", "tensor_scalar", buf[:, 0:width], buf[:, 0:width],
                             MAGIC, 1.0, OP.subtract, OP.min, wait=[("ac", av)])
                        emit("vector", "tensor_scalar", wqo[i % 2][:, 0:width],
                             buf[:, 0:width], -1.0, None, OP.max, inc="ve")
                        wq["use"].append(val("ve"))
                        qvv = val("ve")
                        if kind == "w1":
                            jj, hh = j
                            hdst = w1q_h[jj * 128:(jj + 1) * 128,
                                         hh * 2048:(hh + 1) * 2048]
                        else:
                            hdst = w2q_h[j * 128:(j + 1) * 128, :]
                        emit("sync", "dma_start", hdst, wqo[i % 2][:, 0:width],
                             wait=[("ve", qvv)], inc="sst%d" % (i % 2))
                        wq["st"].append(val("sst%d" % (i % 2)))

            def wq_meanmath(extra_pe_wait):
                if NOWQ:
                    return
                emit("vector", "tensor_reduce", wcol[:, 0:1], wsums[:, 0:16], AX, OP.add)
                emit("vector", "tensor_reduce", wcol[:, 1:2], wsums[:, 16:48], AX,
                     OP.add, inc="ve")
                vv = val("ve")
                emit("tensor", "matmul", pp4[0:2, 0:1], wcol[:, 0:2], onesc[:],
                     start=True, stop=True,
                     wait=[("ve", vv)] + extra_pe_wait, inc="pe")
                pv_ = val("pe")
                emit("vector", "tensor_copy", wsums[0:2, 48:49], pp4[0:2, 0:1],
                     wait=[("pe", pv_)], inc="ve")
                vv = val("ve")
                emit("tensor", "matmul", pp5[0:1, 0:128], wsums[:, 48:49], idf[:],
                     start=True, stop=True, wait=[("ve", vv)], inc="pe")
                pv_ = val("pe")
                emit("vector", "tensor_copy", rowb[0:1, 10:12], pp5[0:1, 0:2],
                     wait=[("pe", pv_)])
                emit("vector", "tensor_scalar", rowb[0:1, 10:12], rowb[0:1, 10:12],
                     1.0 / (D * DH), 1e-5, OP.mult, OP.max)
                emit("vector", "reciprocal", rowb[0:1, 12:14], rowb[0:1, 10:12],
                     inc="ve")
                vv = val("ve")
                for i, src_off in enumerate((12, 10)):
                    for k in range(2):
                        emit("tensor", "matmul", pp5[:, 4 + i * 2 + k:5 + i * 2 + k],
                             onesr[0:1, :], rowb[0:1, src_off + k:src_off + k + 1],
                             start=True, stop=True,
                             wait=([("ve", vv)] if (i == 0 and k == 0) else ()),
                             inc=("pe" if (i == 1 and k == 1) else None))
                pv_ = val("pe")
                emit("vector", "tensor_copy", sw_bc[:, 0:2], pp5[:, 4:6],
                     wait=[("pe", pv_)])
                emit("vector", "tensor_copy", swr_bc[:, 0:2], pp5[:, 6:8], inc="ve")
                wq["sw_ready"] = val("ve")

            if dbg and os.environ.get("KERNEL_DBG_EARLY", "") == "1" and not mute["on"]:
                emit("sync", "dma_start", dbg_y, y_fm[:, :, 1536:2048],
                     wait=[("pe", val("pe")), ("ve", val("ve")), ("ac", val("ac"))],
                     inc="so")
                waitonly("sync", ("so", val("so")))
            exp_done = []
            av_done = []
            gidx = [0]
            last_fin = [0]
            ve_ppb_rd = [val("ve")]
            ve_accs = [val("ve")]
            ve_ppb_mul = [val("ve")]
            rec_ac = [0]
            rl_rd = [0]
            ktc = [0]
            s_done_l = {}
            pend_fin = [None]

            for hp in range(HP):
                phase_mark(4 + hp)
                last_exp = exp_done[-1] if exp_done else 0
                ve_kq = {}
                for wi_, (which, dst) in enumerate((("k", k_hp), ("q", q_hp))):
                    for ntl in range(8):
                        ppv = pp6 if ntl % 2 == 0 else pp7
                        w = [("ac", last_exp), ("sgk%d" % (hp % 2), kq_load[hp])]
                        key = (which, ntl - 2)
                        if key in ve_kq:
                            w.append(("ve", ve_kq[key]))
                        if which == "q" and ntl < 2:
                            w.append(("ve", ve_kq[("k", 6 + ntl)]))
                        if hp == 0 and which == "k" and ntl < 2:
                            w.append(("ve", ve_vd[NT - 2 + ntl]))
                        for ch in range(8):
                            emit("tensor", "matmul", ppv[:, 0:512],
                                 wqkp[:, hp % 2, wi_, ch, 0:128],
                                 y_fm[:, ch, ntl * 512:(ntl + 1) * 512],
                                 start=(ch == 0), stop=(ch == 7),
                                 wait=(w if ch == 0 else ()),
                                 inc=("pe" if ch == 7 else None))
                        pv_ = val("pe")
                        emit("vector", "tensor_copy", dst[:, ntl * 512:(ntl + 1) * 512],
                             ppv[:, 0:512], wait=[("pe", pv_)], inc="ve")
                        ve_kq[(which, ntl)] = val("ve")
                if hp + 2 < HP:
                    pvkq = val("pe")
                    sk = "sgk%d" % (hp % 2)
                    emit("gpsimd", "dma_start", wqkp[:, hp % 2, 0], wkp_in[:, hp + 2],
                         wait=[("pe", pvkq)], inc=sk)
                    emit("gpsimd", "dma_start", wqkp[:, hp % 2, 1], wqp_in[:, hp + 2],
                         inc=sk)
                    kq_load[hp + 2] = val(sk)
                if hp == HP - 1:
                    cnt["pe_last_kq"] = val("pe")

                pe_norm = {0: 0, 1: 0}
                vr0 = max(ve_ppb_rd[0], last_fin[0])
                ve_red = {0: vr0, 1: vr0}
                nidx = 0
                for which, srcb, coli in (("k", k_hp, 0), ("q", q_hp, 1)):
                    for ntl in range(8):
                        par = nidx % 2
                        col0 = 512 * par
                        wv_ = [("ve", ve_kq[(which, ntl)])]
                        if pe_norm[par]:
                            wv_.append(("pe", pe_norm[par]))
                        emit("scalar", "square", sqs_nb[par][:],
                             srcb[:, ntl * 512:(ntl + 1) * 512], wait=wv_, inc="ac")
                        av = val("ac")
                        emit("tensor", "matmul", pp_b[0:2, col0:col0 + 512],
                             ones2[:], sqs_nb[par][:], start=True, stop=True,
                             wait=[("ac", av), ("ve", ve_red[par])], inc="pe")
                        pe_norm[par] = val("pe")
                        if nidx in (0, 8):
                            ve_red[par] = emit(
                                "vector", "tensor_reduce", kq2[0:2, coli:coli + 1],
                                pp_b[0:2, col0:col0 + 512], AX, OP.max,
                                wait=[("pe", pe_norm[par]), ("gp", gp_consts)],
                                inc="ve")
                        else:
                            ve_red[par] = emit(
                                "vector", "tensor_reduce", stat_t3[0:2, 0:1],
                                pp_b[0:2, col0:col0 + 512], AX, OP.max,
                                wait=[("pe", pe_norm[par])])
                            emit("vector", "tensor_max", kq2[0:2, coli:coli + 1],
                                 kq2[0:2, coli:coli + 1], stat_t3[0:2, 0:1], inc="ve")
                        nidx += 1
                ve_ppb_rd[0] = val("ve")
                vkq = val("ve")
                emit("tensor", "matmul", pp4[0:1, 0:128], kq2[:, 0:1], idf[:],
                     start=True, stop=True,
                     wait=[("ve", vkq), ("ac", last_exp), ("gp", gp_consts)])
                emit("tensor", "matmul", pp4[0:1, 128:256], kq2[:, 1:2], idf[:],
                     start=True, stop=True, inc="pe")
                pv_ = val("pe")
                emit("vector", "tensor_copy", rowb[0:1, 0:2], pp4[0:1, 0:2],
                     wait=[("pe", pv_)])
                emit("vector", "tensor_copy", rowb[0:1, 2:4], pp4[0:1, 128:130])
                emit("vector", "tensor_mul", rowb[0:1, 4:6], rowb[0:1, 0:2],
                     rowb[0:1, 2:4], inc="ve")
                vv = val("ve")
                emit("scalar", "sqrt", rowb[0:1, 6:8], rowb[0:1, 4:6],
                     wait=[("ve", vv)], inc="ac")
                av = val("ac")
                emit("vector", "tensor_scalar", rowb[0:1, 6:8], rowb[0:1, 6:8],
                     -0.125, None, OP.mult, wait=[("ac", av)], inc="ve")
                vv = val("ve")
                for h in range(2):
                    emit("tensor", "matmul", pp4[:, 256 + h:257 + h], onesr[0:1, :],
                         rowb[0:1, 6 + h:7 + h], start=True, stop=True,
                         wait=([("ve", vv)] if h == 0 else ()),
                         inc=("pe" if h == 1 else None))
                pv_ = val("pe")
                emit("vector", "tensor_copy", negc[:, 0:2], pp4[:, 256:258],
                     wait=[("pe", pv_), ("ac", last_exp)], inc="ve")
                ve_negc = val("ve")

                for qb in range(8):
                    nkt = 4 * qb + 4
                    q0 = qb * 512
                    gbase = gidx[0]

                    def S_emit(kt, q0=q0, gbase=gbase):
                        gi = gbase + kt
                        sel = gi % 2
                        sA = pp4 if sel == 0 else pp5
                        sB = pp6 if sel == 0 else pp7
                        w = [("ve", ve_negc)]
                        if gi >= 2:
                            w.append(("ac", exp_done[gi - 2]))
                        emit("tensor", "matmul", sA[:, 0:512],
                             k_hp[0:64, kt * 128:(kt + 1) * 128],
                             q_hp[0:64, q0:q0 + 512],
                             start=True, stop=True, tile_position=(0, 0), wait=w)
                        emit("tensor", "matmul", sB[:, 0:512],
                             k_hp[64:128, kt * 128:(kt + 1) * 128],
                             q_hp[64:128, q0:q0 + 512],
                             start=True, stop=True, tile_position=(64, 0),
                             inc="pe")
                        s_done_l[gi] = val("pe")

                    def tail_emit(kt, q0=q0, gbase=gbase, nkt=nkt, hp=hp, qb=qb):
                        gi = gbase + kt
                        sel = gi % 2
                        sA = pp4 if sel == 0 else pp5
                        sB = pp6 if sel == 0 else pp7
                        we = [("pe", s_done_l[gi])]
                        if gi >= 2:
                            we.append(("pe", av_done[gi - 2]))
                        emit("scalar", "activation", ptile[:, 0, sel, :],
                             sA[:, 0:512], AF.Exp, bias=negc[:, 0:1], scale=0.125,
                             wait=we)
                        emit("scalar", "activation", ptile[:, 1, sel, :],
                             sB[:, 0:512], AF.Exp, bias=negc[:, 1:2], scale=0.125,
                             inc="ac", inck=2)
                        exp_done.append(val("ac"))
                        avw = [("ac", exp_done[gi])]
                        if kt >= 4 * qb:
                            dj = kt - 4 * qb
                            msl = mbm[:, 384 - 128 * dj:896 - 128 * dj]
                            emit("vector", "tensor_mul", ptile[:, 0, sel, :],
                                 ptile[:, 0, sel, :], msl,
                                 wait=[("ac", exp_done[gi])])
                            emit("vector", "tensor_mul", ptile[:, 1, sel, :],
                                 ptile[:, 1, sel, :], msl, inc="ve")
                            avw = [("ve", val("ve"))]
                        avw0 = list(avw)
                        if kt == 0:
                            avw0.append(("ve", ve_accs[0]))
                        emit("tensor", "matmul", pp_a[0:65, 0:512],
                             v_sb[:, kt, 130 * hp:130 * hp + 65],
                             ptile[:, 0, sel, :],
                             start=(kt == 0), stop=(kt == nkt - 1), wait=avw0)
                        emit("tensor", "matmul", pp_a[0:65, 512:1024],
                             v_sb[:, kt, 130 * hp + 65:130 * hp + 130],
                             ptile[:, 1, sel, :],
                             start=(kt == 0), stop=(kt == nkt - 1), inc="pe")
                        av_done.append(val("pe"))
                        ktc[0] += 1
                        if ktc[0] % 6 == 0:
                            wq_step(1)

                    for kt in range(nkt):
                        S_emit(kt)
                        if kt == 1 and pend_fin[0] is not None:
                            pend_fin[0]()
                            pend_fin[0] = None
                        if kt >= 1:
                            tail_emit(kt - 1)
                    if pend_fin[0] is not None:
                        pend_fin[0]()
                        pend_fin[0] = None
                    tail_emit(nkt - 1)
                    gidx[0] += nkt
                    pv_ = av_done[-1]
                    emit("vector", "tensor_copy", accs[:, 0, :], pp_a[0:65, 0:512],
                         wait=[("pe", pv_)])
                    emit("vector", "tensor_copy", accs[:, 1, :], pp_a[0:65, 512:1024],
                         inc="ve")
                    ve_accs[0] = val("ve")
                    emit("vector", "reciprocal", rlA64, accs[64:65, 0, :],
                         wait=[("pe", rl_rd[0])])
                    emit("vector", "reciprocal", rlB64, accs[64:65, 1, :],
                         inc="ve")
                    rec_ac[0] = val("ve")

                    def fin_pe(hp=hp, q0=q0, va=ve_accs[0], ra=rec_ac[0]):
                        emit("tensor", "matmul", pp_b[0:64, 0:512],
                             onesr2[64:65, 0:64], rlA64, start=True, stop=True,
                             wait=[("ve", max(va, ra)),
                                   ("ve", ve_ppb_mul[0]), ("gp", gp_consts)])
                        emit("tensor", "matmul", pp_b[0:64, 512:1024],
                             onesr2[64:65, 0:64], rlB64, start=True, stop=True,
                             inc="pe")
                        pv2 = val("pe")
                        rl_rd[0] = pv2
                        emit("vector", "tensor_mul", attn[0:64, hp, q0:q0 + 512],
                             accs[0:64, 0, :], pp_b[0:64, 0:512],
                             wait=[("pe", pv2)])
                        emit("vector", "tensor_mul", atmp[0:64, :],
                             accs[0:64, 1, :], pp_b[0:64, 512:1024], inc="ve")
                        vv2 = val("ve")
                        ve_ppb_mul[0] = vv2
                        emit("tensor", "matmul", pp_b[:, 0:512], esh[0:64, 0:128],
                             atmp[0:64, :], start=True, stop=True,
                             wait=[("ve", vv2)], inc="pe")
                        pv3 = val("pe")
                        emit("vector", "tensor_copy",
                             attn[64:128, hp, q0:q0 + 512],
                             pp_b[64:128, 0:512], wait=[("pe", pv3)], inc="ve")
                        last_fin[0] = val("ve")

                    pend_fin[0] = fin_pe

                if pend_fin[0] is not None:
                    pend_fin[0]()
                    pend_fin[0] = None
                if hp == 1:
                    wq_step(48)
                    wq_meanmath([("ac", exp_done[-1]), ("ve", last_fin[0])])
                    wq.update(li=0, ci=0, ldv=[], use=[], st=[], mode="quant",
                              gate=[("ve", val("ve"))])
            wq_step(48)

            if dbg and os.environ.get("KERNEL_DBG_EARLY", "") == "3" and not mute["on"]:
                emit("sync", "dma_start", dbg_y, y_fm[:, :, 1536:2048],
                     wait=[("pe", val("pe")), ("ve", val("ve")), ("ac", val("ac"))],
                     inc="so")
                emit("sync", "dma_start", dbg_kq[:, 0, :], k_hp[:], inc="so")
                emit("sync", "dma_start", dbg_kq[:, 1, :], q_hp[:], inc="so")
                emit("sync", "dma_start", dbg_at, attn[:], inc="so")
                waitonly("sync", ("so", val("so")))
            phase_mark(8)
            wq_done_ve = val("ve")
            wq_done_so = (val("sst0"), val("sst1"))
            if dbg and not mute["on"]:
                emit("sync", "dma_start", dbg_y, y_fm[:, :, 1536:2048],
                     wait=[("pe", val("pe")), ("ve", val("ve")), ("ac", val("ac"))],
                     inc="so")
                emit("sync", "dma_start", dbg_kq[:, 0, :], k_hp[:], inc="so")
                emit("sync", "dma_start", dbg_kq[:, 1, :], q_hp[:], inc="so")
                emit("sync", "dma_start", dbg_at, attn[:], inc="so")
                waitonly("sync", ("so", val("so")))
            emit("gpsimd", "dma_start", wo8, wo_in,
                 wait=[("pe", cnt["pe_last_kq"]), ("ve", last_fin[0])], inc="sg")
            d_wo8 = val("sg")
            ve_x1 = {}
            so_x1 = {}
            for j in range(NT):
                bi = j % 2
                ppw = pp_a if bi == 0 else pp_b
                w = [("ve", last_fin[0])]
                if j >= 2:
                    w.append(("ve", ve_x1[j - 2]))
                if j == 0:
                    w.append(("sg", d_wo8))
                for half in range(2):
                    for hpj in range(HP):
                        emit("tensor", "matmul", ppw[:, half * 512:(half + 1) * 512],
                             attn[:, hpj, j * 128:(j + 1) * 128],
                             wo8[:, hpj, half * 512:(half + 1) * 512],
                             start=(hpj == 0), stop=(hpj == 3),
                             wait=(w if (half == 0 and hpj == 0) else ()),
                             inc=("pe" if (half == 1 and hpj == 3) else None))
                pv_ = val("pe")
                wd = [("ve", ve_x1.get(j - 2, 0))]
                sx = "sdx%d" % bi
                dv = emit("sync", "dma_start", x0t[bi][:], x_in[j * 128:(j + 1) * 128, :],
                          wait=wd, inc=sx)
                emit("vector", "tensor_copy", stgiv[:, bi:bi + 1], stat_iv[:, j:j + 1],
                     wait=[("ac", cnt.get("ac_n25_%d" % (j - 2), 0))])
                emit("vector", "tensor_copy", stgb1[:, bi:bi + 1], stat_b1[:, j:j + 1],
                     inc="ve")
                stg_ve = val("ve")
                emit("scalar", "activation", n25[:], x0t[bi][:], AF.Identity,
                     bias=stgb1[:, bi:bi + 1], scale=stgiv[:, bi:bi + 1],
                     wait=[(sx, dv), ("ve", ve_x1.get(j - 1, 0)), ("ve", stg_ve),
                           ("sg", d_wo8), ("pe", cnt["pe_last_kq"])], inc="ac")
                cnt["ac_n25_%d" % j] = val("ac")
                av = val("ac")
                emit("vector", "tensor_add", u25[:], x0t[bi][:], n25[:],
                     wait=[("ac", av)])
                emit("vector", "scalar_tensor_tensor", x1p[bi][:], u25[:], 0.5,
                     ppw[:, 0:1024], OP.mult, OP.add,
                     wait=[("pe", pv_), ("sci%d" % bi, so_x1.get(j - 2, 0)),
                           ("sst0", wq_done_so[0]), ("sst1", wq_done_so[1]),
                           ("ve", wq_done_ve)], inc="ve")
                ve_x1[j] = val("ve")
                emit("sync", "dma_start", cin[j * 128:(j + 1) * 128, :], x1p[bi][:],
                     wait=[("ve", ve_x1[j])], inc="sci%d" % bi)
                if j == 15:
                    emit("sync", "dma_start", ag_in[0:1, :], x1p[bi][127:128, :],
                         inc="sci%d" % bi)
                if j == 16:
                    emit("sync", "dma_start", ag_in[1:2, :], x1p[bi][0:1, :],
                         inc="sci%d" % bi)
                so_x1[j] = val("sci%d" % bi)

            phase_mark(9)
            so_all = val("so")
            emit("gpsimd", "collective_compute", "ReduceScatter", OP.add,
                 replica_groups=GROUPS, ins=[cin], outs=[rs_pad[1:T + 1, :]],
                 wait=[("so", so_all), ("sci0", val("sci0")), ("sci1", val("sci1"))],
                 inc="co")
            emit("gpsimd", "collective_compute", "AllGather", OP.bypass,
                 replica_groups=GROUPS, ins=[ag_in], outs=[ag_out], inc="co")
            co_done = val("co")
            dv = emit("sync", "dma_start", agsb[:], ag_out, wait=[("co", co_done)],
                      inc="sd")
            emit("tensor", "matmul", pp4[0:2, 0:512], wselb[:], agsb[:, 0:512],
                 start=True, stop=True,
                 wait=[("sd", dv), ("ac", exp_done[-1]), ("ve", last_fin[0])])
            emit("tensor", "matmul", pp5[0:2, 0:512], wselb[:], agsb[:, 512:1024],
                 start=True, stop=True, inc="pe")
            pv_ = val("pe")
            emit("vector", "tensor_copy", x1halo[:, 0:512], pp4[0:2, 0:512],
                 wait=[("pe", pv_)])
            emit("vector", "tensor_copy", x1halo[:, 512:1024], pp5[0:2, 0:512],
                 inc="ve")
            vv = val("ve")
            emit("sync", "dma_start", rs_pad[0:1, :], x1halo[0:1, :],
                 wait=[("ve", vv)], inc="so")
            emit("sync", "dma_start", rs_pad[T + 1:T + 2, :], x1halo[1:2, :],
                 inc="so")

            finalize_debug(x0t[0][:])
            if not mute.get("done"):
                bv_endA = {s_: val(s_) for s_ in ("sd", "sg", "so", "pe", "ve", "ac", "gp", "co", "sdx0", "sdx1", "swq0", "swq1", "sst0", "sst1", "sci0", "sci1", "sgk0", "sgk1")}
                for eng in ENGS:
                    waitonly(eng, *[(s_, v) for s_, v in bv_endA.items() if v > 0])
            run_block()

        with ExitStack() as fctx:
            ef = fctx.enter_context
            w1q = ef(nc.sbuf_tensor("w1q", [128, 8, DH], BF16))
            w2q = ef(nc.sbuf_tensor("w2q", [128, 32, D], BF16))
            diag = ef(nc.sbuf_tensor("diag", [128, 32, 4, 64], BF16))
            win = [ef(nc.sbuf_tensor("win%d" % i, [128, D], F32)) for i in range(2)]
            x1o = win
            n3 = ef(nc.sbuf_tensor("n3", [128, D], F32))
            sq3 = ef(nc.sbuf_tensor("sq3", [128, D], F32))
            t3 = sq3
            x2b = [n3, sq3]
            xqt = ef(nc.sbuf_tensor("xqt", [128, D], BF16))
            xqf = ef(nc.sbuf_tensor("xqf", [128, 8, 256], BF16))
            h1s = ef(nc.sbuf_tensor("h1s", [128, 2, 256], BF16))
            h3 = ef(nc.sbuf_tensor("h3", [128, 32, 254], F32))
            habs = ef(nc.sbuf_tensor("habs", [128, 254], F32))
            hacc = ef(nc.sbuf_tensor("hacc", [128, 254], F32))
            h3q = ef(nc.sbuf_tensor("h3q", [128, 2, 254], BF16))
            scol = ef(nc.sbuf_tensor("scol", [128, 16], F32))
            c2c = ef(nc.sbuf_tensor("c2c", [128, 6], F32))
            stiv = ef(nc.sbuf_tensor("stiv", [128, 2], F32))
            stb2 = ef(nc.sbuf_tensor("stb2", [128, 2], F32))
            sts1 = ef(nc.sbuf_tensor("sts1", [128, 2], F32))
            e2c = ef(nc.sbuf_tensor("e2c", [128, 2], F32))
            xqtf = xqt.ap().bitcast(F32)
            bc1 = xqtf[:, 0:256]
            bc2 = xqtf[:, 256:512]
            r1row = n3.ap()[0:1, 0:256]
            s2row = n3.ap()[0:1, 256:512]

            d1 = emit("sync", "dma_start", w1q[:],
                      w1q_h.rearrange("(c p) n -> p c n", p=128), inc="swq0")
            d2 = emit("sync", "dma_start", w2q[:],
                      w2q_h.rearrange("(c p) n -> p c n", p=128), inc="swq1")
            d3 = emit("gpsimd", "dma_start", diag[:], diag_in, inc="sg")
            emit("vector", "memset", scol[:], 0.0)
            emit("vector", "memset", c2c[:], 0.0, inc="ve")
            ve_diag = val("ve")
            waitonly("tensor", ("ve", ve_diag), ("swq0", d1), ("swq1", d2),
                     ("sg", d3))

            prev = {"epi_ve": 0, "outs": 0, "mm2_pe": 0, "xq_pe": 0}
            for ci, (c0, n) in enumerate(FFN_CHUNKS):
                phase_mark(10 + ci)
                Wn = n + 2
                ta = min(128, Wn)
                tb = Wn - ta
                na = min(128, n)
                nb = n - na
                wd = [("ve", prev["epi_ve"]), ("so", prev["outs"])]
                dva = emit("sync", "dma_start", win[0][0:ta, :],
                           rs_pad[c0:c0 + ta, :], wait=wd, inc="sd")
                if tb > 0:
                    emit("sync", "dma_start", win[1][0:tb, :],
                         rs_pad[c0 + ta:c0 + Wn, :], inc="sd")
                dva = dvb = val("sd")
                submark(0 + 1000 * ci)
                tiles = [(0, ta, dva)] + ([(1, tb, dvb)] if tb > 0 else [])
                ve_sq_prev = 0
                for wi, rows, dv in tiles:
                    o = 5 * wi
                    wt = win[wi]
                    emit("scalar", "square", sq3[0:rows, :], wt[0:rows, :],
                         wait=[("sd", dv), ("ve", ve_sq_prev),
                               ("so", prev["outs"])], inc="ac")
                    av = val("ac")
                    emit("vector", "tensor_reduce", scol[0:rows, o:o + 1],
                         wt[0:rows, :], AX, OP.add, wait=[("sd", dv)])
                    emit("vector", "tensor_reduce", scol[0:rows, o + 1:o + 2],
                         sq3[0:rows, :], AX, OP.add, wait=[("ac", av)], inc="ve")
                    ve_sq_prev = val("ve")
                    emit("vector", "tensor_scalar", scol[0:rows, o:o + 1],
                         scol[0:rows, o:o + 1], 1.0 / D, None, OP.mult)
                    emit("vector", "tensor_scalar", scol[0:rows, o + 1:o + 2],
                         scol[0:rows, o + 1:o + 2], 1.0 / D, None, OP.mult)
                    emit("vector", "tensor_mul", scol[0:rows, o + 2:o + 3],
                         scol[0:rows, o:o + 1], scol[0:rows, o:o + 1])
                    emit("vector", "tensor_sub", scol[0:rows, o + 1:o + 2],
                         scol[0:rows, o + 1:o + 2], scol[0:rows, o + 2:o + 3])
                    emit("vector", "tensor_scalar", scol[0:rows, o + 2:o + 3],
                         scol[0:rows, o + 1:o + 2], EPS, None, OP.add, inc="ve")
                    vv = val("ve")
                    emit("scalar", "sqrt", scol[0:rows, o + 3:o + 4],
                         scol[0:rows, o + 2:o + 3], wait=[("ve", vv)], inc="ac")
                    av = val("ac")
                    emit("vector", "reciprocal", stiv[0:rows, wi:wi + 1],
                         scol[0:rows, o + 3:o + 4], wait=[("ac", av)])
                    for _ in range(2):
                        emit("vector", "tensor_mul", scol[0:rows, o + 3:o + 4],
                             stiv[0:rows, wi:wi + 1], stiv[0:rows, wi:wi + 1])
                        emit("vector", "tensor_mul", scol[0:rows, o + 3:o + 4],
                             scol[0:rows, o + 3:o + 4], scol[0:rows, o + 2:o + 3])
                        emit("vector", "tensor_scalar", scol[0:rows, o + 3:o + 4],
                             scol[0:rows, o + 3:o + 4], -0.5, 1.5, OP.mult, OP.add)
                        emit("vector", "tensor_mul", stiv[0:rows, wi:wi + 1],
                             stiv[0:rows, wi:wi + 1], scol[0:rows, o + 3:o + 4])
                    emit("vector", "scalar_tensor_tensor", stb2[0:rows, wi:wi + 1],
                         scol[0:rows, o:o + 1], -1.0, stiv[0:rows, wi:wi + 1],
                         OP.mult, OP.mult, inc="ve")

                submark(5 + 1000 * ci)
                ve_stats3 = val("ve")
                ve_nprev = 0
                for wi, rows, dv in tiles:
                    o = 5 * wi
                    wt = win[wi]
                    emit("scalar", "activation", n3[0:rows, :], wt[0:rows, :],
                         AF.Identity, bias=stb2[0:rows, wi:wi + 1],
                         scale=stiv[0:rows, wi:wi + 1],
                         wait=[("ve", ve_stats3), ("ve", ve_nprev)], inc="ac")
                    av = val("ac")
                    emit("vector", "tensor_reduce", scol[0:rows, 10 + wi:11 + wi],
                         n3[0:rows, :], AX, OP.max, apply_absolute_value=True,
                         wait=[("ac", av)])
                    emit("vector", "tensor_scalar", scol[0:rows, 10 + wi:11 + wi],
                         scol[0:rows, 10 + wi:11 + wi], 1e-5, None, OP.max)
                    emit("vector", "reciprocal", sts1[0:rows, wi:wi + 1],
                         scol[0:rows, 10 + wi:11 + wi])
                    emit("vector", "tensor_scalar", sts1[0:rows, wi:wi + 1],
                         sts1[0:rows, wi:wi + 1], 127.0, None, OP.mult)
                    submark(6 + 1000 * ci)
                    emit("vector", "scalar_tensor_tensor", scol[0:rows, 14 + wi:15 + wi],
                         scol[0:rows, 10 + wi:11 + wi], 1.0 / 127.0,
                         swr_bc[0:rows, 0:1], OP.mult, OP.mult)
                    submark(7 + 1000 * ci)
                    emit("vector", "tensor_scalar", t3[0:rows, :], n3[0:rows, :],
                         sts1[0:rows, wi:wi + 1], MAGIC, OP.mult, OP.add)
                    submark(8 + 1000 * ci)
                    emit("vector", "tensor_scalar", xqt[0:rows, :], t3[0:rows, :],
                         MAGIC, None, OP.subtract,
                         wait=[("pe", prev["xq_pe"])], inc="ve")
                    submark(9 + 1000 * ci)
                    ve_nprev = val("ve")
                    submark(10 + 1000 * ci)
                    for c in range(8):
                        pv8 = pp4 if c % 2 == 0 else pp5
                        w = [("ve", ve_nprev)]
                        key = "xqcp_%d" % (c % 2)
                        if key in cnt:
                            w.append(("ve", cnt[key]))
                        if ci > 0 and wi == 0 and c < 2:
                            w.append(("pe", prev["mm2_pe"]))
                        emit("tensor", "matmul", pv8[:, 0:rows],
                             xqt[0:rows, c * 128:(c + 1) * 128], idb[0:rows, 0:rows],
                             start=True, stop=True, wait=w, inc="pe")
                        pv_ = val("pe")
                        emit("vector", "tensor_copy",
                             xqf[:, c, wi * 128:wi * 128 + rows], pv8[:, 0:rows],
                             wait=[("pe", pv_)], inc="ve")
                        cnt[key] = val("ve")
                    prev["xq_pe"] = val("pe")
                submark(20 + 1000 * ci)
                ve_xq = val("ve")
                ac_n3 = val("ac")
                dxa = emit("sync", "dma_start", x1o[0][0:na, :],
                           rs_pad[c0 + 1:c0 + 1 + na, :],
                           wait=[("ac", ac_n3), ("ve", ve_xq)], inc="sd")
                if nb > 0:
                    emit("sync", "dma_start", x1o[1][0:nb, :],
                         rs_pad[c0 + 1 + na:c0 + 1 + n, :], inc="sd")
                dxa = dxb = val("sd")

                emit("tensor", "matmul", pp6[0:1, 0:128], scol[:, 14:15], idf[:],
                     start=True, stop=True, wait=[("ve", ve_xq)])
                emit("tensor", "matmul", pp6[0:1, 128:256], scol[:, 15:16], idf[:],
                     start=True, stop=True, inc="pe")
                pv_ = val("pe")
                emit("vector", "tensor_copy", r1row[0:1, 0:256], pp6[0:1, 0:256],
                     wait=[("pe", pv_)], inc="ve")
                vv = val("ve")
                emit("tensor", "matmul", pp_b[:, 0:Wn], onesr[0:1, :],
                     r1row[0:1, 0:Wn], start=True, stop=True, wait=[("ve", vv)],
                     inc="pe")
                pv_ = val("pe")
                emit("vector", "tensor_copy", bc1[:, 0:Wn], pp_b[:, 0:Wn],
                     wait=[("pe", pv_)], inc="ve")
                ve_bc1 = val("ve")

                submark(30 + 1000 * ci)
                e1 = {}
                s3 = {}
                a3 = {}
                for cc in range(32):
                    ppm = pp4 if cc % 2 == 0 else pp5
                    w = [("ve", ve_bc1)]
                    if cc < 2:
                        w.append(("ve", cnt.get("xqcp_%d" % (cc % 2), 0)))
                    if cc >= 2:
                        w.append(("ve", e1[cc - 2]))
                    for ch in range(8):
                        emit("tensor", "matmul", ppm[:, 0:Wn],
                             w1q[:, ch, cc * 128:(cc + 1) * 128], xqf[:, ch, 0:Wn],
                             start=(ch == 0), stop=(ch == 7),
                             wait=(w if ch == 0 else ()),
                             inc=("pe" if ch == 7 else None))
                    pv_ = val("pe")
                    emit("vector", "tensor_mul", h1s[:, cc % 2, 0:Wn], ppm[:, 0:Wn],
                         bc1[:, 0:Wn], wait=[("pe", pv_)], inc="ve")
                    e1[cc] = val("ve")
                    submark(40 + 1000 * ci)
                    w = [("ve", e1[cc])]
                    if cc >= 1:
                        w.append(("ac", s3[cc - 1]))
                    for tap in range(4):
                        lhsT = (diag[0:64, cc, tap, :], onesb[0:64, 0:n]) \
                            if tap == 3 else \
                            (diag[0:64, cc, tap, :], h1s[0:64, cc % 2, tap:tap + n])
                        emit("tensor", "matmul", pp6[0:64, 0:n], lhsT[0],
                             lhsT[1], start=(tap == 0), stop=(tap == 3),
                             tile_position=(0, 0), wait=(w if tap == 0 else ()))
                        emit("tensor", "ldweights", diag[64:128, cc, tap, :],
                             tile_position=(64, 64))
                        rhsb = onesb[64:128, 0:n] if tap == 3 \
                            else h1s[64:128, cc % 2, tap:tap + n]
                        emit("tensor", "matmul", pp7[64:128, 0:n],
                             diag[64:128, cc, tap, :], rhsb,
                             start=(tap == 0), stop=(tap == 3),
                             tile_position=(64, 64),
                             inc=("pe" if tap == 3 else None))
                    cvv = val("pe")
                    emit("scalar", "activation", h3[0:64, cc, 0:n], pp6[0:64, 0:n],
                         AF.Silu, bias=0.0, scale=1.0, wait=[("pe", cvv)])
                    emit("scalar", "activation", h3[64:128, cc, 0:n],
                         pp7[64:128, 0:n], AF.Silu, bias=0.0, scale=1.0)
                    emit("scalar", "activation", habs[:, 0:n], h3[:, cc, 0:n], AF.Abs,
                         wait=([("ve", a3[cc - 1])] if cc >= 1 else ()), inc="ac")
                    s3[cc] = val("ac")
                    if cc == 0:
                        emit("vector", "tensor_copy", hacc[:, 0:n], habs[:, 0:n],
                             wait=[("ac", s3[cc])], inc="ve")
                    else:
                        emit("vector", "tensor_max", hacc[:, 0:n], hacc[:, 0:n],
                             habs[:, 0:n], wait=[("ac", s3[cc])], inc="ve")
                    a3[cc] = val("ve")

                submark(50 + 1000 * ci)
                emit("tensor", "matmul", pp_a[0:na, 0:128], hacc[:, 0:na], idf[:],
                     start=True, stop=True,
                     wait=[("ve", a3[31]), ("pe", prev["mm2_pe"])],
                     inc=(None if nb > 0 else "pe"))
                if nb > 0:
                    emit("tensor", "matmul", pp_a[0:nb, 512:640], hacc[:, na:n],
                         idf[:], start=True, stop=True, inc="pe")
                pv_ = val("pe")
                emit("vector", "tensor_reduce", c2c[0:na, 0:1], pp_a[0:na, 0:128],
                     AX, OP.max, wait=[("pe", pv_)])
                if nb > 0:
                    emit("vector", "tensor_reduce", c2c[0:nb, 1:2],
                         pp_a[0:nb, 512:640], AX, OP.max)
                emit("vector", "tensor_scalar", c2c[:, 0:2], c2c[:, 0:2], 1e-5, None,
                     OP.max)
                emit("vector", "reciprocal", c2c[:, 2:4], c2c[:, 0:2])
                emit("vector", "tensor_scalar", c2c[:, 2:4], c2c[:, 2:4], 127.0, None,
                     OP.mult)
                emit("vector", "scalar_tensor_tensor", e2c[:, 0:1], c2c[:, 0:1],
                     1.0 / 127.0, swr_bc[:, 1:2], OP.mult, OP.mult)
                emit("vector", "scalar_tensor_tensor", e2c[:, 1:2], c2c[:, 1:2],
                     1.0 / 127.0, swr_bc[:, 1:2], OP.mult, OP.mult, inc="ve")
                vv = val("ve")
                emit("tensor", "matmul", pp6[0:1, 0:128], c2c[:, 2:3], idf[:],
                     start=True, stop=True, wait=[("ve", vv)],
                     inc=(None if nb > 0 else "pe"))
                if nb > 0:
                    emit("tensor", "matmul", pp6[0:1, 128:256], c2c[:, 3:4], idf[:],
                         start=True, stop=True, inc="pe")
                pv_ = val("pe")
                emit("vector", "tensor_copy", s2row[0:1, 0:256], pp6[0:1, 0:256],
                     wait=[("pe", pv_)], inc="ve")
                vv = val("ve")
                emit("tensor", "matmul", pp_b[:, 256:256 + n], onesr[0:1, :],
                     s2row[0:1, 0:n], start=True, stop=True, wait=[("ve", vv)],
                     inc="pe")
                pv_ = val("pe")
                emit("vector", "tensor_copy", bc2[:, 0:n], pp_b[:, 256:256 + n],
                     wait=[("pe", pv_)], inc="ve")
                ve_bc2 = val("ve")

                submark(60 + 1000 * ci)
                qv = {}
                for cc in range(32):
                    w = [("ve", ve_bc2)]
                    if cc >= 2:
                        w.append(("pe", qv[cc - 2]))
                    emit("vector", "tensor_mul", habs[:, 0:n], h3[:, cc, 0:n],
                         bc2[:, 0:n], wait=w, inc="ve")
                    vv0 = val("ve")
                    emit("scalar", "activation", hacc[:, 0:n], habs[:, 0:n],
                         AF.Identity, bias=magicc[:], scale=1.0, wait=[("ve", vv0)],
                         inc="ac")
                    av = val("ac")
                    emit("vector", "tensor_scalar", h3q[:, cc % 2, 0:n], hacc[:, 0:n],
                         MAGIC, None, OP.subtract, wait=[("ac", av)], inc="ve")
                    vvq = val("ve")
                    mtl = [(0, na)] + ([(na, nb)] if nb else [])
                    for mi, (ms, mw) in enumerate(mtl):
                        ppo = pp_a if mi == 0 else pp_b
                        for half in range(2):
                            emit("tensor", "matmul",
                                 ppo[0:mw, half * 512:(half + 1) * 512],
                                 h3q[:, cc % 2, ms:ms + mw],
                                 w2q[:, cc, half * 512:(half + 1) * 512],
                                 start=(cc == 0), stop=(cc == 31),
                                 wait=([("ve", vvq)] if (mi == 0 and half == 0) else ()),
                                 inc=("pe" if (mi == len(mtl) - 1 and half == 1) else None))
                    qv[cc] = val("pe")
                prev["mm2_pe"] = qv[31]

                submark(70 + 1000 * ci)
                for mi, (ms, mw) in enumerate([(0, na)] + ([(na, nb)] if nb else [])):
                    ppo = pp_a if mi == 0 else pp_b
                    ecol = e2c[0:mw, mi:mi + 1]
                    xo = x1o[mi]
                    dv = dxa if mi == 0 else dxb
                    emit("vector", "scalar_tensor_tensor", x2b[mi][0:mw, 0:512],
                         ppo[0:mw, 0:512], ecol, xo[0:mw, 0:512], OP.mult, OP.add,
                         wait=[("pe", prev["mm2_pe"]), ("sd", dv)])
                    emit("vector", "scalar_tensor_tensor", x2b[mi][0:mw, 512:1024],
                         ppo[0:mw, 512:1024], ecol, xo[0:mw, 512:1024],
                         OP.mult, OP.add, inc="ve")
                    vv = val("ve")
                    emit("sync", "dma_start", out[c0 + ms:c0 + ms + mw, :],
                         x2b[mi][0:mw, :], wait=[("ve", vv)], inc="so")
                prev["epi_ve"] = val("ve")
                prev["outs"] = val("so")

            phase_mark(98)
            if dbg and not mute["on"]:
                emit("sync", "dma_start", dbg_cin, cin, inc="so")
                emit("sync", "dma_start", dbg_rs, rs_pad, inc="so")
            finalize_debug(rs_pad[1:129, :])
            waitonly("sync", ("so", val("so")))
            run_block()
    return nc


def kernel(L0_attn_x, L0_attn_ln_w, L0_attn_ln_b, L0_attn_wq, L0_attn_wk,
           L0_attn_wv, L0_attn_wo, L0_ffn_w1, L0_ffn_conv_w, L0_ffn_conv_b,
           L0_ffn_w2):
    x = np.ascontiguousarray(np.asarray(L0_attn_x, dtype=np.float32))
    lnwv = np.asarray(L0_attn_ln_w, np.float32)
    lnbv = np.asarray(L0_attn_ln_b, np.float32)
    wqa = np.asarray(L0_attn_wq, np.float32)
    wka = np.asarray(L0_attn_wk, np.float32)
    wva = np.asarray(L0_attn_wv, np.float32)
    woa = np.asarray(L0_attn_wo, np.float32)
    w1 = np.asarray(L0_ffn_w1, np.float32)
    cw = np.asarray(L0_ffn_conv_w, np.float32)
    cb = np.asarray(L0_ffn_conv_b, np.float32)
    w2 = np.asarray(L0_ffn_w2, np.float32)

    lnw_bc = np.ascontiguousarray(np.broadcast_to(lnwv, (128, D)))
    lnb_bc = np.ascontiguousarray(np.broadcast_to(lnbv, (128, D)))
    w1T = np.ascontiguousarray(w1.T)
    w2T = np.ascontiguousarray(w2.T)
    cwr = cw[:, 0, :].reshape(32, 128, 3).transpose(1, 0, 2)
    cbr = cb.reshape(32, 128).T[:, :, None]
    cw4 = np.concatenate([cwr, cbr], axis=2)
    diag_host = np.zeros((128, 32, 4, 64), np.float32)
    for p in range(128):
        diag_host[p, :, :, p % 64] = cw4[p]

    import os
    _upto = int(os.environ.get("KERNEL_UPTO", "99"))
    _ksub = int(os.environ.get("KERNEL_KSUB", "9999"))
    _dbg = os.environ.get("KERNEL_DBG", "") == "1"
    if _NC_CACHE[0] is None:
        _NC_CACHE[0] = build_program(_upto, _ksub, _dbg)
    nc = _NC_CACHE[0]
    kernel._DBG_RES = [None]

    def pack(w8):
        wp = w8.reshape(4, 2, D, E).transpose(0, 2, 1, 3).reshape(4, D, 128)
        return np.ascontiguousarray(wp.reshape(4, 8, 128, 128).transpose(2, 0, 1, 3))

    in_maps = []
    for c in range(8):
        b, r = c // 2, c % 2
        hsl = slice(8 * r, 8 * r + 8)
        wv8 = wva[hsl].transpose(1, 0, 2).reshape(D, 512)
        wv_sb = np.ascontiguousarray(wv8.reshape(8, 128, 512).transpose(1, 0, 2))
        wo8 = woa[512 * r:512 * (r + 1), :]
        wo_sb = np.ascontiguousarray(wo8.reshape(4, 128, D).transpose(1, 0, 2))
        if r == 0:
            wsel = np.array([[0, 0], [0, 1], [0, 0], [0, 1]], np.float32)
        else:
            wsel = np.array([[1, 0], [0, 0], [1, 0], [0, 0]], np.float32)
        in_maps.append({
            "x": np.ascontiguousarray(x[b]),
            "wqp": pack(wqa[hsl]), "wkp": pack(wka[hsl]),
            "wv8": wv_sb, "wo8": wo_sb,
            "lnw_bc": lnw_bc, "lnb_bc": lnb_bc,
            "w1T": w1T, "w2T": w2T,
            "diag_in": diag_host,
            "wsel": wsel,
        })
    res = run_bass_kernel_spmd(nc, in_maps, core_ids=list(range(8)))
    import kernel as _k
    _k._DBG_RES = res
    y = np.empty((4, S, D), np.float32)
    for c in range(8):
        b, r = c // 2, c % 2
        y[b, r * T:(r + 1) * T, :] = res.results[c]["out"]
    return y


# version 07
# speedup vs baseline: 1.0491x; 1.0491x over previous
import os
import numpy as np
import concourse.bass as bass
import concourse.mybir as mybir
from concourse.alu_op_type import AluOpType as OP
from concourse.bass_utils import run_bass_kernel_spmd

F32 = mybir.dt.float32
BF16 = mybir.dt.bfloat16
AF = mybir.ActivationFunctionType
AX = mybir.AxisListType.X

S, D, E, HP, DH = 4096, 1024, 64, 4, 4096
T = S // 2
EPS = 1e-5
MAGIC = float(np.float32(1.5 * 2 ** 23))
NT = S // 128
FFN_CHUNKS = [(i * 254, 254) for i in range(8)] + [(2032, 16)]
ENGS = ("tensor", "vector", "scalar", "sync", "gpsimd")
GROUPS = [[0, 1], [2, 3], [4, 5], [6, 7]]

_NC_CACHE = [None]


def build_program(upto=99, ksub=10**9, dbg=False):
    nc = bass.Bass("TRN2", target_bir_lowering=False)
    x_in = nc.dram_tensor("x", [S, D], F32, kind="ExternalInput").ap()
    wqp_in = nc.dram_tensor("wqp", [128, HP, 8, 128], F32, kind="ExternalInput").ap()
    wkp_in = nc.dram_tensor("wkp", [128, HP, 8, 128], F32, kind="ExternalInput").ap()
    wv_in = nc.dram_tensor("wv8", [128, 8, 512], F32, kind="ExternalInput").ap()
    wo_in = nc.dram_tensor("wo8", [128, HP, 1024], F32, kind="ExternalInput").ap()
    lnw_in = nc.dram_tensor("lnw_bc", [128, D], F32, kind="ExternalInput").ap()
    lnb_in = nc.dram_tensor("lnb_bc", [128, D], F32, kind="ExternalInput").ap()
    w1T_in = nc.dram_tensor("w1T", [D, DH], F32, kind="ExternalInput").ap()
    w2T_in = nc.dram_tensor("w2T", [DH, D], F32, kind="ExternalInput").ap()
    diag_in = nc.dram_tensor("diag_in", [128, 32, 4, 64], F32, kind="ExternalInput").ap()
    wsel_in = nc.dram_tensor("wsel", [4, 2], F32, kind="ExternalInput").ap()
    out = nc.dram_tensor("out", [T, D], F32, kind="ExternalOutput").ap()
    cin = nc.dram_tensor("cin", [S, D], F32, kind="Internal").ap()
    rs_pad = nc.dram_tensor("rs_pad", [T + 2, D], F32, kind="Internal", addr_space="Local").ap()
    ag_in = nc.dram_tensor("ag_in", [2, D], F32, kind="Internal").ap()
    ag_out = nc.dram_tensor("ag_out", [4, D], F32, kind="Internal", addr_space="Local").ap()
    w1q_h = nc.dram_tensor("w1q_h", [D, DH], BF16, kind="Internal").ap()
    w2q_h = nc.dram_tensor("w2q_h", [DH, D], BF16, kind="Internal").ap()
    if dbg:
        dbg_y = nc.dram_tensor("dbg_y", [128, 8, 512], BF16, kind="ExternalOutput").ap()
        dbg_y2 = nc.dram_tensor("dbg_y2", [128, 8, 512], BF16, kind="ExternalOutput").ap()
        dbg_kq = nc.dram_tensor("dbg_kq", [128, 2, S], BF16, kind="ExternalOutput").ap()
        dbg_at = nc.dram_tensor("dbg_at", [128, HP, S], BF16, kind="ExternalOutput").ap()
        dbg_cin = nc.dram_tensor("dbg_cin", [S, D], F32, kind="ExternalOutput").ap()
        dbg_rs = nc.dram_tensor("dbg_rs", [T + 2, D], F32, kind="ExternalOutput").ap()

    cnt = {}

    def bump(sem, k=1):
        cnt[sem] = cnt.get(sem, 0) + k
        return cnt[sem]

    def val(sem):
        return cnt.get(sem, 0)

    from contextlib import ExitStack
    with ExitStack() as top:
        et = top.enter_context
        sems = {n: et(nc.semaphore(n)) for n in
                ("sd", "sg", "so", "pe", "ve", "ac", "gp", "co",
                 "sdx0", "sdx1", "swq0", "swq1", "sst0", "sst1",
                 "sci0", "sci1", "sgk0", "sgk1")}

        ops = {e: [] for e in ENGS}
        mute = {"on": False}

        def phase_mark(n):
            if n > upto:
                mute["on"] = True

        def submark(k):
            if k > ksub:
                mute["on"] = True

        BLANKET = {"vector": "ve", "scalar": "ac", "gpsimd": "gp"}

        def emit(eng, meth, *args, wait=(), inc=None, inck=1, **kw):
            if mute["on"]:
                return cnt.get(inc, 0)
            if meth == "dma_start":
                inck = 16
            bsem = BLANKET.get(eng)
            if bsem is not None and meth not in ("dma_start", "collective_compute"):
                wait = tuple(wait) + ((bsem, val(bsem)),)
                if inc is None:
                    inc = bsem
            v = bump(inc, inck) if inc else None
            wmax = {}
            for s, w in wait:
                if w > wmax.get(s, 0):
                    wmax[s] = w

            def fn(E, meth=meth, args=args, kw=kw, wait=tuple(wmax.items()),
                   inc=inc, inck=inck):
                for s, w in wait:
                    E.wait_ge(sems[s], w)
                ins = getattr(E, meth)(*args, **kw)
                if inc:
                    ins.then_inc(sems[inc], inck)
            ops[eng].append(fn)
            return v

        def waitonly(eng, *pairs):
            if mute["on"]:
                return
            def fn(E, pairs=tuple(pairs)):
                for s, w in pairs:
                    if w > 0:
                        E.wait_ge(sems[s], w)
            ops[eng].append(fn)

        def finalize_debug(buf):
            if not mute["on"] or mute.get("done"):
                return
            mute["on"] = False
            for sname in ("sd", "sg", "so", "pe", "ve", "ac", "gp", "co", "sdx0", "sdx1", "swq0", "swq1", "sst0", "sst1", "sci0", "sci1", "sgk0", "sgk1"):
                if val(sname) > 0:
                    waitonly("sync", (sname, val(sname)))
            emit("sync", "dma_start", out[0:128, :], buf, inc="so")
            waitonly("sync", ("so", val("so")))
            mute["on"] = True
            mute["done"] = True

        def run_block():
            with nc.Block() as block:
                @block.tensor
                def _(t):
                    for fn in ops["tensor"]:
                        fn(t)

                @block.vector
                def _(v):
                    for fn in ops["vector"]:
                        fn(v)

                @block.scalar
                def _(sc):
                    for fn in ops["scalar"]:
                        fn(sc)

                @block.sync
                def _(sy):
                    for fn in ops["sync"]:
                        fn(sy)

                @block.gpsimd
                def _(g):
                    for fn in ops["gpsimd"]:
                        fn(g)
            for e in ENGS:
                ops[e] = []

        stat_iv = et(nc.sbuf_tensor("stat_iv", [128, NT], F32))
        stat_b1 = et(nc.sbuf_tensor("stat_b1", [128, NT], F32))
        stat_m = et(nc.sbuf_tensor("stat_m", [128, NT], F32))
        stat_R = et(nc.sbuf_tensor("stat_R", [128, NT], F32))
        stat_t1 = et(nc.sbuf_tensor("stat_t1", [128, NT], F32))
        stat_t2 = et(nc.sbuf_tensor("stat_t2", [128, NT], F32))
        stat_t3 = et(nc.sbuf_tensor("stat_t3", [128, NT], F32))
        mbm = et(nc.sbuf_tensor("mbm", [128, 896], BF16))
        idb = et(nc.sbuf_tensor("idb", [128, 128], BF16))
        idf = et(nc.sbuf_tensor("idf", [128, 128], F32))
        esh = et(nc.sbuf_tensor("esh", [128, 128], BF16))
        ones2 = et(nc.sbuf_tensor("ones2", [128, 2], BF16))
        onesr = et(nc.sbuf_tensor("onesr", [1, 128], F32))
        onesr2 = et(nc.sbuf_tensor("onesr2", [65, 128], F32))
        onesc = et(nc.sbuf_tensor("onesc", [128, 1], F32))
        onesb = et(nc.sbuf_tensor("onesb", [128, 256], BF16))
        wselb = et(nc.sbuf_tensor("wselb", [4, 2], F32))
        rowb = et(nc.sbuf_tensor("rowb", [1, 16], F32))
        sw_bc = et(nc.sbuf_tensor("sw_bc", [128, 2], F32))
        swr_bc = et(nc.sbuf_tensor("swr_bc", [128, 2], F32))
        negc = et(nc.sbuf_tensor("negc", [128, 2], F32))
        magicc = et(nc.sbuf_tensor("magicc", [128, 1], F32))
        stgm = et(nc.sbuf_tensor("stgm", [128, 2], F32))
        stgr = et(nc.sbuf_tensor("stgr", [128, 2], F32))
        stgiv = et(nc.sbuf_tensor("stgiv", [128, 2], F32))
        stgb1 = et(nc.sbuf_tensor("stgb1", [128, 2], F32))
        wsums = et(nc.sbuf_tensor("wsums", [128, 49], F32))
        wcol = et(nc.sbuf_tensor("wcol", [128, 2], F32))

        pp_a = et(nc.psum_tensor("pp_a", [128, 1024], F32))
        pp_b = et(nc.psum_tensor("pp_b", [128, 1024], F32))
        pp4 = et(nc.psum_tensor("pp4", [128, 512], F32))
        pp5 = et(nc.psum_tensor("pp5", [128, 512], F32))
        pp6 = et(nc.psum_tensor("pp6", [128, 512], F32))
        pp7 = et(nc.psum_tensor("pp7", [128, 512], F32))

        with ExitStack() as actx:
            ea = actx.enter_context
            y_fm = ea(nc.sbuf_tensor("y_fm", [128, 8, S], BF16))
            v_sb = ea(nc.sbuf_tensor("v_sb", [128, 32, 520], BF16))
            k_hp = ea(nc.sbuf_tensor("k_hp", [128, S], BF16))
            q_hp = ea(nc.sbuf_tensor("q_hp", [128, S], BF16))
            attn = ea(nc.sbuf_tensor("attn", [128, HP, S], BF16))
            wqkp = ea(nc.sbuf_tensor("wqkp_s", [128, 2, 2, 8, 128], BF16))
            x0t = [ea(nc.sbuf_tensor("x0t%d" % i, [128, D], F32)) for i in range(2)]
            s1c = ea(nc.sbuf_tensor("s1c", [128, NT], F32))
            s2c = ea(nc.sbuf_tensor("s2c", [128, NT], F32))
            ptile = ea(nc.sbuf_tensor("ptile", [128, 2, 2, 512], BF16))
            accs = ea(nc.sbuf_tensor("accs", [65, 2, 512], F32))
            kq2 = ea(nc.sbuf_tensor("kq2", [128, 2], F32))
            scr = ea(nc.sbuf_tensor("scr", [128, 8192], F32))
            scr_ap = scr.ap()
            scrb = scr_ap.bitcast(BF16)
            xsq = [scr_ap[:, 0:1024], scr_ap[:, 1024:2048]]
            ytmp = scr_ap[:, 0:1024]
            ytk = [scrb[:, 4096:5120], scrb[:, 5120:6144]]
            lnw = scr_ap[:, 6144:7168]
            lnb = scr_ap[:, 7168:8192]
            wv8 = scrb[:, 8192:12288].rearrange("p (c n) -> p c n", c=8)
            sqs_nb = [scrb[:, 12288:12800], scrb[:, 12800:13312]]
            wbuf = [scr_ap[:, 0:2048], scr_ap[:, 2048:4096]]
            wqo = [scrb[:, 8192:10240], scrb[:, 10240:12288]]
            rlA64 = scr_ap[64:65, 6656:7168]
            rlB64 = scr_ap[64:65, 7680:8192]
            atmp = scrb[0:64, 14336:14848]
            yv = y_fm.ap().rearrange("p a b -> p (a b)")
            yvf = yv.bitcast(F32)
            wo8 = yv[:, 0:4096].rearrange("p (a b) -> p a b", a=HP)
            u25 = yvf[:, 2048:3072]
            n25 = yvf[:, 3072:4096]
            x1p = [yvf[:, 4096:5120], yvf[:, 5120:6144]]
            x1halo = scr_ap[0:2, 0:1024]
            agsb = scr_ap[0:4, 1024:2048]

            emit("sync", "dma_start", lnw[:], lnw_in, inc="sd")
            emit("sync", "dma_start", lnb[:], lnb_in, inc="sd")
            
            emit("sync", "dma_start", wselb[:], wsel_in, inc="sd")
            d_w0 = val("sd")
            kq_load = {}
            for h0 in range(2):
                sk = "sgk%d" % h0
                emit("gpsimd", "dma_start", wqkp[:, h0, 0], wkp_in[:, h0], inc=sk)
                emit("gpsimd", "dma_start", wqkp[:, h0, 1], wqp_in[:, h0], inc=sk)
                kq_load[h0] = val(sk)
            emit("gpsimd", "dma_start", wv8[:, :, :], wv_in, inc="sg")
            d_wg = val("sg")
            emit("gpsimd", "memset", mbm[:], 1.0, inc="gp")
            emit("gpsimd", "affine_select", out=mbm[:], in_=mbm[:], compare_op=OP.is_ge,
                 fill=0.0, base=-384, pattern=[[1, 896]], channel_multiplier=-1,
                 wait=[("gp", val("gp"))])
            emit("gpsimd", "memset", idb[:], 0.0, inc="gp")
            emit("gpsimd", "affine_select", out=idb[:], in_=idb[:], compare_op=OP.not_equal,
                 fill=1.0, base=0, pattern=[[-1, 128]], channel_multiplier=1,
                 wait=[("gp", val("gp"))])
            emit("gpsimd", "memset", idf[:], 0.0, inc="gp")
            emit("gpsimd", "affine_select", out=idf[:], in_=idf[:], compare_op=OP.not_equal,
                 fill=1.0, base=0, pattern=[[-1, 128]], channel_multiplier=1,
                 wait=[("gp", val("gp"))])
            emit("gpsimd", "memset", esh[:], 0.0, inc="gp")
            emit("gpsimd", "affine_select", out=esh[:], in_=esh[:], compare_op=OP.not_equal,
                 fill=1.0, base=64, pattern=[[-1, 128]], channel_multiplier=1,
                 wait=[("gp", val("gp"))])
            emit("gpsimd", "memset", ones2[:], 0.0, inc="gp")
            emit("gpsimd", "memset", ones2[0:64, 0:1], 1.0,
                 wait=[("gp", val("gp"))], inc="gp")
            emit("gpsimd", "memset", ones2[64:128, 1:2], 1.0,
                 wait=[("gp", val("gp"))])
            emit("gpsimd", "memset", onesr[:], 1.0)
            emit("gpsimd", "memset", onesr2[:], 1.0)
            emit("gpsimd", "memset", magicc[:], MAGIC)
            emit("gpsimd", "memset", onesb[:], 1.0)
            emit("gpsimd", "memset", onesc[:], 1.0)
            emit("gpsimd", "memset", v_sb[:], 1.0, inc="gp")
            emit("gpsimd", "memset", kq2[:], 0.0, inc="gp")
            emit("gpsimd", "memset", wsums[:], 0.0, inc="gp")
            gp_consts = val("gp")

            ve_p1a = {}
            for j in range(NT):
                bi = j % 2
                sx = "sdx%d" % bi
                w = [("ve", ve_p1a.get(j - 2, 0))]
                dv = emit("sync", "dma_start", x0t[bi][:], x_in[j * 128:(j + 1) * 128, :],
                          wait=w, inc=sx)
                emit("scalar", "square", xsq[bi][:], x0t[bi][:],
                     wait=[(sx, dv), ("ve", ve_p1a.get(j - 2, 0))], inc="ac")
                a1 = val("ac")
                emit("vector", "tensor_reduce", s1c[:, j:j + 1], x0t[bi][:], AX, OP.add,
                     wait=[(sx, dv)])
                emit("vector", "tensor_reduce", s2c[:, j:j + 1], xsq[bi][:], AX, OP.add,
                     wait=[("ac", a1)], inc="ve")
                ve_p1a[j] = val("ve")

            def newton_rsqrt(dst, src, tmp_a, tmp_b):
                emit("scalar", "sqrt", tmp_a[:], src[:],
                     wait=[("ve", val("ve"))], inc="ac")
                a0 = val("ac")
                emit("vector", "reciprocal", dst[:], tmp_a[:], wait=[("ac", a0)])
                for _ in range(2):
                    emit("vector", "tensor_mul", tmp_b[:], dst[:], dst[:])
                    emit("vector", "tensor_mul", tmp_b[:], tmp_b[:], src[:])
                    emit("vector", "tensor_scalar", tmp_b[:], tmp_b[:], -0.5, 1.5,
                         OP.mult, OP.add)
                    emit("vector", "tensor_mul", dst[:], dst[:], tmp_b[:])

            waitonly("vector", ("ve", ve_p1a[NT - 1]))
            emit("vector", "tensor_scalar", stat_m[:], s1c[:], 1.0 / D, None, OP.mult)
            emit("vector", "tensor_scalar", stat_t1[:], s2c[:], 1.0 / D, None, OP.mult)
            emit("vector", "tensor_mul", stat_t2[:], stat_m[:], stat_m[:])
            emit("vector", "tensor_sub", stat_t1[:], stat_t1[:], stat_t2[:])
            emit("vector", "tensor_scalar", stat_t2[:], stat_t1[:], EPS, None, OP.add,
                 inc="ve")
            newton_rsqrt(stat_iv, stat_t2, stat_t3, s1c)
            emit("vector", "scalar_tensor_tensor", stat_b1[:], stat_m[:], -1.0,
                 stat_iv[:], OP.mult, OP.mult)
            emit("vector", "scalar_tensor_tensor", stat_t2[:], stat_t2[:], EPS,
                 stat_t1[:], OP.mult, OP.add, inc="ve")
            newton_rsqrt(stat_R, stat_t2, stat_t3, s1c)
            emit("vector", "memset", rowb[:], 0.0, inc="ve")
            ve_stats = val("ve")

            phase_mark(2)
            waitonly("sync", ("ve", ve_stats), ("sg", d_wg), ("gp", gp_consts),
                     ("sd", d_w0))
            pe_tp = {}
            ve_cp = {}
            for j in range(NT):
                bi = j % 2
                ve_prev_y = cnt.get("ve_y_%d" % (j - 2), 0)
                sx = "sdx%d" % bi
                dv = emit("sync", "dma_start", x0t[bi][:], x_in[j * 128:(j + 1) * 128, :],
                          wait=[("ve", ve_prev_y)], inc=sx)
                wv_ = [(sx, dv), ("ve", ve_stats)]
                if (j - 2, 7) in pe_tp:
                    wv_.append(("pe", pe_tp[(j - 2, 7)]))
                emit("vector", "tensor_copy", stgm[:, bi:bi + 1], stat_m[:, j:j + 1],
                     wait=wv_)
                emit("vector", "tensor_copy", stgr[:, bi:bi + 1], stat_R[:, j:j + 1])
                emit("vector", "scalar_tensor_tensor", ytmp[:], x0t[bi][:],
                     stgm[:, bi:bi + 1], lnw[:], OP.subtract, OP.mult)
                emit("vector", "scalar_tensor_tensor", ytk[bi][:], ytmp[:],
                     stgr[:, bi:bi + 1], lnb[:], OP.mult, OP.add, inc="ve")
                cnt["ve_y_%d" % j] = val("ve")
                for c in range(8):
                    pv8 = pp4 if c % 2 == 0 else pp5
                    w = [("ve", cnt["ve_y_%d" % j])]
                    key = (j, c - 2) if c >= 2 else ((j - 1, c + 6) if j >= 1 else None)
                    if key is not None and key in ve_cp:
                        w.append(("ve", ve_cp[key]))
                    emit("tensor", "matmul", pv8[:, 0:128],
                         ytk[bi][:, c * 128:(c + 1) * 128], idb[:],
                         start=True, stop=True, wait=w, inc="pe")
                    pe_tp[(j, c)] = val("pe")
                    emit("vector", "tensor_copy", y_fm[:, c, j * 128:(j + 1) * 128],
                         pv8[:, 0:128], wait=[("pe", pe_tp[(j, c)])], inc="ve")
                    ve_cp[(j, c)] = val("ve")

            if dbg and os.environ.get("KERNEL_DBG_EARLY", "") == "2" and not mute["on"]:
                emit("sync", "dma_start", dbg_y, y_fm[:, :, 1536:2048],
                     wait=[("pe", val("pe")), ("ve", val("ve")), ("ac", val("ac"))],
                     inc="so")
                emit("sync", "dma_start", dbg_y2, y_fm[:, :, 1536:2048], inc="so")
                waitonly("sync", ("so", val("so")))
            phase_mark(3)
            ve_all_cp = val("ve")
            waitonly("tensor", ("ve", ve_all_cp), ("sg", d_wg), ("gp", gp_consts))
            ve_vd = {}
            for kt in range(NT):
                ppv = pp6 if kt % 2 == 0 else pp7
                w = [("ve", ve_vd.get(kt - 2, 0))]
                for ch in range(8):
                    emit("tensor", "matmul", ppv[:, 0:512],
                         y_fm[:, ch, kt * 128:(kt + 1) * 128], wv8[:, ch, 0:512],
                         start=(ch == 0), stop=(ch == 7), wait=(w if ch == 0 else ()),
                         inc=("pe" if ch == 7 else None))
                pv_ = val("pe")
                if os.environ.get("KERNEL_NOVEVAC", "") == "1":
                    emit("vector", "tensor_copy", v_sb[:, kt, 0:512], ppv[:, 0:512],
                         wait=[("pe", pv_)], inc="ve")
                else:
                    src = ppv[:, 0:512].rearrange("p (h e) -> p h e", h=8)
                    dst = v_sb[:, kt, 0:520].rearrange("p (h e) -> p h e", e=65)[:, :, 0:64]
                    emit("vector", "tensor_copy", dst, src, wait=[("pe", pv_)], inc="ve")
                ve_vd[kt] = val("ve")

            NOWQ = os.environ.get("KERNEL_NOWQ", "") == "1"
            wq_chunks = ([("w1", (j, h)) for j in range(8) for h in range(2)]
                         + [("w2", j) for j in range(32)])
            wq = {"li": 0, "ci": 0, "ldv": [], "use": [], "st": [], "mode": "red",
                  "gate": [("pe", val("pe")), ("ve", val("ve"))]}

            def wq_step(n):
                if NOWQ:
                    return
                for _ in range(n):
                    while wq["li"] < len(wq_chunks) and wq["li"] < wq["ci"] + 2:
                        i = wq["li"]
                        kind, j = wq_chunks[i]
                        if kind == "w1":
                            jj, hh = j
                            width = 2048
                            src = w1T_in[jj * 128:(jj + 1) * 128,
                                         hh * 2048:(hh + 1) * 2048]
                        else:
                            width = D
                            src = w2T_in[j * 128:(j + 1) * 128, :]
                        w = [("ve", wq["use"][i - 2])] if i >= 2 else list(wq["gate"])
                        if wq["mode"] == "quant" and i >= 2:
                            w.append(("sst%d" % (i % 2), wq["st"][i - 2]))
                        dv = emit("sync", "dma_start", wbuf[i % 2][:, 0:width], src,
                                  wait=w, inc="swq%d" % (i % 2))
                        wq["ldv"].append(dv)
                        wq["li"] += 1
                    i = wq["ci"]
                    if i >= len(wq_chunks):
                        return
                    wq["ci"] += 1
                    kind, j = wq_chunks[i]
                    width = 2048 if kind == "w1" else D
                    buf = wbuf[i % 2]
                    dv = wq["ldv"][i]
                    if wq["mode"] == "red":
                        ci_col = (2 * j[0] + j[1]) if kind == "w1" else 16 + j
                        emit("vector", "tensor_reduce", wsums[:, ci_col:ci_col + 1],
                             buf[:, 0:width], AX, OP.add,
                             apply_absolute_value=True,
                             wait=[("swq%d" % (i % 2), dv)] +
                                  ([("gp", gp_consts)] if i == 0 else []),
                             inc="ve")
                        wq["use"].append(val("ve"))
                    else:
                        swcol = sw_bc[:, 0:1] if kind == "w1" else sw_bc[:, 1:2]
                        emit("scalar", "activation", buf[:, 0:width], buf[:, 0:width],
                             AF.Identity, bias=magicc[:], scale=swcol,
                             wait=[("swq%d" % (i % 2), dv), ("ve", wq["sw_ready"])] +
                                  ([("sst%d" % (i % 2), wq["st"][i - 2])] if i >= 2 else []),
                             inc="ac")
                        av = val("ac")
                        emit("vector", "tensor_scalar", buf[:, 0:width], buf[:, 0:width],
                             MAGIC, 1.0, OP.subtract, OP.min, wait=[("ac", av)])
                        emit("vector", "tensor_scalar", wqo[i % 2][:, 0:width],
                             buf[:, 0:width], -1.0, None, OP.max, inc="ve")
                        wq["use"].append(val("ve"))
                        qvv = val("ve")
                        if kind == "w1":
                            jj, hh = j
                            hdst = w1q_h[jj * 128:(jj + 1) * 128,
                                         hh * 2048:(hh + 1) * 2048]
                        else:
                            hdst = w2q_h[j * 128:(j + 1) * 128, :]
                        emit("sync", "dma_start", hdst, wqo[i % 2][:, 0:width],
                             wait=[("ve", qvv)], inc="sst%d" % (i % 2))
                        wq["st"].append(val("sst%d" % (i % 2)))

            def wq_meanmath(extra_pe_wait):
                if NOWQ:
                    return
                emit("vector", "tensor_reduce", wcol[:, 0:1], wsums[:, 0:16], AX, OP.add)
                emit("vector", "tensor_reduce", wcol[:, 1:2], wsums[:, 16:48], AX,
                     OP.add, inc="ve")
                vv = val("ve")
                emit("tensor", "matmul", pp4[0:2, 0:1], wcol[:, 0:2], onesc[:],
                     start=True, stop=True,
                     wait=[("ve", vv)] + extra_pe_wait, inc="pe")
                pv_ = val("pe")
                emit("vector", "tensor_copy", wsums[0:2, 48:49], pp4[0:2, 0:1],
                     wait=[("pe", pv_)], inc="ve")
                vv = val("ve")
                emit("tensor", "matmul", pp5[0:1, 0:128], wsums[:, 48:49], idf[:],
                     start=True, stop=True, wait=[("ve", vv)], inc="pe")
                pv_ = val("pe")
                emit("vector", "tensor_copy", rowb[0:1, 10:12], pp5[0:1, 0:2],
                     wait=[("pe", pv_)])
                emit("vector", "tensor_scalar", rowb[0:1, 10:12], rowb[0:1, 10:12],
                     1.0 / (D * DH), 1e-5, OP.mult, OP.max)
                emit("vector", "reciprocal", rowb[0:1, 12:14], rowb[0:1, 10:12],
                     inc="ve")
                vv = val("ve")
                for i, src_off in enumerate((12, 10)):
                    for k in range(2):
                        emit("tensor", "matmul", pp5[:, 4 + i * 2 + k:5 + i * 2 + k],
                             onesr[0:1, :], rowb[0:1, src_off + k:src_off + k + 1],
                             start=True, stop=True,
                             wait=([("ve", vv)] if (i == 0 and k == 0) else ()),
                             inc=("pe" if (i == 1 and k == 1) else None))
                pv_ = val("pe")
                emit("vector", "tensor_copy", sw_bc[:, 0:2], pp5[:, 4:6],
                     wait=[("pe", pv_)])
                emit("vector", "tensor_copy", swr_bc[:, 0:2], pp5[:, 6:8], inc="ve")
                wq["sw_ready"] = val("ve")

            if dbg and os.environ.get("KERNEL_DBG_EARLY", "") == "1" and not mute["on"]:
                emit("sync", "dma_start", dbg_y, y_fm[:, :, 1536:2048],
                     wait=[("pe", val("pe")), ("ve", val("ve")), ("ac", val("ac"))],
                     inc="so")
                waitonly("sync", ("so", val("so")))
            exp_done = []
            av_done = []
            gidx = [0]
            last_fin = [0]
            ve_ppb_rd = [val("ve")]
            ve_accs = [val("ve")]
            ve_ppb_mul = [val("ve")]
            rec_ac = [0]
            rl_rd = [0]
            ktc = [0]
            s_done_l = {}
            pend_fin = [None]

            for hp in range(HP):
                phase_mark(4 + hp)
                last_exp = exp_done[-1] if exp_done else 0
                ve_kq = {}
                for wi_, (which, dst) in enumerate((("k", k_hp), ("q", q_hp))):
                    for ntl in range(8):
                        ppv = pp6 if ntl % 2 == 0 else pp7
                        w = [("ac", last_exp), ("sgk%d" % (hp % 2), kq_load[hp])]
                        key = (which, ntl - 2)
                        if key in ve_kq:
                            w.append(("ve", ve_kq[key]))
                        if which == "q" and ntl < 2:
                            w.append(("ve", ve_kq[("k", 6 + ntl)]))
                        if hp == 0 and which == "k" and ntl < 2:
                            w.append(("ve", ve_vd[NT - 2 + ntl]))
                        for ch in range(8):
                            emit("tensor", "matmul", ppv[:, 0:512],
                                 wqkp[:, hp % 2, wi_, ch, 0:128],
                                 y_fm[:, ch, ntl * 512:(ntl + 1) * 512],
                                 start=(ch == 0), stop=(ch == 7),
                                 wait=(w if ch == 0 else ()),
                                 inc=("pe" if ch == 7 else None))
                        pv_ = val("pe")
                        emit("vector", "tensor_copy", dst[:, ntl * 512:(ntl + 1) * 512],
                             ppv[:, 0:512], wait=[("pe", pv_)], inc="ve")
                        ve_kq[(which, ntl)] = val("ve")
                if hp + 2 < HP:
                    pvkq = val("pe")
                    sk = "sgk%d" % (hp % 2)
                    emit("gpsimd", "dma_start", wqkp[:, hp % 2, 0], wkp_in[:, hp + 2],
                         wait=[("pe", pvkq)], inc=sk)
                    emit("gpsimd", "dma_start", wqkp[:, hp % 2, 1], wqp_in[:, hp + 2],
                         inc=sk)
                    kq_load[hp + 2] = val(sk)
                if hp == HP - 1:
                    cnt["pe_last_kq"] = val("pe")

                pe_norm = {0: 0, 1: 0}
                vr0 = max(ve_ppb_rd[0], last_fin[0])
                ve_red = {0: vr0, 1: vr0}
                nidx = 0
                for which, srcb, coli in (("k", k_hp, 0), ("q", q_hp, 1)):
                    for ntl in range(8):
                        par = nidx % 2
                        col0 = 512 * par
                        wv_ = [("ve", ve_kq[(which, ntl)])]
                        if pe_norm[par]:
                            wv_.append(("pe", pe_norm[par]))
                        emit("scalar", "square", sqs_nb[par][:],
                             srcb[:, ntl * 512:(ntl + 1) * 512], wait=wv_, inc="ac")
                        av = val("ac")
                        emit("tensor", "matmul", pp_b[0:2, col0:col0 + 512],
                             ones2[:], sqs_nb[par][:], start=True, stop=True,
                             wait=[("ac", av), ("ve", ve_red[par])], inc="pe")
                        pe_norm[par] = val("pe")
                        if nidx in (0, 8):
                            ve_red[par] = emit(
                                "vector", "tensor_reduce", kq2[0:2, coli:coli + 1],
                                pp_b[0:2, col0:col0 + 512], AX, OP.max,
                                wait=[("pe", pe_norm[par]), ("gp", gp_consts)],
                                inc="ve")
                        else:
                            ve_red[par] = emit(
                                "vector", "tensor_reduce", stat_t3[0:2, 0:1],
                                pp_b[0:2, col0:col0 + 512], AX, OP.max,
                                wait=[("pe", pe_norm[par])])
                            emit("vector", "tensor_max", kq2[0:2, coli:coli + 1],
                                 kq2[0:2, coli:coli + 1], stat_t3[0:2, 0:1], inc="ve")
                        nidx += 1
                ve_ppb_rd[0] = val("ve")
                vkq = val("ve")
                emit("tensor", "matmul", pp4[0:1, 0:128], kq2[:, 0:1], idf[:],
                     start=True, stop=True,
                     wait=[("ve", vkq), ("ac", last_exp), ("gp", gp_consts)])
                emit("tensor", "matmul", pp4[0:1, 128:256], kq2[:, 1:2], idf[:],
                     start=True, stop=True, inc="pe")
                pv_ = val("pe")
                emit("vector", "tensor_copy", rowb[0:1, 0:2], pp4[0:1, 0:2],
                     wait=[("pe", pv_)])
                emit("vector", "tensor_copy", rowb[0:1, 2:4], pp4[0:1, 128:130])
                emit("vector", "tensor_mul", rowb[0:1, 4:6], rowb[0:1, 0:2],
                     rowb[0:1, 2:4], inc="ve")
                vv = val("ve")
                emit("scalar", "sqrt", rowb[0:1, 6:8], rowb[0:1, 4:6],
                     wait=[("ve", vv)], inc="ac")
                av = val("ac")
                emit("vector", "tensor_scalar", rowb[0:1, 6:8], rowb[0:1, 6:8],
                     -0.125, None, OP.mult, wait=[("ac", av)], inc="ve")
                vv = val("ve")
                for h in range(2):
                    emit("tensor", "matmul", pp4[:, 256 + h:257 + h], onesr[0:1, :],
                         rowb[0:1, 6 + h:7 + h], start=True, stop=True,
                         wait=([("ve", vv)] if h == 0 else ()),
                         inc=("pe" if h == 1 else None))
                pv_ = val("pe")
                emit("vector", "tensor_copy", negc[:, 0:2], pp4[:, 256:258],
                     wait=[("pe", pv_), ("ac", last_exp)], inc="ve")
                ve_negc = val("ve")

                for qb in range(8):
                    nkt = 4 * qb + 4
                    q0 = qb * 512
                    gbase = gidx[0]

                    def S_emit(kt, q0=q0, gbase=gbase):
                        gi = gbase + kt
                        sel = gi % 2
                        sA = pp4 if sel == 0 else pp5
                        sB = pp6 if sel == 0 else pp7
                        w = [("ve", ve_negc)]
                        if gi >= 2:
                            w.append(("ac", exp_done[gi - 2]))
                        emit("tensor", "matmul", sA[:, 0:512],
                             k_hp[0:64, kt * 128:(kt + 1) * 128],
                             q_hp[0:64, q0:q0 + 512],
                             start=True, stop=True, tile_position=(0, 0), wait=w)
                        emit("tensor", "matmul", sB[:, 0:512],
                             k_hp[64:128, kt * 128:(kt + 1) * 128],
                             q_hp[64:128, q0:q0 + 512],
                             start=True, stop=True, tile_position=(64, 0),
                             inc="pe")
                        s_done_l[gi] = val("pe")

                    def tail_emit(kt, q0=q0, gbase=gbase, nkt=nkt, hp=hp, qb=qb):
                        gi = gbase + kt
                        sel = gi % 2
                        sA = pp4 if sel == 0 else pp5
                        sB = pp6 if sel == 0 else pp7
                        we = [("pe", s_done_l[gi])]
                        if gi >= 2:
                            we.append(("pe", av_done[gi - 2]))
                        emit("scalar", "activation", ptile[:, 0, sel, :],
                             sA[:, 0:512], AF.Exp, bias=negc[:, 0:1], scale=0.125,
                             wait=we)
                        emit("scalar", "activation", ptile[:, 1, sel, :],
                             sB[:, 0:512], AF.Exp, bias=negc[:, 1:2], scale=0.125,
                             inc="ac", inck=2)
                        exp_done.append(val("ac"))
                        avw = [("ac", exp_done[gi])]
                        if kt >= 4 * qb:
                            dj = kt - 4 * qb
                            msl = mbm[:, 384 - 128 * dj:896 - 128 * dj]
                            emit("vector", "tensor_mul", ptile[:, 0, sel, :],
                                 ptile[:, 0, sel, :], msl,
                                 wait=[("ac", exp_done[gi])])
                            emit("vector", "tensor_mul", ptile[:, 1, sel, :],
                                 ptile[:, 1, sel, :], msl, inc="ve")
                            avw = [("ve", val("ve"))]
                        avw0 = list(avw)
                        if kt == 0:
                            avw0.append(("ve", ve_accs[0]))
                        emit("tensor", "matmul", pp_a[0:65, 0:512],
                             v_sb[:, kt, 130 * hp:130 * hp + 65],
                             ptile[:, 0, sel, :],
                             start=(kt == 0), stop=(kt == nkt - 1), wait=avw0)
                        emit("tensor", "matmul", pp_a[0:65, 512:1024],
                             v_sb[:, kt, 130 * hp + 65:130 * hp + 130],
                             ptile[:, 1, sel, :],
                             start=(kt == 0), stop=(kt == nkt - 1), inc="pe")
                        av_done.append(val("pe"))
                        ktc[0] += 1
                        if ktc[0] % 6 == 0:
                            wq_step(1)

                    for kt in range(nkt):
                        S_emit(kt)
                        if kt == 1 and pend_fin[0] is not None:
                            pend_fin[0]()
                            pend_fin[0] = None
                        if kt >= 1:
                            tail_emit(kt - 1)
                    if pend_fin[0] is not None:
                        pend_fin[0]()
                        pend_fin[0] = None
                    tail_emit(nkt - 1)
                    gidx[0] += nkt
                    pv_ = av_done[-1]
                    emit("vector", "tensor_copy", accs[:, 0, :], pp_a[0:65, 0:512],
                         wait=[("pe", pv_)])
                    emit("vector", "tensor_copy", accs[:, 1, :], pp_a[0:65, 512:1024],
                         inc="ve")
                    ve_accs[0] = val("ve")
                    emit("vector", "reciprocal", rlA64, accs[64:65, 0, :],
                         wait=[("pe", rl_rd[0])])
                    emit("vector", "reciprocal", rlB64, accs[64:65, 1, :],
                         inc="ve")
                    rec_ac[0] = val("ve")

                    def fin_pe(hp=hp, q0=q0, va=ve_accs[0], ra=rec_ac[0]):
                        emit("tensor", "matmul", pp_b[0:64, 0:512],
                             onesr2[64:65, 0:64], rlA64, start=True, stop=True,
                             wait=[("ve", max(va, ra)),
                                   ("ve", ve_ppb_mul[0]), ("gp", gp_consts)])
                        emit("tensor", "matmul", pp_b[0:64, 512:1024],
                             onesr2[64:65, 0:64], rlB64, start=True, stop=True,
                             inc="pe")
                        pv2 = val("pe")
                        rl_rd[0] = pv2
                        emit("vector", "tensor_mul", attn[0:64, hp, q0:q0 + 512],
                             accs[0:64, 0, :], pp_b[0:64, 0:512],
                             wait=[("pe", pv2)])
                        emit("vector", "tensor_mul", atmp[0:64, :],
                             accs[0:64, 1, :], pp_b[0:64, 512:1024], inc="ve")
                        vv2 = val("ve")
                        ve_ppb_mul[0] = vv2
                        emit("tensor", "matmul", pp_b[:, 0:512], esh[0:64, 0:128],
                             atmp[0:64, :], start=True, stop=True,
                             wait=[("ve", vv2)], inc="pe")
                        pv3 = val("pe")
                        emit("vector", "tensor_copy",
                             attn[64:128, hp, q0:q0 + 512],
                             pp_b[64:128, 0:512], wait=[("pe", pv3)], inc="ve")
                        last_fin[0] = val("ve")

                    pend_fin[0] = fin_pe

                if pend_fin[0] is not None:
                    pend_fin[0]()
                    pend_fin[0] = None
                if hp == 1:
                    wq_step(48)
                    wq_meanmath([("ac", exp_done[-1]), ("ve", last_fin[0])])
                    wq.update(li=0, ci=0, ldv=[], use=[], st=[], mode="quant",
                              gate=[("ve", val("ve"))])
            wq_step(48)

            if dbg and os.environ.get("KERNEL_DBG_EARLY", "") == "3" and not mute["on"]:
                emit("sync", "dma_start", dbg_y, y_fm[:, :, 1536:2048],
                     wait=[("pe", val("pe")), ("ve", val("ve")), ("ac", val("ac"))],
                     inc="so")
                emit("sync", "dma_start", dbg_kq[:, 0, :], k_hp[:], inc="so")
                emit("sync", "dma_start", dbg_kq[:, 1, :], q_hp[:], inc="so")
                emit("sync", "dma_start", dbg_at, attn[:], inc="so")
                waitonly("sync", ("so", val("so")))
            phase_mark(8)
            wq_done_ve = val("ve")
            wq_done_so = (val("sst0"), val("sst1"))
            if dbg and not mute["on"]:
                emit("sync", "dma_start", dbg_y, y_fm[:, :, 1536:2048],
                     wait=[("pe", val("pe")), ("ve", val("ve")), ("ac", val("ac"))],
                     inc="so")
                emit("sync", "dma_start", dbg_kq[:, 0, :], k_hp[:], inc="so")
                emit("sync", "dma_start", dbg_kq[:, 1, :], q_hp[:], inc="so")
                emit("sync", "dma_start", dbg_at, attn[:], inc="so")
                waitonly("sync", ("so", val("so")))
            emit("gpsimd", "dma_start", wo8, wo_in,
                 wait=[("pe", cnt["pe_last_kq"]), ("ve", last_fin[0])], inc="sg")
            d_wo8 = val("sg")
            ve_x1 = {}
            so_x1 = {}
            for j in range(NT):
                bi = j % 2
                ppw = pp_a if bi == 0 else pp_b
                w = [("ve", last_fin[0])]
                if j >= 2:
                    w.append(("ve", ve_x1[j - 2]))
                if j == 0:
                    w.append(("sg", d_wo8))
                for half in range(2):
                    for hpj in range(HP):
                        emit("tensor", "matmul", ppw[:, half * 512:(half + 1) * 512],
                             attn[:, hpj, j * 128:(j + 1) * 128],
                             wo8[:, hpj, half * 512:(half + 1) * 512],
                             start=(hpj == 0), stop=(hpj == 3),
                             wait=(w if (half == 0 and hpj == 0) else ()),
                             inc=("pe" if (half == 1 and hpj == 3) else None))
                pv_ = val("pe")
                wd = [("ve", ve_x1.get(j - 2, 0))]
                sx = "sdx%d" % bi
                dv = emit("sync", "dma_start", x0t[bi][:], x_in[j * 128:(j + 1) * 128, :],
                          wait=wd, inc=sx)
                emit("vector", "tensor_copy", stgiv[:, bi:bi + 1], stat_iv[:, j:j + 1],
                     wait=[("ac", cnt.get("ac_n25_%d" % (j - 2), 0))])
                emit("vector", "tensor_copy", stgb1[:, bi:bi + 1], stat_b1[:, j:j + 1],
                     inc="ve")
                stg_ve = val("ve")
                emit("scalar", "activation", n25[:], x0t[bi][:], AF.Identity,
                     bias=stgb1[:, bi:bi + 1], scale=stgiv[:, bi:bi + 1],
                     wait=[(sx, dv), ("ve", ve_x1.get(j - 1, 0)), ("ve", stg_ve),
                           ("sg", d_wo8), ("pe", cnt["pe_last_kq"])], inc="ac")
                cnt["ac_n25_%d" % j] = val("ac")
                av = val("ac")
                emit("vector", "tensor_add", u25[:], x0t[bi][:], n25[:],
                     wait=[("ac", av)])
                emit("vector", "scalar_tensor_tensor", x1p[bi][:], u25[:], 0.5,
                     ppw[:, 0:1024], OP.mult, OP.add,
                     wait=[("pe", pv_), ("sci%d" % bi, so_x1.get(j - 2, 0)),
                           ("sst0", wq_done_so[0]), ("sst1", wq_done_so[1]),
                           ("ve", wq_done_ve)], inc="ve")
                ve_x1[j] = val("ve")
                cr = ((j % 16) // 4) * 1024 + (j // 16) * 512 + (j % 4) * 128
                emit("sync", "dma_start", cin[cr:cr + 128, :], x1p[bi][:],
                     wait=[("ve", ve_x1[j])], inc="sci%d" % bi)
                if j == 15:
                    emit("sync", "dma_start", ag_in[0:1, :], x1p[bi][127:128, :],
                         inc="sci%d" % bi)
                if j == 16:
                    emit("sync", "dma_start", ag_in[1:2, :], x1p[bi][0:1, :],
                         inc="sci%d" % bi)
                so_x1[j] = val("sci%d" % bi)

            phase_mark(9)
            emit("gpsimd", "collective_compute", "AllGather", OP.bypass,
                 replica_groups=GROUPS, ins=[ag_in], outs=[ag_out],
                 wait=[("sci1", so_x1[15]), ("sci0", so_x1[16])], inc="co")
            for k in range(4):
                emit("gpsimd", "collective_compute", "ReduceScatter", OP.add,
                     replica_groups=GROUPS,
                     ins=[cin[k * 1024:(k + 1) * 1024, :]],
                     outs=[rs_pad[1 + k * 512:1 + (k + 1) * 512, :]],
                     wait=[("sci0", so_x1[18 + 4 * k]),
                           ("sci1", so_x1[19 + 4 * k])],
                     inc="co")
            dv = emit("sync", "dma_start", agsb[:], ag_out, wait=[("co", 1)],
                      inc="sd")
            emit("tensor", "matmul", pp4[0:2, 0:512], wselb[:], agsb[:, 0:512],
                 start=True, stop=True,
                 wait=[("sd", dv), ("ac", exp_done[-1]), ("ve", last_fin[0])])
            emit("tensor", "matmul", pp5[0:2, 0:512], wselb[:], agsb[:, 512:1024],
                 start=True, stop=True, inc="pe")
            pv_ = val("pe")
            emit("vector", "tensor_copy", x1halo[:, 0:512], pp4[0:2, 0:512],
                 wait=[("pe", pv_)])
            emit("vector", "tensor_copy", x1halo[:, 512:1024], pp5[0:2, 0:512],
                 inc="ve")
            vv = val("ve")
            emit("sync", "dma_start", rs_pad[0:1, :], x1halo[0:1, :],
                 wait=[("ve", vv)], inc="so")
            emit("sync", "dma_start", rs_pad[T + 1:T + 2, :], x1halo[1:2, :],
                 inc="so")
            so_halo = val("so")
            co_base = val("co") - 4

            finalize_debug(x0t[0][:])
            if not mute.get("done"):
                bv_endA = {s_: val(s_) for s_ in ("sd", "sg", "so", "pe", "ve", "ac", "gp", "sdx0", "sdx1", "swq0", "swq1", "sst0", "sst1", "sci0", "sci1", "sgk0", "sgk1")}
                for eng in ENGS:
                    waitonly(eng, *[(s_, v) for s_, v in bv_endA.items() if v > 0])
            run_block()

        with ExitStack() as fctx:
            ef = fctx.enter_context
            w1q = ef(nc.sbuf_tensor("w1q", [128, 8, DH], BF16))
            w2q = ef(nc.sbuf_tensor("w2q", [128, 32, D], BF16))
            diag = ef(nc.sbuf_tensor("diag", [128, 32, 4, 64], BF16))
            win = [ef(nc.sbuf_tensor("win%d" % i, [128, D], F32)) for i in range(2)]
            x1o = win
            n3 = ef(nc.sbuf_tensor("n3", [128, D], F32))
            sq3 = ef(nc.sbuf_tensor("sq3", [128, D], F32))
            t3 = sq3
            x2b = [n3, sq3]
            xqt = ef(nc.sbuf_tensor("xqt", [128, D], BF16))
            xqf = ef(nc.sbuf_tensor("xqf", [128, 8, 256], BF16))
            h1s = ef(nc.sbuf_tensor("h1s", [128, 2, 256], BF16))
            h3 = ef(nc.sbuf_tensor("h3", [128, 32, 254], F32))
            habs = ef(nc.sbuf_tensor("habs", [128, 254], F32))
            hacc = ef(nc.sbuf_tensor("hacc", [128, 254], F32))
            h3q = ef(nc.sbuf_tensor("h3q", [128, 2, 254], BF16))
            scol = ef(nc.sbuf_tensor("scol", [128, 16], F32))
            c2c = ef(nc.sbuf_tensor("c2c", [128, 6], F32))
            stiv = ef(nc.sbuf_tensor("stiv", [128, 2], F32))
            stb2 = ef(nc.sbuf_tensor("stb2", [128, 2], F32))
            sts1 = ef(nc.sbuf_tensor("sts1", [128, 2], F32))
            e2c = ef(nc.sbuf_tensor("e2c", [128, 2], F32))
            xqtf = xqt.ap().bitcast(F32)
            bc1 = xqtf[:, 0:256]
            bc2 = xqtf[:, 256:512]
            r1row = n3.ap()[0:1, 0:256]
            s2row = n3.ap()[0:1, 256:512]

            d1 = emit("sync", "dma_start", w1q[:],
                      w1q_h.rearrange("(c p) n -> p c n", p=128), inc="swq0")
            d2 = emit("sync", "dma_start", w2q[:],
                      w2q_h.rearrange("(c p) n -> p c n", p=128), inc="swq1")
            d3 = emit("gpsimd", "dma_start", diag[:], diag_in, inc="sg")
            emit("vector", "memset", scol[:], 0.0)
            emit("vector", "memset", c2c[:], 0.0, inc="ve")
            ve_diag = val("ve")
            waitonly("tensor", ("ve", ve_diag), ("swq0", d1), ("swq1", d2),
                     ("sg", d3))

            prev = {"epi_ve": 0, "outs": 0, "mm2_pe": 0, "xq_pe": 0}
            for ci, (c0, n) in enumerate(FFN_CHUNKS):
                phase_mark(10 + ci)
                Wn = n + 2
                ta = min(128, Wn)
                tb = Wn - ta
                na = min(128, n)
                nb = n - na
                wd = [("ve", prev["epi_ve"]), ("so", prev["outs"]),
                      ("co", co_base + [0, 0, 1, 1, 2, 2, 3, 3, 3][ci] + 1)]
                if ci in (0, 8):
                    wd.append(("so", so_halo))
                dva = emit("sync", "dma_start", win[0][0:ta, :],
                           rs_pad[c0:c0 + ta, :], wait=wd, inc="sd")
                if tb > 0:
                    emit("sync", "dma_start", win[1][0:tb, :],
                         rs_pad[c0 + ta:c0 + Wn, :], inc="sd")
                dva = dvb = val("sd")
                submark(0 + 1000 * ci)
                tiles = [(0, ta, dva)] + ([(1, tb, dvb)] if tb > 0 else [])
                ve_sq_prev = 0
                for wi, rows, dv in tiles:
                    o = 5 * wi
                    wt = win[wi]
                    emit("scalar", "square", sq3[0:rows, :], wt[0:rows, :],
                         wait=[("sd", dv), ("ve", ve_sq_prev),
                               ("so", prev["outs"])], inc="ac")
                    av = val("ac")
                    emit("vector", "tensor_reduce", scol[0:rows, o:o + 1],
                         wt[0:rows, :], AX, OP.add, wait=[("sd", dv)])
                    emit("vector", "tensor_reduce", scol[0:rows, o + 1:o + 2],
                         sq3[0:rows, :], AX, OP.add, wait=[("ac", av)], inc="ve")
                    ve_sq_prev = val("ve")
                    emit("vector", "tensor_scalar", scol[0:rows, o:o + 1],
                         scol[0:rows, o:o + 1], 1.0 / D, None, OP.mult)
                    emit("vector", "tensor_scalar", scol[0:rows, o + 1:o + 2],
                         scol[0:rows, o + 1:o + 2], 1.0 / D, None, OP.mult)
                    emit("vector", "tensor_mul", scol[0:rows, o + 2:o + 3],
                         scol[0:rows, o:o + 1], scol[0:rows, o:o + 1])
                    emit("vector", "tensor_sub", scol[0:rows, o + 1:o + 2],
                         scol[0:rows, o + 1:o + 2], scol[0:rows, o + 2:o + 3])
                    emit("vector", "tensor_scalar", scol[0:rows, o + 2:o + 3],
                         scol[0:rows, o + 1:o + 2], EPS, None, OP.add, inc="ve")
                    vv = val("ve")
                    emit("scalar", "sqrt", scol[0:rows, o + 3:o + 4],
                         scol[0:rows, o + 2:o + 3], wait=[("ve", vv)], inc="ac")
                    av = val("ac")
                    emit("vector", "reciprocal", stiv[0:rows, wi:wi + 1],
                         scol[0:rows, o + 3:o + 4], wait=[("ac", av)])
                    for _ in range(2):
                        emit("vector", "tensor_mul", scol[0:rows, o + 3:o + 4],
                             stiv[0:rows, wi:wi + 1], stiv[0:rows, wi:wi + 1])
                        emit("vector", "tensor_mul", scol[0:rows, o + 3:o + 4],
                             scol[0:rows, o + 3:o + 4], scol[0:rows, o + 2:o + 3])
                        emit("vector", "tensor_scalar", scol[0:rows, o + 3:o + 4],
                             scol[0:rows, o + 3:o + 4], -0.5, 1.5, OP.mult, OP.add)
                        emit("vector", "tensor_mul", stiv[0:rows, wi:wi + 1],
                             stiv[0:rows, wi:wi + 1], scol[0:rows, o + 3:o + 4])
                    emit("vector", "scalar_tensor_tensor", stb2[0:rows, wi:wi + 1],
                         scol[0:rows, o:o + 1], -1.0, stiv[0:rows, wi:wi + 1],
                         OP.mult, OP.mult, inc="ve")

                submark(5 + 1000 * ci)
                ve_stats3 = val("ve")
                ve_nprev = 0
                for wi, rows, dv in tiles:
                    o = 5 * wi
                    wt = win[wi]
                    emit("scalar", "activation", n3[0:rows, :], wt[0:rows, :],
                         AF.Identity, bias=stb2[0:rows, wi:wi + 1],
                         scale=stiv[0:rows, wi:wi + 1],
                         wait=[("ve", ve_stats3), ("ve", ve_nprev)], inc="ac")
                    av = val("ac")
                    emit("vector", "tensor_reduce", scol[0:rows, 10 + wi:11 + wi],
                         n3[0:rows, :], AX, OP.max, apply_absolute_value=True,
                         wait=[("ac", av)])
                    emit("vector", "tensor_scalar", scol[0:rows, 10 + wi:11 + wi],
                         scol[0:rows, 10 + wi:11 + wi], 1e-5, None, OP.max)
                    emit("vector", "reciprocal", sts1[0:rows, wi:wi + 1],
                         scol[0:rows, 10 + wi:11 + wi])
                    emit("vector", "tensor_scalar", sts1[0:rows, wi:wi + 1],
                         sts1[0:rows, wi:wi + 1], 127.0, None, OP.mult)
                    submark(6 + 1000 * ci)
                    emit("vector", "scalar_tensor_tensor", scol[0:rows, 14 + wi:15 + wi],
                         scol[0:rows, 10 + wi:11 + wi], 1.0 / 127.0,
                         swr_bc[0:rows, 0:1], OP.mult, OP.mult)
                    submark(7 + 1000 * ci)
                    emit("vector", "tensor_scalar", t3[0:rows, :], n3[0:rows, :],
                         sts1[0:rows, wi:wi + 1], MAGIC, OP.mult, OP.add)
                    submark(8 + 1000 * ci)
                    emit("vector", "tensor_scalar", xqt[0:rows, :], t3[0:rows, :],
                         MAGIC, None, OP.subtract,
                         wait=[("pe", prev["xq_pe"])], inc="ve")
                    submark(9 + 1000 * ci)
                    ve_nprev = val("ve")
                    submark(10 + 1000 * ci)
                    for c in range(8):
                        pv8 = pp4 if c % 2 == 0 else pp5
                        w = [("ve", ve_nprev)]
                        key = "xqcp_%d" % (c % 2)
                        if key in cnt:
                            w.append(("ve", cnt[key]))
                        if ci > 0 and wi == 0 and c < 2:
                            w.append(("pe", prev["mm2_pe"]))
                        emit("tensor", "matmul", pv8[:, 0:rows],
                             xqt[0:rows, c * 128:(c + 1) * 128], idb[0:rows, 0:rows],
                             start=True, stop=True, wait=w, inc="pe")
                        pv_ = val("pe")
                        emit("vector", "tensor_copy",
                             xqf[:, c, wi * 128:wi * 128 + rows], pv8[:, 0:rows],
                             wait=[("pe", pv_)], inc="ve")
                        cnt[key] = val("ve")
                    prev["xq_pe"] = val("pe")
                submark(20 + 1000 * ci)
                ve_xq = val("ve")
                ac_n3 = val("ac")
                dxa = emit("sync", "dma_start", x1o[0][0:na, :],
                           rs_pad[c0 + 1:c0 + 1 + na, :],
                           wait=[("ac", ac_n3), ("ve", ve_xq)], inc="sd")
                if nb > 0:
                    emit("sync", "dma_start", x1o[1][0:nb, :],
                         rs_pad[c0 + 1 + na:c0 + 1 + n, :], inc="sd")
                dxa = dxb = val("sd")

                emit("tensor", "matmul", pp6[0:1, 0:128], scol[:, 14:15], idf[:],
                     start=True, stop=True, wait=[("ve", ve_xq)])
                emit("tensor", "matmul", pp6[0:1, 128:256], scol[:, 15:16], idf[:],
                     start=True, stop=True, inc="pe")
                pv_ = val("pe")
                emit("vector", "tensor_copy", r1row[0:1, 0:256], pp6[0:1, 0:256],
                     wait=[("pe", pv_)], inc="ve")
                vv = val("ve")
                emit("tensor", "matmul", pp_b[:, 0:Wn], onesr[0:1, :],
                     r1row[0:1, 0:Wn], start=True, stop=True, wait=[("ve", vv)],
                     inc="pe")
                pv_ = val("pe")
                emit("vector", "tensor_copy", bc1[:, 0:Wn], pp_b[:, 0:Wn],
                     wait=[("pe", pv_)], inc="ve")
                ve_bc1 = val("ve")

                submark(30 + 1000 * ci)
                e1 = {}
                s3 = {}
                a3 = {}
                for cc in range(32):
                    ppm = pp4 if cc % 2 == 0 else pp5
                    w = [("ve", ve_bc1)]
                    if cc < 2:
                        w.append(("ve", cnt.get("xqcp_%d" % (cc % 2), 0)))
                    if cc >= 2:
                        w.append(("ve", e1[cc - 2]))
                    for ch in range(8):
                        emit("tensor", "matmul", ppm[:, 0:Wn],
                             w1q[:, ch, cc * 128:(cc + 1) * 128], xqf[:, ch, 0:Wn],
                             start=(ch == 0), stop=(ch == 7),
                             wait=(w if ch == 0 else ()),
                             inc=("pe" if ch == 7 else None))
                    pv_ = val("pe")
                    emit("vector", "tensor_mul", h1s[:, cc % 2, 0:Wn], ppm[:, 0:Wn],
                         bc1[:, 0:Wn], wait=[("pe", pv_)], inc="ve")
                    e1[cc] = val("ve")
                    submark(40 + 1000 * ci)
                    ppc = pp6 if cc % 2 == 0 else pp7
                    w = [("ve", e1[cc])]
                    if cc >= 2:
                        w.append(("ac", s3[cc - 2]))
                    for tap in range(3):
                        emit("tensor", "matmul", ppc[0:64, 0:n],
                             diag[0:64, cc, tap, :], h1s[0:64, cc % 2, tap:tap + n],
                             start=(tap == 0), stop=False,
                             tile_position=(0, 0), wait=(w if tap == 0 else ()))
                    emit("tensor", "matmul", ppc[0:64, 0:n],
                         diag[0:64, cc, 3, :], onesb[0:64, 0:n],
                         start=False, stop=True, tile_position=(0, 0))
                    for tap in range(3):
                        emit("tensor", "ldweights", diag[64:128, cc, tap, :],
                             tile_position=(64, 64))
                        emit("tensor", "matmul", ppc[64:128, 0:n],
                             diag[64:128, cc, tap, :], h1s[64:128, cc % 2, tap:tap + n],
                             start=(tap == 0), stop=False,
                             tile_position=(64, 64))
                    emit("tensor", "ldweights", diag[64:128, cc, 3, :],
                         tile_position=(64, 64))
                    emit("tensor", "matmul", ppc[64:128, 0:n],
                         diag[64:128, cc, 3, :], onesb[64:128, 0:n],
                         start=False, stop=True,
                         tile_position=(64, 64), inc="pe")
                    cvv = val("pe")
                    emit("scalar", "activation", h3[:, cc, 0:n], ppc[:, 0:n], AF.Silu,
                         bias=0.0, scale=1.0, wait=[("pe", cvv)])
                    emit("scalar", "activation", habs[:, 0:n], h3[:, cc, 0:n], AF.Abs,
                         wait=([("ve", a3[cc - 1])] if cc >= 1 else ()), inc="ac")
                    s3[cc] = val("ac")
                    if cc == 0:
                        emit("vector", "tensor_copy", hacc[:, 0:n], habs[:, 0:n],
                             wait=[("ac", s3[cc])], inc="ve")
                    else:
                        emit("vector", "tensor_max", hacc[:, 0:n], hacc[:, 0:n],
                             habs[:, 0:n], wait=[("ac", s3[cc])], inc="ve")
                    a3[cc] = val("ve")

                submark(50 + 1000 * ci)
                emit("tensor", "matmul", pp_a[0:na, 0:128], hacc[:, 0:na], idf[:],
                     start=True, stop=True,
                     wait=[("ve", a3[31]), ("pe", prev["mm2_pe"])],
                     inc=(None if nb > 0 else "pe"))
                if nb > 0:
                    emit("tensor", "matmul", pp_a[0:nb, 512:640], hacc[:, na:n],
                         idf[:], start=True, stop=True, inc="pe")
                pv_ = val("pe")
                emit("vector", "tensor_reduce", c2c[0:na, 0:1], pp_a[0:na, 0:128],
                     AX, OP.max, wait=[("pe", pv_)])
                if nb > 0:
                    emit("vector", "tensor_reduce", c2c[0:nb, 1:2],
                         pp_a[0:nb, 512:640], AX, OP.max)
                emit("vector", "tensor_scalar", c2c[:, 0:2], c2c[:, 0:2], 1e-5, None,
                     OP.max)
                emit("vector", "reciprocal", c2c[:, 2:4], c2c[:, 0:2])
                emit("vector", "tensor_scalar", c2c[:, 2:4], c2c[:, 2:4], 127.0, None,
                     OP.mult)
                emit("vector", "scalar_tensor_tensor", e2c[:, 0:1], c2c[:, 0:1],
                     1.0 / 127.0, swr_bc[:, 1:2], OP.mult, OP.mult)
                emit("vector", "scalar_tensor_tensor", e2c[:, 1:2], c2c[:, 1:2],
                     1.0 / 127.0, swr_bc[:, 1:2], OP.mult, OP.mult, inc="ve")
                vv = val("ve")
                emit("tensor", "matmul", pp6[0:1, 0:128], c2c[:, 2:3], idf[:],
                     start=True, stop=True, wait=[("ve", vv)],
                     inc=(None if nb > 0 else "pe"))
                if nb > 0:
                    emit("tensor", "matmul", pp6[0:1, 128:256], c2c[:, 3:4], idf[:],
                         start=True, stop=True, inc="pe")
                pv_ = val("pe")
                emit("vector", "tensor_copy", s2row[0:1, 0:256], pp6[0:1, 0:256],
                     wait=[("pe", pv_)], inc="ve")
                vv = val("ve")
                emit("tensor", "matmul", pp_b[:, 256:256 + n], onesr[0:1, :],
                     s2row[0:1, 0:n], start=True, stop=True, wait=[("ve", vv)],
                     inc="pe")
                pv_ = val("pe")
                emit("vector", "tensor_copy", bc2[:, 0:n], pp_b[:, 256:256 + n],
                     wait=[("pe", pv_)], inc="ve")
                ve_bc2 = val("ve")

                submark(60 + 1000 * ci)
                qv = {}
                for cc in range(32):
                    w = [("ve", ve_bc2)]
                    if cc >= 2:
                        w.append(("pe", qv[cc - 2]))
                    emit("vector", "tensor_mul", habs[:, 0:n], h3[:, cc, 0:n],
                         bc2[:, 0:n], wait=w, inc="ve")
                    vv0 = val("ve")
                    emit("scalar", "activation", hacc[:, 0:n], habs[:, 0:n],
                         AF.Identity, bias=magicc[:], scale=1.0, wait=[("ve", vv0)],
                         inc="ac")
                    av = val("ac")
                    emit("vector", "tensor_scalar", h3q[:, cc % 2, 0:n], hacc[:, 0:n],
                         MAGIC, None, OP.subtract, wait=[("ac", av)], inc="ve")
                    vvq = val("ve")
                    mtl = [(0, na)] + ([(na, nb)] if nb else [])
                    for mi, (ms, mw) in enumerate(mtl):
                        ppo = pp_a if mi == 0 else pp_b
                        for half in range(2):
                            emit("tensor", "matmul",
                                 ppo[0:mw, half * 512:(half + 1) * 512],
                                 h3q[:, cc % 2, ms:ms + mw],
                                 w2q[:, cc, half * 512:(half + 1) * 512],
                                 start=(cc == 0), stop=(cc == 31),
                                 wait=([("ve", vvq)] if (mi == 0 and half == 0) else ()),
                                 inc=("pe" if (mi == len(mtl) - 1 and half == 1) else None))
                    qv[cc] = val("pe")
                prev["mm2_pe"] = qv[31]

                submark(70 + 1000 * ci)
                for mi, (ms, mw) in enumerate([(0, na)] + ([(na, nb)] if nb else [])):
                    ppo = pp_a if mi == 0 else pp_b
                    ecol = e2c[0:mw, mi:mi + 1]
                    xo = x1o[mi]
                    dv = dxa if mi == 0 else dxb
                    emit("vector", "scalar_tensor_tensor", x2b[mi][0:mw, 0:512],
                         ppo[0:mw, 0:512], ecol, xo[0:mw, 0:512], OP.mult, OP.add,
                         wait=[("pe", prev["mm2_pe"]), ("sd", dv)])
                    emit("vector", "scalar_tensor_tensor", x2b[mi][0:mw, 512:1024],
                         ppo[0:mw, 512:1024], ecol, xo[0:mw, 512:1024],
                         OP.mult, OP.add, inc="ve")
                    vv = val("ve")
                    emit("sync", "dma_start", out[c0 + ms:c0 + ms + mw, :],
                         x2b[mi][0:mw, :], wait=[("ve", vv)], inc="so")
                prev["epi_ve"] = val("ve")
                prev["outs"] = val("so")

            phase_mark(98)
            if dbg and not mute["on"]:
                emit("sync", "dma_start", dbg_cin, cin, inc="so")
                emit("sync", "dma_start", dbg_rs, rs_pad, inc="so")
            finalize_debug(rs_pad[1:129, :])
            waitonly("sync", ("so", val("so")))
            run_block()
    return nc


def kernel(L0_attn_x, L0_attn_ln_w, L0_attn_ln_b, L0_attn_wq, L0_attn_wk,
           L0_attn_wv, L0_attn_wo, L0_ffn_w1, L0_ffn_conv_w, L0_ffn_conv_b,
           L0_ffn_w2):
    x = np.ascontiguousarray(np.asarray(L0_attn_x, dtype=np.float32))
    lnwv = np.asarray(L0_attn_ln_w, np.float32)
    lnbv = np.asarray(L0_attn_ln_b, np.float32)
    wqa = np.asarray(L0_attn_wq, np.float32)
    wka = np.asarray(L0_attn_wk, np.float32)
    wva = np.asarray(L0_attn_wv, np.float32)
    woa = np.asarray(L0_attn_wo, np.float32)
    w1 = np.asarray(L0_ffn_w1, np.float32)
    cw = np.asarray(L0_ffn_conv_w, np.float32)
    cb = np.asarray(L0_ffn_conv_b, np.float32)
    w2 = np.asarray(L0_ffn_w2, np.float32)

    lnw_bc = np.ascontiguousarray(np.broadcast_to(lnwv, (128, D)))
    lnb_bc = np.ascontiguousarray(np.broadcast_to(lnbv, (128, D)))
    w1T = np.ascontiguousarray(w1.T)
    w2T = np.ascontiguousarray(w2.T)
    cwr = cw[:, 0, :].reshape(32, 128, 3).transpose(1, 0, 2)
    cbr = cb.reshape(32, 128).T[:, :, None]
    cw4 = np.concatenate([cwr, cbr], axis=2)
    diag_host = np.zeros((128, 32, 4, 64), np.float32)
    for p in range(128):
        diag_host[p, :, :, p % 64] = cw4[p]

    import os
    _upto = int(os.environ.get("KERNEL_UPTO", "99"))
    _ksub = int(os.environ.get("KERNEL_KSUB", "9999"))
    _dbg = os.environ.get("KERNEL_DBG", "") == "1"
    if _NC_CACHE[0] is None:
        _NC_CACHE[0] = build_program(_upto, _ksub, _dbg)
    nc = _NC_CACHE[0]
    kernel._DBG_RES = [None]

    def pack(w8):
        wp = w8.reshape(4, 2, D, E).transpose(0, 2, 1, 3).reshape(4, D, 128)
        return np.ascontiguousarray(wp.reshape(4, 8, 128, 128).transpose(2, 0, 1, 3))

    in_maps = []
    for c in range(8):
        b, r = c // 2, c % 2
        hsl = slice(8 * r, 8 * r + 8)
        wv8 = wva[hsl].transpose(1, 0, 2).reshape(D, 512)
        wv_sb = np.ascontiguousarray(wv8.reshape(8, 128, 512).transpose(1, 0, 2))
        wo8 = woa[512 * r:512 * (r + 1), :]
        wo_sb = np.ascontiguousarray(wo8.reshape(4, 128, D).transpose(1, 0, 2))
        if r == 0:
            wsel = np.array([[0, 0], [0, 1], [0, 0], [0, 1]], np.float32)
        else:
            wsel = np.array([[1, 0], [0, 0], [1, 0], [0, 0]], np.float32)
        in_maps.append({
            "x": np.ascontiguousarray(x[b]),
            "wqp": pack(wqa[hsl]), "wkp": pack(wka[hsl]),
            "wv8": wv_sb, "wo8": wo_sb,
            "lnw_bc": lnw_bc, "lnb_bc": lnb_bc,
            "w1T": w1T, "w2T": w2T,
            "diag_in": diag_host,
            "wsel": wsel,
        })
    res = run_bass_kernel_spmd(nc, in_maps, core_ids=list(range(8)))
    import kernel as _k
    _k._DBG_RES = res
    y = np.empty((4, S, D), np.float32)
    for c in range(8):
        b, r = c // 2, c % 2
        y[b, r * T:(r + 1) * T, :] = res.results[c]["out"]
    return y


# version 08
# speedup vs baseline: 1.0547x; 1.0053x over previous
import os
import numpy as np
import concourse.bass as bass
import concourse.mybir as mybir
from concourse.alu_op_type import AluOpType as OP
from concourse.bass_utils import run_bass_kernel_spmd

F32 = mybir.dt.float32
BF16 = mybir.dt.bfloat16
AF = mybir.ActivationFunctionType
AX = mybir.AxisListType.X

S, D, E, HP, DH = 4096, 1024, 64, 4, 4096
T = S // 2
EPS = 1e-5
MAGIC = float(np.float32(1.5 * 2 ** 23))
NT = S // 128
FFN_CHUNKS = [(i * 254, 254) for i in range(8)] + [(2032, 16)]
ENGS = ("tensor", "vector", "scalar", "sync", "gpsimd")
GROUPS = [[0, 1], [2, 3], [4, 5], [6, 7]]

_NC_CACHE = [None]


def build_program(upto=99, ksub=10**9, dbg=False):
    nc = bass.Bass("TRN2", target_bir_lowering=False)
    x_in = nc.dram_tensor("x", [S, D], F32, kind="ExternalInput").ap()
    wqp_in = nc.dram_tensor("wqp", [128, HP, 8, 128], F32, kind="ExternalInput").ap()
    wkp_in = nc.dram_tensor("wkp", [128, HP, 8, 128], F32, kind="ExternalInput").ap()
    wv_in = nc.dram_tensor("wv8", [128, 8, 512], F32, kind="ExternalInput").ap()
    wo_in = nc.dram_tensor("wo8", [128, HP, 1024], F32, kind="ExternalInput").ap()
    lnw_in = nc.dram_tensor("lnw_bc", [128, D], F32, kind="ExternalInput").ap()
    lnb_in = nc.dram_tensor("lnb_bc", [128, D], F32, kind="ExternalInput").ap()
    w1T_in = nc.dram_tensor("w1T", [D, DH], F32, kind="ExternalInput").ap()
    w2T_in = nc.dram_tensor("w2T", [DH, D], F32, kind="ExternalInput").ap()
    diag_in = nc.dram_tensor("diag_in", [128, 32, 4, 64], F32, kind="ExternalInput").ap()
    wsel_in = nc.dram_tensor("wsel", [4, 2], F32, kind="ExternalInput").ap()
    out = nc.dram_tensor("out", [T, D], F32, kind="ExternalOutput").ap()
    cin = nc.dram_tensor("cin", [S, D], F32, kind="Internal").ap()
    rs_pad = nc.dram_tensor("rs_pad", [T + 2, D], F32, kind="Internal", addr_space="Local").ap()
    ag_in = nc.dram_tensor("ag_in", [2, D], F32, kind="Internal").ap()
    ag_out = nc.dram_tensor("ag_out", [4, D], F32, kind="Internal", addr_space="Local").ap()
    w1q_h = nc.dram_tensor("w1q_h", [D, DH], BF16, kind="Internal").ap()
    w2q_h = nc.dram_tensor("w2q_h", [DH, D], BF16, kind="Internal").ap()
    if dbg:
        dbg_y = nc.dram_tensor("dbg_y", [128, 8, 512], BF16, kind="ExternalOutput").ap()
        dbg_y2 = nc.dram_tensor("dbg_y2", [128, 8, 512], BF16, kind="ExternalOutput").ap()
        dbg_kq = nc.dram_tensor("dbg_kq", [128, 2, S], BF16, kind="ExternalOutput").ap()
        dbg_at = nc.dram_tensor("dbg_at", [128, HP, S], BF16, kind="ExternalOutput").ap()
        dbg_cin = nc.dram_tensor("dbg_cin", [S, D], F32, kind="ExternalOutput").ap()
        dbg_rs = nc.dram_tensor("dbg_rs", [T + 2, D], F32, kind="ExternalOutput").ap()

    cnt = {}

    def bump(sem, k=1):
        cnt[sem] = cnt.get(sem, 0) + k
        return cnt[sem]

    def val(sem):
        return cnt.get(sem, 0)

    from contextlib import ExitStack
    with ExitStack() as top:
        et = top.enter_context
        sems = {n: et(nc.semaphore(n)) for n in
                ("sd", "sg", "so", "pe", "ve", "ac", "gp", "co",
                 "sdx0", "sdx1", "swq0", "swq1", "sst0", "sst1",
                 "sci0", "sci1", "sgk0", "sgk1")}

        ops = {e: [] for e in ENGS}
        mute = {"on": False}

        def phase_mark(n):
            if n > upto:
                mute["on"] = True

        def submark(k):
            if k > ksub:
                mute["on"] = True

        BLANKET = {"vector": "ve", "scalar": "ac", "gpsimd": "gp"}

        def emit(eng, meth, *args, wait=(), inc=None, inck=1, **kw):
            if mute["on"]:
                return cnt.get(inc, 0)
            if meth == "dma_start":
                inck = 16
            bsem = BLANKET.get(eng)
            if bsem is not None and meth not in ("dma_start", "collective_compute"):
                wait = tuple(wait) + ((bsem, val(bsem)),)
                if inc is None:
                    inc = bsem
            v = bump(inc, inck) if inc else None
            wmax = {}
            for s, w in wait:
                if w > wmax.get(s, 0):
                    wmax[s] = w

            def fn(E, meth=meth, args=args, kw=kw, wait=tuple(wmax.items()),
                   inc=inc, inck=inck):
                for s, w in wait:
                    E.wait_ge(sems[s], w)
                ins = getattr(E, meth)(*args, **kw)
                if inc:
                    ins.then_inc(sems[inc], inck)
            ops[eng].append(fn)
            return v

        def waitonly(eng, *pairs):
            if mute["on"]:
                return
            def fn(E, pairs=tuple(pairs)):
                for s, w in pairs:
                    if w > 0:
                        E.wait_ge(sems[s], w)
            ops[eng].append(fn)

        def finalize_debug(buf):
            if not mute["on"] or mute.get("done"):
                return
            mute["on"] = False
            for sname in ("sd", "sg", "so", "pe", "ve", "ac", "gp", "co", "sdx0", "sdx1", "swq0", "swq1", "sst0", "sst1", "sci0", "sci1", "sgk0", "sgk1"):
                if val(sname) > 0:
                    waitonly("sync", (sname, val(sname)))
            emit("sync", "dma_start", out[0:128, :], buf, inc="so")
            waitonly("sync", ("so", val("so")))
            mute["on"] = True
            mute["done"] = True

        def run_block():
            with nc.Block() as block:
                @block.tensor
                def _(t):
                    for fn in ops["tensor"]:
                        fn(t)

                @block.vector
                def _(v):
                    for fn in ops["vector"]:
                        fn(v)

                @block.scalar
                def _(sc):
                    for fn in ops["scalar"]:
                        fn(sc)

                @block.sync
                def _(sy):
                    for fn in ops["sync"]:
                        fn(sy)

                @block.gpsimd
                def _(g):
                    for fn in ops["gpsimd"]:
                        fn(g)
            for e in ENGS:
                ops[e] = []

        stat_iv = et(nc.sbuf_tensor("stat_iv", [128, NT], F32))
        stat_b1 = et(nc.sbuf_tensor("stat_b1", [128, NT], F32))
        stat_m = et(nc.sbuf_tensor("stat_m", [128, NT], F32))
        stat_R = et(nc.sbuf_tensor("stat_R", [128, NT], F32))
        stat_t1 = et(nc.sbuf_tensor("stat_t1", [128, NT], F32))
        stat_t2 = et(nc.sbuf_tensor("stat_t2", [128, NT], F32))
        stat_t3 = et(nc.sbuf_tensor("stat_t3", [128, NT], F32))
        mbm = et(nc.sbuf_tensor("mbm", [128, 896], BF16))
        idb = et(nc.sbuf_tensor("idb", [128, 128], BF16))
        idf = et(nc.sbuf_tensor("idf", [128, 128], F32))
        esh = et(nc.sbuf_tensor("esh", [128, 128], BF16))
        ones2 = et(nc.sbuf_tensor("ones2", [128, 2], BF16))
        onesr = et(nc.sbuf_tensor("onesr", [1, 128], F32))
        onesr2 = et(nc.sbuf_tensor("onesr2", [65, 128], F32))
        onesc = et(nc.sbuf_tensor("onesc", [128, 1], F32))
        onesb = et(nc.sbuf_tensor("onesb", [128, 256], BF16))
        wselb = et(nc.sbuf_tensor("wselb", [4, 2], F32))
        rowb = et(nc.sbuf_tensor("rowb", [1, 16], F32))
        sw_bc = et(nc.sbuf_tensor("sw_bc", [128, 2], F32))
        swr_bc = et(nc.sbuf_tensor("swr_bc", [128, 2], F32))
        negc = et(nc.sbuf_tensor("negc", [128, 2], F32))
        magicc = et(nc.sbuf_tensor("magicc", [128, 1], F32))
        stgm = et(nc.sbuf_tensor("stgm", [128, 2], F32))
        stgr = et(nc.sbuf_tensor("stgr", [128, 2], F32))
        stgiv = et(nc.sbuf_tensor("stgiv", [128, 2], F32))
        stgb1 = et(nc.sbuf_tensor("stgb1", [128, 2], F32))
        wsums = et(nc.sbuf_tensor("wsums", [128, 49], F32))
        wcol = et(nc.sbuf_tensor("wcol", [128, 2], F32))

        pp_a = et(nc.psum_tensor("pp_a", [128, 1024], F32))
        pp_b = et(nc.psum_tensor("pp_b", [128, 1024], F32))
        pp4 = et(nc.psum_tensor("pp4", [128, 512], F32))
        pp5 = et(nc.psum_tensor("pp5", [128, 512], F32))
        pp6 = et(nc.psum_tensor("pp6", [128, 512], F32))
        pp7 = et(nc.psum_tensor("pp7", [128, 512], F32))

        with ExitStack() as actx:
            ea = actx.enter_context
            y_fm = ea(nc.sbuf_tensor("y_fm", [128, 8, S], BF16))
            v_sb = ea(nc.sbuf_tensor("v_sb", [128, 32, 520], BF16))
            k_hp = ea(nc.sbuf_tensor("k_hp", [128, S], BF16))
            q_hp = ea(nc.sbuf_tensor("q_hp", [128, S], BF16))
            attn = ea(nc.sbuf_tensor("attn", [128, HP, S], BF16))
            wqkp = ea(nc.sbuf_tensor("wqkp_s", [128, 2, 2, 8, 128], BF16))
            x0t = [ea(nc.sbuf_tensor("x0t%d" % i, [128, D], F32)) for i in range(2)]
            s1c = ea(nc.sbuf_tensor("s1c", [128, NT], F32))
            s2c = ea(nc.sbuf_tensor("s2c", [128, NT], F32))
            ptile = ea(nc.sbuf_tensor("ptile", [128, 2, 2, 512], BF16))
            accs = ea(nc.sbuf_tensor("accs", [65, 2, 512], F32))
            kq2 = ea(nc.sbuf_tensor("kq2", [128, 2], F32))
            scr = ea(nc.sbuf_tensor("scr", [128, 8192], F32))
            scr_ap = scr.ap()
            scrb = scr_ap.bitcast(BF16)
            xsq = [scr_ap[:, 0:1024], scr_ap[:, 1024:2048]]
            ytmp = scr_ap[:, 0:1024]
            ytk = [scrb[:, 4096:5120], scrb[:, 5120:6144]]
            lnw = scr_ap[:, 6144:7168]
            lnb = scr_ap[:, 7168:8192]
            wv8 = scrb[:, 8192:12288].rearrange("p (c n) -> p c n", c=8)
            sqs_nb = [scrb[:, 12288:12800], scrb[:, 12800:13312]]
            wbuf = [scr_ap[:, 0:2048], scr_ap[:, 2048:4096]]
            wqo = [scrb[:, 8192:10240], scrb[:, 10240:12288]]
            rlA64 = scr_ap[64:65, 6656:7168]
            rlB64 = scr_ap[64:65, 7680:8192]
            atmp = scrb[0:64, 14336:14848]
            yv = y_fm.ap().rearrange("p a b -> p (a b)")
            yvf = yv.bitcast(F32)
            wo8 = yv[:, 0:4096].rearrange("p (a b) -> p a b", a=HP)
            u25b = [yvf[:, 2048:3072], yvf[:, 6144:7168]]
            n25b = [yvf[:, 3072:4096], yvf[:, 7168:8192]]
            x1p = [yvf[:, 4096:5120], yvf[:, 5120:6144]]
            x1halo = scr_ap[0:2, 2048:3072]
            agsb = scr_ap[0:4, 3072:4096]

            emit("sync", "dma_start", lnw[:], lnw_in, inc="sd")
            emit("sync", "dma_start", lnb[:], lnb_in, inc="sd")
            
            emit("sync", "dma_start", wselb[:], wsel_in, inc="sd")
            d_w0 = val("sd")
            kq_load = {}
            for h0 in range(2):
                sk = "sgk%d" % h0
                emit("gpsimd", "dma_start", wqkp[:, h0, 0], wkp_in[:, h0], inc=sk)
                emit("gpsimd", "dma_start", wqkp[:, h0, 1], wqp_in[:, h0], inc=sk)
                kq_load[h0] = val(sk)
            emit("gpsimd", "dma_start", wv8[:, :, :], wv_in, inc="sg")
            d_wg = val("sg")
            emit("gpsimd", "memset", mbm[:], 1.0, inc="gp")
            emit("gpsimd", "affine_select", out=mbm[:], in_=mbm[:], compare_op=OP.is_ge,
                 fill=0.0, base=-384, pattern=[[1, 896]], channel_multiplier=-1,
                 wait=[("gp", val("gp"))])
            emit("gpsimd", "memset", idb[:], 0.0, inc="gp")
            emit("gpsimd", "affine_select", out=idb[:], in_=idb[:], compare_op=OP.not_equal,
                 fill=1.0, base=0, pattern=[[-1, 128]], channel_multiplier=1,
                 wait=[("gp", val("gp"))])
            emit("gpsimd", "memset", idf[:], 0.0, inc="gp")
            emit("gpsimd", "affine_select", out=idf[:], in_=idf[:], compare_op=OP.not_equal,
                 fill=1.0, base=0, pattern=[[-1, 128]], channel_multiplier=1,
                 wait=[("gp", val("gp"))])
            emit("gpsimd", "memset", esh[:], 0.0, inc="gp")
            emit("gpsimd", "affine_select", out=esh[:], in_=esh[:], compare_op=OP.not_equal,
                 fill=1.0, base=64, pattern=[[-1, 128]], channel_multiplier=1,
                 wait=[("gp", val("gp"))])
            emit("gpsimd", "memset", ones2[:], 0.0, inc="gp")
            emit("gpsimd", "memset", ones2[0:64, 0:1], 1.0,
                 wait=[("gp", val("gp"))], inc="gp")
            emit("gpsimd", "memset", ones2[64:128, 1:2], 1.0,
                 wait=[("gp", val("gp"))])
            emit("gpsimd", "memset", onesr[:], 1.0)
            emit("gpsimd", "memset", onesr2[:], 1.0)
            emit("gpsimd", "memset", magicc[:], MAGIC)
            emit("gpsimd", "memset", onesb[:], 1.0)
            emit("gpsimd", "memset", onesc[:], 1.0)
            emit("gpsimd", "memset", v_sb[:], 1.0, inc="gp")
            emit("gpsimd", "memset", kq2[:], 0.0, inc="gp")
            emit("gpsimd", "memset", wsums[:], 0.0, inc="gp")
            gp_consts = val("gp")

            ve_p1a = {}
            for j in range(NT):
                bi = j % 2
                sx = "sdx%d" % bi
                w = [("ve", ve_p1a.get(j - 2, 0))]
                dv = emit("sync", "dma_start", x0t[bi][:], x_in[j * 128:(j + 1) * 128, :],
                          wait=w, inc=sx)
                emit("scalar", "square", xsq[bi][:], x0t[bi][:],
                     wait=[(sx, dv), ("ve", ve_p1a.get(j - 2, 0))], inc="ac")
                a1 = val("ac")
                emit("vector", "tensor_reduce", s1c[:, j:j + 1], x0t[bi][:], AX, OP.add,
                     wait=[(sx, dv)])
                emit("vector", "tensor_reduce", s2c[:, j:j + 1], xsq[bi][:], AX, OP.add,
                     wait=[("ac", a1)], inc="ve")
                ve_p1a[j] = val("ve")

            def newton_rsqrt(dst, src, tmp_a, tmp_b):
                emit("scalar", "sqrt", tmp_a[:], src[:],
                     wait=[("ve", val("ve"))], inc="ac")
                a0 = val("ac")
                emit("vector", "reciprocal", dst[:], tmp_a[:], wait=[("ac", a0)])
                for _ in range(2):
                    emit("vector", "tensor_mul", tmp_b[:], dst[:], dst[:])
                    emit("vector", "tensor_mul", tmp_b[:], tmp_b[:], src[:])
                    emit("vector", "tensor_scalar", tmp_b[:], tmp_b[:], -0.5, 1.5,
                         OP.mult, OP.add)
                    emit("vector", "tensor_mul", dst[:], dst[:], tmp_b[:])

            waitonly("vector", ("ve", ve_p1a[NT - 1]))
            emit("vector", "tensor_scalar", stat_m[:], s1c[:], 1.0 / D, None, OP.mult)
            emit("vector", "tensor_scalar", stat_t1[:], s2c[:], 1.0 / D, None, OP.mult)
            emit("vector", "tensor_mul", stat_t2[:], stat_m[:], stat_m[:])
            emit("vector", "tensor_sub", stat_t1[:], stat_t1[:], stat_t2[:])
            emit("vector", "tensor_scalar", stat_t2[:], stat_t1[:], EPS, None, OP.add,
                 inc="ve")
            newton_rsqrt(stat_iv, stat_t2, stat_t3, s1c)
            emit("vector", "scalar_tensor_tensor", stat_b1[:], stat_m[:], -1.0,
                 stat_iv[:], OP.mult, OP.mult)
            emit("vector", "scalar_tensor_tensor", stat_t2[:], stat_t2[:], EPS,
                 stat_t1[:], OP.mult, OP.add, inc="ve")
            newton_rsqrt(stat_R, stat_t2, stat_t3, s1c)
            emit("vector", "memset", rowb[:], 0.0, inc="ve")
            ve_stats = val("ve")

            phase_mark(2)
            waitonly("sync", ("ve", ve_stats), ("sg", d_wg), ("gp", gp_consts),
                     ("sd", d_w0))
            pe_tp = {}
            ve_cp = {}
            for j in range(NT):
                bi = j % 2
                ve_prev_y = cnt.get("ve_y_%d" % (j - 2), 0)
                sx = "sdx%d" % bi
                dv = emit("sync", "dma_start", x0t[bi][:], x_in[j * 128:(j + 1) * 128, :],
                          wait=[("ve", ve_prev_y)], inc=sx)
                wv_ = [(sx, dv), ("ve", ve_stats)]
                if (j - 2, 7) in pe_tp:
                    wv_.append(("pe", pe_tp[(j - 2, 7)]))
                emit("vector", "tensor_copy", stgm[:, bi:bi + 1], stat_m[:, j:j + 1],
                     wait=wv_)
                emit("vector", "tensor_copy", stgr[:, bi:bi + 1], stat_R[:, j:j + 1])
                emit("vector", "scalar_tensor_tensor", ytmp[:], x0t[bi][:],
                     stgm[:, bi:bi + 1], lnw[:], OP.subtract, OP.mult)
                emit("vector", "scalar_tensor_tensor", ytk[bi][:], ytmp[:],
                     stgr[:, bi:bi + 1], lnb[:], OP.mult, OP.add, inc="ve")
                cnt["ve_y_%d" % j] = val("ve")
                for c in range(8):
                    pv8 = pp4 if c % 2 == 0 else pp5
                    w = [("ve", cnt["ve_y_%d" % j])]
                    key = (j, c - 2) if c >= 2 else ((j - 1, c + 6) if j >= 1 else None)
                    if key is not None and key in ve_cp:
                        w.append(("ve", ve_cp[key]))
                    emit("tensor", "matmul", pv8[:, 0:128],
                         ytk[bi][:, c * 128:(c + 1) * 128], idb[:],
                         start=True, stop=True, wait=w, inc="pe")
                    pe_tp[(j, c)] = val("pe")
                    emit("vector", "tensor_copy", y_fm[:, c, j * 128:(j + 1) * 128],
                         pv8[:, 0:128], wait=[("pe", pe_tp[(j, c)])], inc="ve")
                    ve_cp[(j, c)] = val("ve")

            if dbg and os.environ.get("KERNEL_DBG_EARLY", "") == "2" and not mute["on"]:
                emit("sync", "dma_start", dbg_y, y_fm[:, :, 1536:2048],
                     wait=[("pe", val("pe")), ("ve", val("ve")), ("ac", val("ac"))],
                     inc="so")
                emit("sync", "dma_start", dbg_y2, y_fm[:, :, 1536:2048], inc="so")
                waitonly("sync", ("so", val("so")))
            phase_mark(3)
            ve_all_cp = val("ve")
            waitonly("tensor", ("ve", ve_all_cp), ("sg", d_wg), ("gp", gp_consts))
            ve_vd = {}
            for kt in range(NT):
                ppv = pp6 if kt % 2 == 0 else pp7
                w = [("ve", ve_vd.get(kt - 2, 0))]
                for ch in range(8):
                    emit("tensor", "matmul", ppv[:, 0:512],
                         y_fm[:, ch, kt * 128:(kt + 1) * 128], wv8[:, ch, 0:512],
                         start=(ch == 0), stop=(ch == 7), wait=(w if ch == 0 else ()),
                         inc=("pe" if ch == 7 else None))
                pv_ = val("pe")
                if os.environ.get("KERNEL_NOVEVAC", "") == "1":
                    emit("vector", "tensor_copy", v_sb[:, kt, 0:512], ppv[:, 0:512],
                         wait=[("pe", pv_)], inc="ve")
                else:
                    src = ppv[:, 0:512].rearrange("p (h e) -> p h e", h=8)
                    dst = v_sb[:, kt, 0:520].rearrange("p (h e) -> p h e", e=65)[:, :, 0:64]
                    emit("vector", "tensor_copy", dst, src, wait=[("pe", pv_)], inc="ve")
                ve_vd[kt] = val("ve")

            NOWQ = os.environ.get("KERNEL_NOWQ", "") == "1"
            wq_chunks = ([("w1", (j, h)) for j in range(8) for h in range(2)]
                         + [("w2", j) for j in range(32)])
            wq = {"li": 0, "ci": 0, "ldv": [], "use": [], "st": [], "mode": "red",
                  "gate": [("pe", val("pe")), ("ve", val("ve"))]}

            def wq_step(n):
                if NOWQ:
                    return
                for _ in range(n):
                    while wq["li"] < len(wq_chunks) and wq["li"] < wq["ci"] + 2:
                        i = wq["li"]
                        kind, j = wq_chunks[i]
                        if kind == "w1":
                            jj, hh = j
                            width = 2048
                            src = w1T_in[jj * 128:(jj + 1) * 128,
                                         hh * 2048:(hh + 1) * 2048]
                        else:
                            width = D
                            src = w2T_in[j * 128:(j + 1) * 128, :]
                        w = [("ve", wq["use"][i - 2])] if i >= 2 else list(wq["gate"])
                        if wq["mode"] == "quant" and i >= 2:
                            w.append(("sst%d" % (i % 2), wq["st"][i - 2]))
                        dv = emit("sync", "dma_start", wbuf[i % 2][:, 0:width], src,
                                  wait=w, inc="swq%d" % (i % 2))
                        wq["ldv"].append(dv)
                        wq["li"] += 1
                    i = wq["ci"]
                    if i >= len(wq_chunks):
                        return
                    wq["ci"] += 1
                    kind, j = wq_chunks[i]
                    width = 2048 if kind == "w1" else D
                    buf = wbuf[i % 2]
                    dv = wq["ldv"][i]
                    if wq["mode"] == "red":
                        ci_col = (2 * j[0] + j[1]) if kind == "w1" else 16 + j
                        emit("vector", "tensor_reduce", wsums[:, ci_col:ci_col + 1],
                             buf[:, 0:width], AX, OP.add,
                             apply_absolute_value=True,
                             wait=[("swq%d" % (i % 2), dv)] +
                                  ([("gp", gp_consts)] if i == 0 else []),
                             inc="ve")
                        wq["use"].append(val("ve"))
                    else:
                        swcol = sw_bc[:, 0:1] if kind == "w1" else sw_bc[:, 1:2]
                        emit("scalar", "activation", buf[:, 0:width], buf[:, 0:width],
                             AF.Identity, bias=magicc[:], scale=swcol,
                             wait=[("swq%d" % (i % 2), dv), ("ve", wq["sw_ready"])] +
                                  ([("sst%d" % (i % 2), wq["st"][i - 2])] if i >= 2 else []),
                             inc="ac")
                        av = val("ac")
                        emit("vector", "tensor_scalar", buf[:, 0:width], buf[:, 0:width],
                             MAGIC, 1.0, OP.subtract, OP.min, wait=[("ac", av)])
                        emit("vector", "tensor_scalar", wqo[i % 2][:, 0:width],
                             buf[:, 0:width], -1.0, None, OP.max, inc="ve")
                        wq["use"].append(val("ve"))
                        qvv = val("ve")
                        if kind == "w1":
                            jj, hh = j
                            hdst = w1q_h[jj * 128:(jj + 1) * 128,
                                         hh * 2048:(hh + 1) * 2048]
                        else:
                            hdst = w2q_h[j * 128:(j + 1) * 128, :]
                        emit("sync", "dma_start", hdst, wqo[i % 2][:, 0:width],
                             wait=[("ve", qvv)], inc="sst%d" % (i % 2))
                        wq["st"].append(val("sst%d" % (i % 2)))

            def wq_meanmath(extra_pe_wait):
                if NOWQ:
                    return
                emit("vector", "tensor_reduce", wcol[:, 0:1], wsums[:, 0:16], AX, OP.add)
                emit("vector", "tensor_reduce", wcol[:, 1:2], wsums[:, 16:48], AX,
                     OP.add, inc="ve")
                vv = val("ve")
                emit("tensor", "matmul", pp4[0:2, 0:1], wcol[:, 0:2], onesc[:],
                     start=True, stop=True,
                     wait=[("ve", vv)] + extra_pe_wait, inc="pe")
                pv_ = val("pe")
                emit("vector", "tensor_copy", wsums[0:2, 48:49], pp4[0:2, 0:1],
                     wait=[("pe", pv_)], inc="ve")
                vv = val("ve")
                emit("tensor", "matmul", pp5[0:1, 0:128], wsums[:, 48:49], idf[:],
                     start=True, stop=True, wait=[("ve", vv)], inc="pe")
                pv_ = val("pe")
                emit("vector", "tensor_copy", rowb[0:1, 10:12], pp5[0:1, 0:2],
                     wait=[("pe", pv_)])
                emit("vector", "tensor_scalar", rowb[0:1, 10:12], rowb[0:1, 10:12],
                     1.0 / (D * DH), 1e-5, OP.mult, OP.max)
                emit("vector", "reciprocal", rowb[0:1, 12:14], rowb[0:1, 10:12],
                     inc="ve")
                vv = val("ve")
                for i, src_off in enumerate((12, 10)):
                    for k in range(2):
                        emit("tensor", "matmul", pp5[:, 4 + i * 2 + k:5 + i * 2 + k],
                             onesr[0:1, :], rowb[0:1, src_off + k:src_off + k + 1],
                             start=True, stop=True,
                             wait=([("ve", vv)] if (i == 0 and k == 0) else ()),
                             inc=("pe" if (i == 1 and k == 1) else None))
                pv_ = val("pe")
                emit("vector", "tensor_copy", sw_bc[:, 0:2], pp5[:, 4:6],
                     wait=[("pe", pv_)])
                emit("vector", "tensor_copy", swr_bc[:, 0:2], pp5[:, 6:8], inc="ve")
                wq["sw_ready"] = val("ve")

            if dbg and os.environ.get("KERNEL_DBG_EARLY", "") == "1" and not mute["on"]:
                emit("sync", "dma_start", dbg_y, y_fm[:, :, 1536:2048],
                     wait=[("pe", val("pe")), ("ve", val("ve")), ("ac", val("ac"))],
                     inc="so")
                waitonly("sync", ("so", val("so")))
            exp_done = []
            av_done = []
            gidx = [0]
            last_fin = [0]
            ve_ppb_rd = [val("ve")]
            ve_accs = [val("ve")]
            ve_ppb_mul = [val("ve")]
            rec_ac = [0]
            rl_rd = [0]
            ktc = [0]
            s_done_l = {}
            pend_fin = [None]

            for hp in range(HP):
                phase_mark(4 + hp)
                last_exp = exp_done[-1] if exp_done else 0
                ve_kq = {}
                for wi_, (which, dst) in enumerate((("k", k_hp), ("q", q_hp))):
                    for ntl in range(8):
                        ppv = pp6 if ntl % 2 == 0 else pp7
                        w = [("ac", last_exp), ("sgk%d" % (hp % 2), kq_load[hp])]
                        key = (which, ntl - 2)
                        if key in ve_kq:
                            w.append(("ve", ve_kq[key]))
                        if which == "q" and ntl < 2:
                            w.append(("ve", ve_kq[("k", 6 + ntl)]))
                        if hp == 0 and which == "k" and ntl < 2:
                            w.append(("ve", ve_vd[NT - 2 + ntl]))
                        for ch in range(8):
                            emit("tensor", "matmul", ppv[:, 0:512],
                                 wqkp[:, hp % 2, wi_, ch, 0:128],
                                 y_fm[:, ch, ntl * 512:(ntl + 1) * 512],
                                 start=(ch == 0), stop=(ch == 7),
                                 wait=(w if ch == 0 else ()),
                                 inc=("pe" if ch == 7 else None))
                        pv_ = val("pe")
                        emit("vector", "tensor_copy", dst[:, ntl * 512:(ntl + 1) * 512],
                             ppv[:, 0:512], wait=[("pe", pv_)], inc="ve")
                        ve_kq[(which, ntl)] = val("ve")
                if hp + 2 < HP:
                    pvkq = val("pe")
                    sk = "sgk%d" % (hp % 2)
                    emit("gpsimd", "dma_start", wqkp[:, hp % 2, 0], wkp_in[:, hp + 2],
                         wait=[("pe", pvkq)], inc=sk)
                    emit("gpsimd", "dma_start", wqkp[:, hp % 2, 1], wqp_in[:, hp + 2],
                         inc=sk)
                    kq_load[hp + 2] = val(sk)
                if hp == HP - 1:
                    cnt["pe_last_kq"] = val("pe")

                pe_norm = {0: 0, 1: 0}
                vr0 = max(ve_ppb_rd[0], last_fin[0])
                ve_red = {0: vr0, 1: vr0}
                nidx = 0
                for which, srcb, coli in (("k", k_hp, 0), ("q", q_hp, 1)):
                    for ntl in range(8):
                        par = nidx % 2
                        col0 = 512 * par
                        wv_ = [("ve", ve_kq[(which, ntl)])]
                        if pe_norm[par]:
                            wv_.append(("pe", pe_norm[par]))
                        emit("scalar", "square", sqs_nb[par][:],
                             srcb[:, ntl * 512:(ntl + 1) * 512], wait=wv_, inc="ac")
                        av = val("ac")
                        emit("tensor", "matmul", pp_b[0:2, col0:col0 + 512],
                             ones2[:], sqs_nb[par][:], start=True, stop=True,
                             wait=[("ac", av), ("ve", ve_red[par])], inc="pe")
                        pe_norm[par] = val("pe")
                        if nidx in (0, 8):
                            ve_red[par] = emit(
                                "vector", "tensor_reduce", kq2[0:2, coli:coli + 1],
                                pp_b[0:2, col0:col0 + 512], AX, OP.max,
                                wait=[("pe", pe_norm[par]), ("gp", gp_consts)],
                                inc="ve")
                        else:
                            ve_red[par] = emit(
                                "vector", "tensor_reduce", stat_t3[0:2, 0:1],
                                pp_b[0:2, col0:col0 + 512], AX, OP.max,
                                wait=[("pe", pe_norm[par])])
                            emit("vector", "tensor_max", kq2[0:2, coli:coli + 1],
                                 kq2[0:2, coli:coli + 1], stat_t3[0:2, 0:1], inc="ve")
                        nidx += 1
                ve_ppb_rd[0] = val("ve")
                vkq = val("ve")
                emit("tensor", "matmul", pp4[0:1, 0:128], kq2[:, 0:1], idf[:],
                     start=True, stop=True,
                     wait=[("ve", vkq), ("ac", last_exp), ("gp", gp_consts)])
                emit("tensor", "matmul", pp4[0:1, 128:256], kq2[:, 1:2], idf[:],
                     start=True, stop=True, inc="pe")
                pv_ = val("pe")
                emit("vector", "tensor_copy", rowb[0:1, 0:2], pp4[0:1, 0:2],
                     wait=[("pe", pv_)])
                emit("vector", "tensor_copy", rowb[0:1, 2:4], pp4[0:1, 128:130])
                emit("vector", "tensor_mul", rowb[0:1, 4:6], rowb[0:1, 0:2],
                     rowb[0:1, 2:4], inc="ve")
                vv = val("ve")
                emit("scalar", "sqrt", rowb[0:1, 6:8], rowb[0:1, 4:6],
                     wait=[("ve", vv)], inc="ac")
                av = val("ac")
                emit("vector", "tensor_scalar", rowb[0:1, 6:8], rowb[0:1, 6:8],
                     -0.125, None, OP.mult, wait=[("ac", av)], inc="ve")
                vv = val("ve")
                for h in range(2):
                    emit("tensor", "matmul", pp4[:, 256 + h:257 + h], onesr[0:1, :],
                         rowb[0:1, 6 + h:7 + h], start=True, stop=True,
                         wait=([("ve", vv)] if h == 0 else ()),
                         inc=("pe" if h == 1 else None))
                pv_ = val("pe")
                emit("vector", "tensor_copy", negc[:, 0:2], pp4[:, 256:258],
                     wait=[("pe", pv_), ("ac", last_exp)], inc="ve")
                ve_negc = val("ve")

                for qb in range(8):
                    nkt = 4 * qb + 4
                    q0 = qb * 512
                    gbase = gidx[0]

                    def S_emit(kt, q0=q0, gbase=gbase):
                        gi = gbase + kt
                        sel = gi % 2
                        sA = pp4 if sel == 0 else pp5
                        sB = pp6 if sel == 0 else pp7
                        w = [("ve", ve_negc)]
                        if gi >= 2:
                            w.append(("ac", exp_done[gi - 2]))
                        emit("tensor", "matmul", sA[:, 0:512],
                             k_hp[0:64, kt * 128:(kt + 1) * 128],
                             q_hp[0:64, q0:q0 + 512],
                             start=True, stop=True, tile_position=(0, 0), wait=w)
                        emit("tensor", "matmul", sB[:, 0:512],
                             k_hp[64:128, kt * 128:(kt + 1) * 128],
                             q_hp[64:128, q0:q0 + 512],
                             start=True, stop=True, tile_position=(64, 0),
                             inc="pe")
                        s_done_l[gi] = val("pe")

                    def tail_emit(kt, q0=q0, gbase=gbase, nkt=nkt, hp=hp, qb=qb):
                        gi = gbase + kt
                        sel = gi % 2
                        sA = pp4 if sel == 0 else pp5
                        sB = pp6 if sel == 0 else pp7
                        we = [("pe", s_done_l[gi])]
                        if gi >= 2:
                            we.append(("pe", av_done[gi - 2]))
                        emit("scalar", "activation", ptile[:, 0, sel, :],
                             sA[:, 0:512], AF.Exp, bias=negc[:, 0:1], scale=0.125,
                             wait=we)
                        emit("scalar", "activation", ptile[:, 1, sel, :],
                             sB[:, 0:512], AF.Exp, bias=negc[:, 1:2], scale=0.125,
                             inc="ac", inck=2)
                        exp_done.append(val("ac"))
                        avw = [("ac", exp_done[gi])]
                        if kt >= 4 * qb:
                            dj = kt - 4 * qb
                            msl = mbm[:, 384 - 128 * dj:896 - 128 * dj]
                            emit("vector", "tensor_mul", ptile[:, 0, sel, :],
                                 ptile[:, 0, sel, :], msl,
                                 wait=[("ac", exp_done[gi])])
                            emit("vector", "tensor_mul", ptile[:, 1, sel, :],
                                 ptile[:, 1, sel, :], msl, inc="ve")
                            avw = [("ve", val("ve"))]
                        avw0 = list(avw)
                        if kt == 0:
                            avw0.append(("ve", ve_accs[0]))
                        emit("tensor", "matmul", pp_a[0:65, 0:512],
                             v_sb[:, kt, 130 * hp:130 * hp + 65],
                             ptile[:, 0, sel, :],
                             start=(kt == 0), stop=(kt == nkt - 1), wait=avw0)
                        emit("tensor", "matmul", pp_a[0:65, 512:1024],
                             v_sb[:, kt, 130 * hp + 65:130 * hp + 130],
                             ptile[:, 1, sel, :],
                             start=(kt == 0), stop=(kt == nkt - 1), inc="pe")
                        av_done.append(val("pe"))
                        ktc[0] += 1
                        if ktc[0] % 6 == 0:
                            wq_step(1)

                    for kt in range(nkt):
                        S_emit(kt)
                        if kt == 1 and pend_fin[0] is not None:
                            pend_fin[0]()
                            pend_fin[0] = None
                        if kt >= 1:
                            tail_emit(kt - 1)
                    if pend_fin[0] is not None:
                        pend_fin[0]()
                        pend_fin[0] = None
                    tail_emit(nkt - 1)
                    gidx[0] += nkt
                    pv_ = av_done[-1]
                    emit("vector", "tensor_copy", accs[:, 0, :], pp_a[0:65, 0:512],
                         wait=[("pe", pv_)])
                    emit("vector", "tensor_copy", accs[:, 1, :], pp_a[0:65, 512:1024],
                         inc="ve")
                    ve_accs[0] = val("ve")
                    emit("vector", "reciprocal", rlA64, accs[64:65, 0, :],
                         wait=[("pe", rl_rd[0])])
                    emit("vector", "reciprocal", rlB64, accs[64:65, 1, :],
                         inc="ve")
                    rec_ac[0] = val("ve")

                    def fin_pe(hp=hp, q0=q0, va=ve_accs[0], ra=rec_ac[0]):
                        emit("tensor", "matmul", pp_b[0:64, 0:512],
                             onesr2[64:65, 0:64], rlA64, start=True, stop=True,
                             wait=[("ve", max(va, ra)),
                                   ("ve", ve_ppb_mul[0]), ("gp", gp_consts)])
                        emit("tensor", "matmul", pp_b[0:64, 512:1024],
                             onesr2[64:65, 0:64], rlB64, start=True, stop=True,
                             inc="pe")
                        pv2 = val("pe")
                        rl_rd[0] = pv2
                        emit("vector", "tensor_mul", attn[0:64, hp, q0:q0 + 512],
                             accs[0:64, 0, :], pp_b[0:64, 0:512],
                             wait=[("pe", pv2)])
                        emit("vector", "tensor_mul", atmp[0:64, :],
                             accs[0:64, 1, :], pp_b[0:64, 512:1024], inc="ve")
                        vv2 = val("ve")
                        ve_ppb_mul[0] = vv2
                        emit("tensor", "matmul", pp_b[:, 0:512], esh[0:64, 0:128],
                             atmp[0:64, :], start=True, stop=True,
                             wait=[("ve", vv2)], inc="pe")
                        pv3 = val("pe")
                        emit("vector", "tensor_copy",
                             attn[64:128, hp, q0:q0 + 512],
                             pp_b[64:128, 0:512], wait=[("pe", pv3)], inc="ve")
                        last_fin[0] = val("ve")

                    pend_fin[0] = fin_pe

                if pend_fin[0] is not None:
                    pend_fin[0]()
                    pend_fin[0] = None
                if hp == 1:
                    wq_step(48)
                    wq_meanmath([("ac", exp_done[-1]), ("ve", last_fin[0])])
                    wq.update(li=0, ci=0, ldv=[], use=[], st=[], mode="quant",
                              gate=[("ve", val("ve"))])
            wq_step(48)

            if dbg and os.environ.get("KERNEL_DBG_EARLY", "") == "3" and not mute["on"]:
                emit("sync", "dma_start", dbg_y, y_fm[:, :, 1536:2048],
                     wait=[("pe", val("pe")), ("ve", val("ve")), ("ac", val("ac"))],
                     inc="so")
                emit("sync", "dma_start", dbg_kq[:, 0, :], k_hp[:], inc="so")
                emit("sync", "dma_start", dbg_kq[:, 1, :], q_hp[:], inc="so")
                emit("sync", "dma_start", dbg_at, attn[:], inc="so")
                waitonly("sync", ("so", val("so")))
            phase_mark(8)
            wq_done_ve = val("ve")
            wq_done_ac = val("ac")
            wq_done_so = (val("sst0"), val("sst1"))
            if dbg and not mute["on"]:
                emit("sync", "dma_start", dbg_y, y_fm[:, :, 1536:2048],
                     wait=[("pe", val("pe")), ("ve", val("ve")), ("ac", val("ac"))],
                     inc="so")
                emit("sync", "dma_start", dbg_kq[:, 0, :], k_hp[:], inc="so")
                emit("sync", "dma_start", dbg_kq[:, 1, :], q_hp[:], inc="so")
                emit("sync", "dma_start", dbg_at, attn[:], inc="so")
                waitonly("sync", ("so", val("so")))
            emit("gpsimd", "dma_start", wo8, wo_in,
                 wait=[("pe", cnt["pe_last_kq"]), ("ve", last_fin[0])], inc="sg")
            d_wo8 = val("sg")
            ve_x1 = {}
            so_x1 = {}
            ve_add = {}
            x25 = [x0t[0], x0t[1], xsq[0], xsq[1]]
            s25 = ["sdx0", "sdx1", "swq0", "swq1"]
            for j in range(NT):
                bi = j % 2
                bi4 = j % 4
                ppw = pp_a if bi == 0 else pp_b
                w = [("ve", last_fin[0])]
                if j >= 2:
                    w.append(("ve", ve_x1[j - 2]))
                if j == 0:
                    w.append(("sg", d_wo8))
                for half in range(2):
                    for hpj in range(HP):
                        emit("tensor", "matmul", ppw[:, half * 512:(half + 1) * 512],
                             attn[:, hpj, j * 128:(j + 1) * 128],
                             wo8[:, hpj, half * 512:(half + 1) * 512],
                             start=(hpj == 0), stop=(hpj == 3),
                             wait=(w if (half == 0 and hpj == 0) else ()),
                             inc=("pe" if (half == 1 and hpj == 3) else None))
                pv_ = val("pe")
                wd = [("ve", ve_x1.get(j - 4, 0))]
                if j in (2, 3):
                    wd.append(("ve", wq_done_ve))
                    wd.append(("ac", wq_done_ac))
                    wd.append(("sst0", wq_done_so[0]))
                    wd.append(("sst1", wq_done_so[1]))
                sx = s25[bi4]
                dv = emit("sync", "dma_start", x25[bi4][:],
                          x_in[j * 128:(j + 1) * 128, :], wait=wd, inc=sx)
                emit("vector", "tensor_copy", stgiv[:, bi:bi + 1], stat_iv[:, j:j + 1],
                     wait=[("ac", cnt.get("ac_n25_%d" % (j - 2), 0))])
                emit("vector", "tensor_copy", stgb1[:, bi:bi + 1], stat_b1[:, j:j + 1],
                     inc="ve")
                stg_ve = val("ve")
                emit("scalar", "activation", n25b[bi][:], x25[bi4][:], AF.Identity,
                     bias=stgb1[:, bi:bi + 1], scale=stgiv[:, bi:bi + 1],
                     wait=[(sx, dv), ("ve", ve_add.get(j - 2, 0)), ("ve", stg_ve),
                           ("sg", d_wo8), ("pe", cnt["pe_last_kq"])], inc="ac")
                cnt["ac_n25_%d" % j] = val("ac")
                av = val("ac")
                emit("vector", "tensor_add", u25b[bi][:], x25[bi4][:], n25b[bi][:],
                     wait=[("ac", av)], inc="ve")
                ve_add[j] = val("ve")
                emit("vector", "scalar_tensor_tensor", x1p[bi][:], u25b[bi][:], 0.5,
                     ppw[:, 0:1024], OP.mult, OP.add,
                     wait=[("pe", pv_), ("sci%d" % bi, so_x1.get(j - 2, 0)),
                           ("sst0", wq_done_so[0]), ("sst1", wq_done_so[1]),
                           ("ve", wq_done_ve)], inc="ve")
                ve_x1[j] = val("ve")
                cr = ((j % 16) // 4) * 1024 + (j // 16) * 512 + (j % 4) * 128
                emit("sync", "dma_start", cin[cr:cr + 128, :], x1p[bi][:],
                     wait=[("ve", ve_x1[j])], inc="sci%d" % bi)
                if j == 15:
                    emit("sync", "dma_start", ag_in[0:1, :], x1p[bi][127:128, :],
                         inc="sci%d" % bi)
                if j == 16:
                    emit("sync", "dma_start", ag_in[1:2, :], x1p[bi][0:1, :],
                         inc="sci%d" % bi)
                so_x1[j] = val("sci%d" % bi)

            phase_mark(9)
            emit("gpsimd", "collective_compute", "AllGather", OP.bypass,
                 replica_groups=GROUPS, ins=[ag_in], outs=[ag_out],
                 wait=[("sci1", so_x1[15]), ("sci0", so_x1[16])], inc="co")
            for k in range(4):
                emit("gpsimd", "collective_compute", "ReduceScatter", OP.add,
                     replica_groups=GROUPS,
                     ins=[cin[k * 1024:(k + 1) * 1024, :]],
                     outs=[rs_pad[1 + k * 512:1 + (k + 1) * 512, :]],
                     wait=[("sci0", so_x1[18 + 4 * k]),
                           ("sci1", so_x1[19 + 4 * k])],
                     inc="co")
            dv = emit("sync", "dma_start", agsb[:], ag_out,
                      wait=[("co", 1), ("ve", wq_done_ve), ("ac", wq_done_ac)],
                      inc="sd")
            emit("tensor", "matmul", pp4[0:2, 0:512], wselb[:], agsb[:, 0:512],
                 start=True, stop=True,
                 wait=[("sd", dv), ("ac", exp_done[-1]), ("ve", last_fin[0])])
            emit("tensor", "matmul", pp5[0:2, 0:512], wselb[:], agsb[:, 512:1024],
                 start=True, stop=True, inc="pe")
            pv_ = val("pe")
            emit("vector", "tensor_copy", x1halo[:, 0:512], pp4[0:2, 0:512],
                 wait=[("pe", pv_), ("ac", wq_done_ac)])
            emit("vector", "tensor_copy", x1halo[:, 512:1024], pp5[0:2, 0:512],
                 inc="ve")
            vv = val("ve")
            emit("sync", "dma_start", rs_pad[0:1, :], x1halo[0:1, :],
                 wait=[("ve", vv)], inc="so")
            emit("sync", "dma_start", rs_pad[T + 1:T + 2, :], x1halo[1:2, :],
                 inc="so")
            so_halo = val("so")
            co_base = val("co") - 4

            finalize_debug(x0t[0][:])
            if not mute.get("done"):
                bv_endA = {s_: val(s_) for s_ in ("sd", "sg", "so", "pe", "ve", "ac", "gp", "sdx0", "sdx1", "swq0", "swq1", "sst0", "sst1", "sci0", "sci1", "sgk0", "sgk1")}
                for eng in ENGS:
                    waitonly(eng, *[(s_, v) for s_, v in bv_endA.items() if v > 0])
            run_block()

        with ExitStack() as fctx:
            ef = fctx.enter_context
            w1q = ef(nc.sbuf_tensor("w1q", [128, 8, DH], BF16))
            w2q = ef(nc.sbuf_tensor("w2q", [128, 32, D], BF16))
            diag = ef(nc.sbuf_tensor("diag", [128, 32, 4, 64], BF16))
            win = [ef(nc.sbuf_tensor("win%d" % i, [128, D], F32)) for i in range(2)]
            x1o = win
            n3 = ef(nc.sbuf_tensor("n3", [128, D], F32))
            sq3 = ef(nc.sbuf_tensor("sq3", [128, D], F32))
            t3 = sq3
            x2b = [n3, sq3]
            xqt = ef(nc.sbuf_tensor("xqt", [128, D], BF16))
            xqf = ef(nc.sbuf_tensor("xqf", [128, 8, 256], BF16))
            h1s = ef(nc.sbuf_tensor("h1s", [128, 2, 256], BF16))
            h3 = ef(nc.sbuf_tensor("h3", [128, 32, 254], F32))
            habs = ef(nc.sbuf_tensor("habs", [128, 254], F32))
            hacc = ef(nc.sbuf_tensor("hacc", [128, 254], F32))
            h3q = ef(nc.sbuf_tensor("h3q", [128, 2, 254], BF16))
            scol = ef(nc.sbuf_tensor("scol", [128, 16], F32))
            c2c = ef(nc.sbuf_tensor("c2c", [128, 6], F32))
            stiv = ef(nc.sbuf_tensor("stiv", [128, 2], F32))
            stb2 = ef(nc.sbuf_tensor("stb2", [128, 2], F32))
            sts1 = ef(nc.sbuf_tensor("sts1", [128, 2], F32))
            e2c = ef(nc.sbuf_tensor("e2c", [128, 2], F32))
            xqtf = xqt.ap().bitcast(F32)
            bc1 = xqtf[:, 0:256]
            bc2 = xqtf[:, 256:512]
            r1row = n3.ap()[0:1, 0:256]
            s2row = n3.ap()[0:1, 256:512]

            d1 = emit("gpsimd", "dma_start", w1q[:],
                      w1q_h.rearrange("(c p) n -> p c n", p=128), inc="sgk0")
            d2 = emit("gpsimd", "dma_start", w2q[:],
                      w2q_h.rearrange("(c p) n -> p c n", p=128), inc="sgk1")
            d3 = emit("gpsimd", "dma_start", diag[:], diag_in, inc="sg")
            emit("vector", "memset", scol[:], 0.0)
            emit("vector", "memset", c2c[:], 0.0, inc="ve")
            ve_diag = val("ve")
            waitonly("tensor", ("ve", ve_diag))

            prev = {"epi_ve": 0, "outs": 0, "mm2_pe": 0, "xq_pe": 0}
            for ci, (c0, n) in enumerate(FFN_CHUNKS):
                phase_mark(10 + ci)
                Wn = n + 2
                ta = min(128, Wn)
                tb = Wn - ta
                na = min(128, n)
                nb = n - na
                wd = [("ve", prev["epi_ve"]), ("so", prev["outs"]),
                      ("co", co_base + [0, 0, 1, 1, 2, 2, 3, 3, 3][ci] + 1)]
                if ci in (0, 8):
                    wd.append(("so", so_halo))
                dva = emit("sync", "dma_start", win[0][0:ta, :],
                           rs_pad[c0:c0 + ta, :], wait=wd, inc="sd")
                if tb > 0:
                    emit("sync", "dma_start", win[1][0:tb, :],
                         rs_pad[c0 + ta:c0 + Wn, :], inc="sd")
                dva = dvb = val("sd")
                submark(0 + 1000 * ci)
                tiles = [(0, ta, dva)] + ([(1, tb, dvb)] if tb > 0 else [])
                ve_sq_prev = 0
                for wi, rows, dv in tiles:
                    o = 5 * wi
                    wt = win[wi]
                    emit("scalar", "square", sq3[0:rows, :], wt[0:rows, :],
                         wait=[("sd", dv), ("ve", ve_sq_prev),
                               ("so", prev["outs"])], inc="ac")
                    av = val("ac")
                    emit("vector", "tensor_reduce", scol[0:rows, o:o + 1],
                         wt[0:rows, :], AX, OP.add, wait=[("sd", dv)])
                    emit("vector", "tensor_reduce", scol[0:rows, o + 1:o + 2],
                         sq3[0:rows, :], AX, OP.add, wait=[("ac", av)], inc="ve")
                    ve_sq_prev = val("ve")
                    emit("vector", "tensor_scalar", scol[0:rows, o:o + 1],
                         scol[0:rows, o:o + 1], 1.0 / D, None, OP.mult)
                    emit("vector", "tensor_scalar", scol[0:rows, o + 1:o + 2],
                         scol[0:rows, o + 1:o + 2], 1.0 / D, None, OP.mult)
                    emit("vector", "tensor_mul", scol[0:rows, o + 2:o + 3],
                         scol[0:rows, o:o + 1], scol[0:rows, o:o + 1])
                    emit("vector", "tensor_sub", scol[0:rows, o + 1:o + 2],
                         scol[0:rows, o + 1:o + 2], scol[0:rows, o + 2:o + 3])
                    emit("vector", "tensor_scalar", scol[0:rows, o + 2:o + 3],
                         scol[0:rows, o + 1:o + 2], EPS, None, OP.add, inc="ve")
                    vv = val("ve")
                    emit("scalar", "sqrt", scol[0:rows, o + 3:o + 4],
                         scol[0:rows, o + 2:o + 3], wait=[("ve", vv)], inc="ac")
                    av = val("ac")
                    emit("vector", "reciprocal", stiv[0:rows, wi:wi + 1],
                         scol[0:rows, o + 3:o + 4], wait=[("ac", av)])
                    for _ in range(2):
                        emit("vector", "tensor_mul", scol[0:rows, o + 3:o + 4],
                             stiv[0:rows, wi:wi + 1], stiv[0:rows, wi:wi + 1])
                        emit("vector", "tensor_mul", scol[0:rows, o + 3:o + 4],
                             scol[0:rows, o + 3:o + 4], scol[0:rows, o + 2:o + 3])
                        emit("vector", "tensor_scalar", scol[0:rows, o + 3:o + 4],
                             scol[0:rows, o + 3:o + 4], -0.5, 1.5, OP.mult, OP.add)
                        emit("vector", "tensor_mul", stiv[0:rows, wi:wi + 1],
                             stiv[0:rows, wi:wi + 1], scol[0:rows, o + 3:o + 4])
                    emit("vector", "scalar_tensor_tensor", stb2[0:rows, wi:wi + 1],
                         scol[0:rows, o:o + 1], -1.0, stiv[0:rows, wi:wi + 1],
                         OP.mult, OP.mult, inc="ve")

                submark(5 + 1000 * ci)
                ve_stats3 = val("ve")
                ve_nprev = 0
                for wi, rows, dv in tiles:
                    o = 5 * wi
                    wt = win[wi]
                    emit("scalar", "activation", n3[0:rows, :], wt[0:rows, :],
                         AF.Identity, bias=stb2[0:rows, wi:wi + 1],
                         scale=stiv[0:rows, wi:wi + 1],
                         wait=[("ve", ve_stats3), ("ve", ve_nprev)], inc="ac")
                    av = val("ac")
                    emit("vector", "tensor_reduce", scol[0:rows, 10 + wi:11 + wi],
                         n3[0:rows, :], AX, OP.max, apply_absolute_value=True,
                         wait=[("ac", av)])
                    emit("vector", "tensor_scalar", scol[0:rows, 10 + wi:11 + wi],
                         scol[0:rows, 10 + wi:11 + wi], 1e-5, None, OP.max)
                    emit("vector", "reciprocal", sts1[0:rows, wi:wi + 1],
                         scol[0:rows, 10 + wi:11 + wi])
                    emit("vector", "tensor_scalar", sts1[0:rows, wi:wi + 1],
                         sts1[0:rows, wi:wi + 1], 127.0, None, OP.mult)
                    submark(6 + 1000 * ci)
                    emit("vector", "scalar_tensor_tensor", scol[0:rows, 14 + wi:15 + wi],
                         scol[0:rows, 10 + wi:11 + wi], 1.0 / 127.0,
                         swr_bc[0:rows, 0:1], OP.mult, OP.mult)
                    submark(7 + 1000 * ci)
                    emit("vector", "tensor_scalar", t3[0:rows, :], n3[0:rows, :],
                         sts1[0:rows, wi:wi + 1], MAGIC, OP.mult, OP.add)
                    submark(8 + 1000 * ci)
                    emit("vector", "tensor_scalar", xqt[0:rows, :], t3[0:rows, :],
                         MAGIC, None, OP.subtract,
                         wait=[("pe", prev["xq_pe"])], inc="ve")
                    submark(9 + 1000 * ci)
                    ve_nprev = val("ve")
                    submark(10 + 1000 * ci)
                    for c in range(8):
                        pv8 = pp4 if c % 2 == 0 else pp5
                        w = [("ve", ve_nprev)]
                        key = "xqcp_%d" % (c % 2)
                        if key in cnt:
                            w.append(("ve", cnt[key]))
                        if ci > 0 and wi == 0 and c < 2:
                            w.append(("pe", prev["mm2_pe"]))
                        emit("tensor", "matmul", pv8[:, 0:rows],
                             xqt[0:rows, c * 128:(c + 1) * 128], idb[0:rows, 0:rows],
                             start=True, stop=True, wait=w, inc="pe")
                        pv_ = val("pe")
                        emit("vector", "tensor_copy",
                             xqf[:, c, wi * 128:wi * 128 + rows], pv8[:, 0:rows],
                             wait=[("pe", pv_)], inc="ve")
                        cnt[key] = val("ve")
                    prev["xq_pe"] = val("pe")
                submark(20 + 1000 * ci)
                ve_xq = val("ve")
                ac_n3 = val("ac")
                dxa = emit("sync", "dma_start", x1o[0][0:na, :],
                           rs_pad[c0 + 1:c0 + 1 + na, :],
                           wait=[("ac", ac_n3), ("ve", ve_xq)], inc="sd")
                if nb > 0:
                    emit("sync", "dma_start", x1o[1][0:nb, :],
                         rs_pad[c0 + 1 + na:c0 + 1 + n, :], inc="sd")
                dxa = dxb = val("sd")

                emit("tensor", "matmul", pp6[0:1, 0:128], scol[:, 14:15], idf[:],
                     start=True, stop=True, wait=[("ve", ve_xq)])
                emit("tensor", "matmul", pp6[0:1, 128:256], scol[:, 15:16], idf[:],
                     start=True, stop=True, inc="pe")
                pv_ = val("pe")
                emit("vector", "tensor_copy", r1row[0:1, 0:256], pp6[0:1, 0:256],
                     wait=[("pe", pv_)], inc="ve")
                vv = val("ve")
                emit("tensor", "matmul", pp_b[:, 0:Wn], onesr[0:1, :],
                     r1row[0:1, 0:Wn], start=True, stop=True, wait=[("ve", vv)],
                     inc="pe")
                pv_ = val("pe")
                emit("vector", "tensor_copy", bc1[:, 0:Wn], pp_b[:, 0:Wn],
                     wait=[("pe", pv_)], inc="ve")
                ve_bc1 = val("ve")

                submark(30 + 1000 * ci)
                e1 = {}
                s3 = {}
                a3 = {}
                for cc in range(32):
                    ppm = pp4 if cc % 2 == 0 else pp5
                    w = [("ve", ve_bc1)]
                    if ci == 0 and cc == 0:
                        w.append(("sgk0", d1))
                    if cc < 2:
                        w.append(("ve", cnt.get("xqcp_%d" % (cc % 2), 0)))
                    if cc >= 2:
                        w.append(("ve", e1[cc - 2]))
                    for ch in range(8):
                        emit("tensor", "matmul", ppm[:, 0:Wn],
                             w1q[:, ch, cc * 128:(cc + 1) * 128], xqf[:, ch, 0:Wn],
                             start=(ch == 0), stop=(ch == 7),
                             wait=(w if ch == 0 else ()),
                             inc=("pe" if ch == 7 else None))
                    pv_ = val("pe")
                    emit("vector", "tensor_mul", h1s[:, cc % 2, 0:Wn], ppm[:, 0:Wn],
                         bc1[:, 0:Wn], wait=[("pe", pv_)], inc="ve")
                    e1[cc] = val("ve")
                    submark(40 + 1000 * ci)
                    ppc = pp6 if cc % 2 == 0 else pp7
                    w = [("ve", e1[cc])]
                    if ci == 0 and cc == 0:
                        w.append(("sg", d3))
                    if cc >= 2:
                        w.append(("ac", s3[cc - 2]))
                    for tap in range(3):
                        emit("tensor", "matmul", ppc[0:64, 0:n],
                             diag[0:64, cc, tap, :], h1s[0:64, cc % 2, tap:tap + n],
                             start=(tap == 0), stop=False,
                             tile_position=(0, 0), wait=(w if tap == 0 else ()))
                    emit("tensor", "matmul", ppc[0:64, 0:n],
                         diag[0:64, cc, 3, :], onesb[0:64, 0:n],
                         start=False, stop=True, tile_position=(0, 0))
                    for tap in range(3):
                        emit("tensor", "ldweights", diag[64:128, cc, tap, :],
                             tile_position=(64, 64))
                        emit("tensor", "matmul", ppc[64:128, 0:n],
                             diag[64:128, cc, tap, :], h1s[64:128, cc % 2, tap:tap + n],
                             start=(tap == 0), stop=False,
                             tile_position=(64, 64))
                    emit("tensor", "ldweights", diag[64:128, cc, 3, :],
                         tile_position=(64, 64))
                    emit("tensor", "matmul", ppc[64:128, 0:n],
                         diag[64:128, cc, 3, :], onesb[64:128, 0:n],
                         start=False, stop=True,
                         tile_position=(64, 64), inc="pe")
                    cvv = val("pe")
                    emit("scalar", "activation", h3[:, cc, 0:n], ppc[:, 0:n], AF.Silu,
                         bias=0.0, scale=1.0, wait=[("pe", cvv)])
                    emit("scalar", "activation", habs[:, 0:n], h3[:, cc, 0:n], AF.Abs,
                         wait=([("ve", a3[cc - 1])] if cc >= 1 else ()), inc="ac")
                    s3[cc] = val("ac")
                    if cc == 0:
                        emit("vector", "tensor_copy", hacc[:, 0:n], habs[:, 0:n],
                             wait=[("ac", s3[cc])], inc="ve")
                    else:
                        emit("vector", "tensor_max", hacc[:, 0:n], hacc[:, 0:n],
                             habs[:, 0:n], wait=[("ac", s3[cc])], inc="ve")
                    a3[cc] = val("ve")

                submark(50 + 1000 * ci)
                emit("tensor", "matmul", pp_a[0:na, 0:128], hacc[:, 0:na], idf[:],
                     start=True, stop=True,
                     wait=[("ve", a3[31]), ("pe", prev["mm2_pe"])],
                     inc=(None if nb > 0 else "pe"))
                if nb > 0:
                    emit("tensor", "matmul", pp_a[0:nb, 512:640], hacc[:, na:n],
                         idf[:], start=True, stop=True, inc="pe")
                pv_ = val("pe")
                emit("vector", "tensor_reduce", c2c[0:na, 0:1], pp_a[0:na, 0:128],
                     AX, OP.max, wait=[("pe", pv_)])
                if nb > 0:
                    emit("vector", "tensor_reduce", c2c[0:nb, 1:2],
                         pp_a[0:nb, 512:640], AX, OP.max)
                emit("vector", "tensor_scalar", c2c[:, 0:2], c2c[:, 0:2], 1e-5, None,
                     OP.max)
                emit("vector", "reciprocal", c2c[:, 2:4], c2c[:, 0:2])
                emit("vector", "tensor_scalar", c2c[:, 2:4], c2c[:, 2:4], 127.0, None,
                     OP.mult)
                emit("vector", "scalar_tensor_tensor", e2c[:, 0:1], c2c[:, 0:1],
                     1.0 / 127.0, swr_bc[:, 1:2], OP.mult, OP.mult)
                emit("vector", "scalar_tensor_tensor", e2c[:, 1:2], c2c[:, 1:2],
                     1.0 / 127.0, swr_bc[:, 1:2], OP.mult, OP.mult, inc="ve")
                vv = val("ve")
                emit("tensor", "matmul", pp6[0:1, 0:128], c2c[:, 2:3], idf[:],
                     start=True, stop=True, wait=[("ve", vv)],
                     inc=(None if nb > 0 else "pe"))
                if nb > 0:
                    emit("tensor", "matmul", pp6[0:1, 128:256], c2c[:, 3:4], idf[:],
                         start=True, stop=True, inc="pe")
                pv_ = val("pe")
                emit("vector", "tensor_copy", s2row[0:1, 0:256], pp6[0:1, 0:256],
                     wait=[("pe", pv_)], inc="ve")
                vv = val("ve")
                emit("tensor", "matmul", pp_b[:, 256:256 + n], onesr[0:1, :],
                     s2row[0:1, 0:n], start=True, stop=True, wait=[("ve", vv)],
                     inc="pe")
                pv_ = val("pe")
                emit("vector", "tensor_copy", bc2[:, 0:n], pp_b[:, 256:256 + n],
                     wait=[("pe", pv_)], inc="ve")
                ve_bc2 = val("ve")

                submark(60 + 1000 * ci)
                qv = {}
                for cc in range(32):
                    w = [("ve", ve_bc2)]
                    if cc >= 2:
                        w.append(("pe", qv[cc - 2]))
                    emit("vector", "tensor_mul", habs[:, 0:n], h3[:, cc, 0:n],
                         bc2[:, 0:n], wait=w, inc="ve")
                    vv0 = val("ve")
                    emit("scalar", "activation", hacc[:, 0:n], habs[:, 0:n],
                         AF.Identity, bias=magicc[:], scale=1.0, wait=[("ve", vv0)],
                         inc="ac")
                    av = val("ac")
                    emit("vector", "tensor_scalar", h3q[:, cc % 2, 0:n], hacc[:, 0:n],
                         MAGIC, None, OP.subtract, wait=[("ac", av)], inc="ve")
                    vvq = val("ve")
                    mtl = [(0, na)] + ([(na, nb)] if nb else [])
                    for mi, (ms, mw) in enumerate(mtl):
                        ppo = pp_a if mi == 0 else pp_b
                        for half in range(2):
                            wq2 = [("ve", vvq)] if (mi == 0 and half == 0) else []
                            if ci == 0 and cc == 0 and mi == 0 and half == 0:
                                wq2.append(("sgk1", d2))
                            emit("tensor", "matmul",
                                 ppo[0:mw, half * 512:(half + 1) * 512],
                                 h3q[:, cc % 2, ms:ms + mw],
                                 w2q[:, cc, half * 512:(half + 1) * 512],
                                 start=(cc == 0), stop=(cc == 31),
                                 wait=wq2,
                                 inc=("pe" if (mi == len(mtl) - 1 and half == 1) else None))
                    qv[cc] = val("pe")
                prev["mm2_pe"] = qv[31]

                submark(70 + 1000 * ci)
                for mi, (ms, mw) in enumerate([(0, na)] + ([(na, nb)] if nb else [])):
                    ppo = pp_a if mi == 0 else pp_b
                    ecol = e2c[0:mw, mi:mi + 1]
                    xo = x1o[mi]
                    dv = dxa if mi == 0 else dxb
                    emit("vector", "scalar_tensor_tensor", x2b[mi][0:mw, 0:512],
                         ppo[0:mw, 0:512], ecol, xo[0:mw, 0:512], OP.mult, OP.add,
                         wait=[("pe", prev["mm2_pe"]), ("sd", dv)])
                    emit("vector", "scalar_tensor_tensor", x2b[mi][0:mw, 512:1024],
                         ppo[0:mw, 512:1024], ecol, xo[0:mw, 512:1024],
                         OP.mult, OP.add, inc="ve")
                    vv = val("ve")
                    emit("sync", "dma_start", out[c0 + ms:c0 + ms + mw, :],
                         x2b[mi][0:mw, :], wait=[("ve", vv)], inc="so")
                prev["epi_ve"] = val("ve")
                prev["outs"] = val("so")

            phase_mark(98)
            if dbg and not mute["on"]:
                emit("sync", "dma_start", dbg_cin, cin, inc="so")
                emit("sync", "dma_start", dbg_rs, rs_pad, inc="so")
            finalize_debug(rs_pad[1:129, :])
            waitonly("sync", ("so", val("so")))
            run_block()
    return nc


def kernel(L0_attn_x, L0_attn_ln_w, L0_attn_ln_b, L0_attn_wq, L0_attn_wk,
           L0_attn_wv, L0_attn_wo, L0_ffn_w1, L0_ffn_conv_w, L0_ffn_conv_b,
           L0_ffn_w2):
    x = np.ascontiguousarray(np.asarray(L0_attn_x, dtype=np.float32))
    lnwv = np.asarray(L0_attn_ln_w, np.float32)
    lnbv = np.asarray(L0_attn_ln_b, np.float32)
    wqa = np.asarray(L0_attn_wq, np.float32)
    wka = np.asarray(L0_attn_wk, np.float32)
    wva = np.asarray(L0_attn_wv, np.float32)
    woa = np.asarray(L0_attn_wo, np.float32)
    w1 = np.asarray(L0_ffn_w1, np.float32)
    cw = np.asarray(L0_ffn_conv_w, np.float32)
    cb = np.asarray(L0_ffn_conv_b, np.float32)
    w2 = np.asarray(L0_ffn_w2, np.float32)

    lnw_bc = np.ascontiguousarray(np.broadcast_to(lnwv, (128, D)))
    lnb_bc = np.ascontiguousarray(np.broadcast_to(lnbv, (128, D)))
    w1T = np.ascontiguousarray(w1.T)
    w2T = np.ascontiguousarray(w2.T)
    cwr = cw[:, 0, :].reshape(32, 128, 3).transpose(1, 0, 2)
    cbr = cb.reshape(32, 128).T[:, :, None]
    cw4 = np.concatenate([cwr, cbr], axis=2)
    diag_host = np.zeros((128, 32, 4, 64), np.float32)
    for p in range(128):
        diag_host[p, :, :, p % 64] = cw4[p]

    import os
    _upto = int(os.environ.get("KERNEL_UPTO", "99"))
    _ksub = int(os.environ.get("KERNEL_KSUB", "9999"))
    _dbg = os.environ.get("KERNEL_DBG", "") == "1"
    if _NC_CACHE[0] is None:
        _NC_CACHE[0] = build_program(_upto, _ksub, _dbg)
    nc = _NC_CACHE[0]
    kernel._DBG_RES = [None]

    def pack(w8):
        wp = w8.reshape(4, 2, D, E).transpose(0, 2, 1, 3).reshape(4, D, 128)
        return np.ascontiguousarray(wp.reshape(4, 8, 128, 128).transpose(2, 0, 1, 3))

    in_maps = []
    for c in range(8):
        b, r = c // 2, c % 2
        hsl = slice(8 * r, 8 * r + 8)
        wv8 = wva[hsl].transpose(1, 0, 2).reshape(D, 512)
        wv_sb = np.ascontiguousarray(wv8.reshape(8, 128, 512).transpose(1, 0, 2))
        wo8 = woa[512 * r:512 * (r + 1), :]
        wo_sb = np.ascontiguousarray(wo8.reshape(4, 128, D).transpose(1, 0, 2))
        if r == 0:
            wsel = np.array([[0, 0], [0, 1], [0, 0], [0, 1]], np.float32)
        else:
            wsel = np.array([[1, 0], [0, 0], [1, 0], [0, 0]], np.float32)
        in_maps.append({
            "x": np.ascontiguousarray(x[b]),
            "wqp": pack(wqa[hsl]), "wkp": pack(wka[hsl]),
            "wv8": wv_sb, "wo8": wo_sb,
            "lnw_bc": lnw_bc, "lnb_bc": lnb_bc,
            "w1T": w1T, "w2T": w2T,
            "diag_in": diag_host,
            "wsel": wsel,
        })
    res = run_bass_kernel_spmd(nc, in_maps, core_ids=list(range(8)))
    import kernel as _k
    _k._DBG_RES = res
    y = np.empty((4, S, D), np.float32)
    for c in range(8):
        b, r = c // 2, c % 2
        y[b, r * T:(r + 1) * T, :] = res.results[c]["out"]
    return y


# version 09 (final)
# speedup vs baseline: 1.1676x; 1.1071x over previous
import os
import numpy as np
import concourse.bass as bass
import concourse.mybir as mybir
from concourse.alu_op_type import AluOpType as OP
from concourse.bass_utils import run_bass_kernel_spmd

F32 = mybir.dt.float32
BF16 = mybir.dt.bfloat16
AF = mybir.ActivationFunctionType
AX = mybir.AxisListType.X

S, D, E, HP, DH = 4096, 1024, 64, 4, 4096
T = S // 2
EPS = 1e-5
MAGIC = float(np.float32(1.5 * 2 ** 23))
NT = S // 128
FFN_CHUNKS = [(i * 254, 254) for i in range(8)] + [(2032, 16)]
ENGS = ("tensor", "vector", "scalar", "sync", "gpsimd")
GROUPS = [[0, 1], [2, 3], [4, 5], [6, 7]]

_NC_CACHE = [None]


def build_program(upto=99, ksub=10**9, dbg=False):
    nc = bass.Bass("TRN2", target_bir_lowering=False)
    x_in = nc.dram_tensor("x", [S, D], F32, kind="ExternalInput").ap()
    wqp_in = nc.dram_tensor("wqp", [128, HP, 8, 128], F32, kind="ExternalInput").ap()
    wkp_in = nc.dram_tensor("wkp", [128, HP, 8, 128], F32, kind="ExternalInput").ap()
    wv_in = nc.dram_tensor("wv8", [128, 8, 512], F32, kind="ExternalInput").ap()
    wo_in = nc.dram_tensor("wo8", [128, HP, 1024], F32, kind="ExternalInput").ap()
    lnw_in = nc.dram_tensor("lnw_bc", [128, D], F32, kind="ExternalInput").ap()
    lnb_in = nc.dram_tensor("lnb_bc", [128, D], F32, kind="ExternalInput").ap()
    w1T_in = nc.dram_tensor("w1T", [D, DH], F32, kind="ExternalInput").ap()
    w2T_in = nc.dram_tensor("w2T", [DH, D], F32, kind="ExternalInput").ap()
    diag_in = nc.dram_tensor("diag_in", [128, 32, 4, 64], F32, kind="ExternalInput").ap()
    wsel_in = nc.dram_tensor("wsel", [4, 2], F32, kind="ExternalInput").ap()
    out = nc.dram_tensor("out", [T, D], F32, kind="ExternalOutput").ap()
    cin = nc.dram_tensor("cin", [S, D], F32, kind="Internal").ap()
    rs_pad = nc.dram_tensor("rs_pad", [T + 2, D], F32, kind="Internal", addr_space="Local").ap()
    ag_in = nc.dram_tensor("ag_in", [2, D], F32, kind="Internal").ap()
    ag_out = nc.dram_tensor("ag_out", [4, D], F32, kind="Internal", addr_space="Local").ap()
    w1q_h = nc.dram_tensor("w1q_h", [D, DH], BF16, kind="Internal").ap()
    w2q_h = nc.dram_tensor("w2q_h", [DH, D], BF16, kind="Internal").ap()
    if dbg:
        dbg_y = nc.dram_tensor("dbg_y", [128, 8, 512], BF16, kind="ExternalOutput").ap()
        dbg_y2 = nc.dram_tensor("dbg_y2", [128, 8, 512], BF16, kind="ExternalOutput").ap()
        dbg_kq = nc.dram_tensor("dbg_kq", [128, 2, S], BF16, kind="ExternalOutput").ap()
        dbg_at = nc.dram_tensor("dbg_at", [128, HP, S], BF16, kind="ExternalOutput").ap()
        dbg_cin = nc.dram_tensor("dbg_cin", [S, D], F32, kind="ExternalOutput").ap()
        dbg_rs = nc.dram_tensor("dbg_rs", [T + 2, D], F32, kind="ExternalOutput").ap()

    cnt = {}

    def bump(sem, k=1):
        cnt[sem] = cnt.get(sem, 0) + k
        return cnt[sem]

    def val(sem):
        return cnt.get(sem, 0)

    from contextlib import ExitStack
    with ExitStack() as top:
        et = top.enter_context
        sems = {n: et(nc.semaphore(n)) for n in
                ("sd", "sg", "so", "pe", "ve", "ac", "gp", "co",
                 "sdx0", "sdx1", "swq0", "swq1", "sst0", "sst1",
                 "sci0", "sci1", "sgk0", "sgk1")}

        ops = {e: [] for e in ENGS}
        mute = {"on": False}

        def phase_mark(n):
            if n > upto:
                mute["on"] = True

        def submark(k):
            if k > ksub:
                mute["on"] = True

        BLANKET = {"vector": "ve", "scalar": "ac", "gpsimd": "gp"}

        def emit(eng, meth, *args, wait=(), inc=None, inck=1, **kw):
            if mute["on"]:
                return cnt.get(inc, 0)
            if meth == "dma_start":
                inck = 16
            bsem = BLANKET.get(eng)
            if bsem is not None and meth not in ("dma_start", "collective_compute"):
                wait = tuple(wait) + ((bsem, val(bsem)),)
                if inc is None:
                    inc = bsem
            v = bump(inc, inck) if inc else None
            wmax = {}
            for s, w in wait:
                if w > wmax.get(s, 0):
                    wmax[s] = w

            def fn(E, meth=meth, args=args, kw=kw, wait=tuple(wmax.items()),
                   inc=inc, inck=inck):
                for s, w in wait:
                    E.wait_ge(sems[s], w)
                ins = getattr(E, meth)(*args, **kw)
                if inc:
                    ins.then_inc(sems[inc], inck)
            ops[eng].append(fn)
            return v

        def waitonly(eng, *pairs):
            if mute["on"]:
                return
            def fn(E, pairs=tuple(pairs)):
                for s, w in pairs:
                    if w > 0:
                        E.wait_ge(sems[s], w)
            ops[eng].append(fn)

        def finalize_debug(buf):
            if not mute["on"] or mute.get("done"):
                return
            mute["on"] = False
            for sname in ("sd", "sg", "so", "pe", "ve", "ac", "gp", "co", "sdx0", "sdx1", "swq0", "swq1", "sst0", "sst1", "sci0", "sci1", "sgk0", "sgk1"):
                if val(sname) > 0:
                    waitonly("sync", (sname, val(sname)))
            emit("sync", "dma_start", out[0:128, :], buf, inc="so")
            waitonly("sync", ("so", val("so")))
            mute["on"] = True
            mute["done"] = True

        def run_block():
            with nc.Block() as block:
                @block.tensor
                def _(t):
                    for fn in ops["tensor"]:
                        fn(t)

                @block.vector
                def _(v):
                    for fn in ops["vector"]:
                        fn(v)

                @block.scalar
                def _(sc):
                    for fn in ops["scalar"]:
                        fn(sc)

                @block.sync
                def _(sy):
                    for fn in ops["sync"]:
                        fn(sy)

                @block.gpsimd
                def _(g):
                    for fn in ops["gpsimd"]:
                        fn(g)
            for e in ENGS:
                ops[e] = []

        stat_iv = et(nc.sbuf_tensor("stat_iv", [128, NT], F32))
        stat_b1 = et(nc.sbuf_tensor("stat_b1", [128, NT], F32))
        stat_m = et(nc.sbuf_tensor("stat_m", [128, NT], F32))
        stat_R = et(nc.sbuf_tensor("stat_R", [128, NT], F32))
        stat_t1 = et(nc.sbuf_tensor("stat_t1", [128, NT], F32))
        stat_t2 = et(nc.sbuf_tensor("stat_t2", [128, NT], F32))
        stat_t3 = et(nc.sbuf_tensor("stat_t3", [128, NT], F32))
        mbm = et(nc.sbuf_tensor("mbm", [128, 896], BF16))
        idb = et(nc.sbuf_tensor("idb", [128, 128], BF16))
        idf = et(nc.sbuf_tensor("idf", [128, 128], F32))
        esh = et(nc.sbuf_tensor("esh", [128, 128], BF16))
        ones2 = et(nc.sbuf_tensor("ones2", [128, 2], BF16))
        onesr = et(nc.sbuf_tensor("onesr", [1, 128], F32))
        onesr2 = et(nc.sbuf_tensor("onesr2", [65, 128], F32))
        onesc = et(nc.sbuf_tensor("onesc", [128, 1], F32))
        onesb = et(nc.sbuf_tensor("onesb", [128, 256], BF16))
        wselb = et(nc.sbuf_tensor("wselb", [4, 2], F32))
        rowb = et(nc.sbuf_tensor("rowb", [1, 16], F32))
        sw_bc = et(nc.sbuf_tensor("sw_bc", [128, 2], F32))
        swr_bc = et(nc.sbuf_tensor("swr_bc", [128, 2], F32))
        negc = et(nc.sbuf_tensor("negc", [128, 2], F32))
        magicc = et(nc.sbuf_tensor("magicc", [128, 1], F32))
        stgm = et(nc.sbuf_tensor("stgm", [128, 2], F32))
        stgr = et(nc.sbuf_tensor("stgr", [128, 2], F32))
        stgiv = et(nc.sbuf_tensor("stgiv", [128, 2], F32))
        stgb1 = et(nc.sbuf_tensor("stgb1", [128, 2], F32))
        wsums = et(nc.sbuf_tensor("wsums", [128, 49], F32))
        wcol = et(nc.sbuf_tensor("wcol", [128, 2], F32))

        pp_a = et(nc.psum_tensor("pp_a", [128, 1024], F32))
        pp_b = et(nc.psum_tensor("pp_b", [128, 1024], F32))
        pp4 = et(nc.psum_tensor("pp4", [128, 512], F32))
        pp5 = et(nc.psum_tensor("pp5", [128, 512], F32))
        pp6 = et(nc.psum_tensor("pp6", [128, 512], F32))
        pp7 = et(nc.psum_tensor("pp7", [128, 512], F32))

        with ExitStack() as actx:
            ea = actx.enter_context
            y_fm = ea(nc.sbuf_tensor("y_fm", [128, 8, S], BF16))
            v_sb = ea(nc.sbuf_tensor("v_sb", [128, 32, 520], BF16))
            k_hp = ea(nc.sbuf_tensor("k_hp", [128, S], BF16))
            q_hp = ea(nc.sbuf_tensor("q_hp", [128, S], BF16))
            attn = ea(nc.sbuf_tensor("attn", [128, HP, S], BF16))
            wqkp = ea(nc.sbuf_tensor("wqkp_s", [128, 2, 2, 8, 128], BF16))
            x0t = [ea(nc.sbuf_tensor("x0t%d" % i, [128, D], F32)) for i in range(2)]
            s1c = ea(nc.sbuf_tensor("s1c", [128, NT], F32))
            s2c = ea(nc.sbuf_tensor("s2c", [128, NT], F32))
            ptile = ea(nc.sbuf_tensor("ptile", [128, 2, 2, 512], BF16))
            accs = ea(nc.sbuf_tensor("accs", [65, 2, 512], F32))
            kq2 = ea(nc.sbuf_tensor("kq2", [128, 2], F32))
            scr = ea(nc.sbuf_tensor("scr", [128, 8192], F32))
            scr_ap = scr.ap()
            scrb = scr_ap.bitcast(BF16)
            xsq = [scr_ap[:, 0:1024], scr_ap[:, 1024:2048]]
            ytmp = scr_ap[:, 0:1024]
            ytk = [scrb[:, 4096:5120], scrb[:, 5120:6144]]
            lnw = scr_ap[:, 6144:7168]
            lnb = scr_ap[:, 7168:8192]
            wv8 = scrb[:, 8192:12288].rearrange("p (c n) -> p c n", c=8)
            sqs_nb = [scrb[:, 12288:12800], scrb[:, 12800:13312]]
            wbuf = [scr_ap[:, 0:2048], scr_ap[:, 2048:4096]]
            wqo = [scrb[:, 8192:10240], scrb[:, 10240:12288]]
            rlA64 = scr_ap[64:65, 6656:7168]
            rlB64 = scr_ap[64:65, 7680:8192]
            atmp = scrb[0:64, 14336:14848]
            yv = y_fm.ap().rearrange("p a b -> p (a b)")
            yvf = yv.bitcast(F32)
            wo8 = yv[:, 0:4096].rearrange("p (a b) -> p a b", a=HP)
            u25b = [yvf[:, 2048:3072], yvf[:, 6144:7168]]
            n25b = [yvf[:, 3072:4096], yvf[:, 7168:8192]]
            x1p = [yvf[:, 4096:5120], yvf[:, 5120:6144]]
            x1halo = scr_ap[0:2, 2048:3072]
            agsb = scr_ap[0:4, 3072:4096]

            emit("sync", "dma_start", lnw[:], lnw_in, inc="sd")
            emit("sync", "dma_start", lnb[:], lnb_in, inc="sd")
            
            emit("sync", "dma_start", wselb[:], wsel_in, inc="sd")
            d_w0 = val("sd")
            kq_load = {}
            for h0 in range(2):
                sk = "sgk%d" % h0
                emit("gpsimd", "dma_start", wqkp[:, h0, 0], wkp_in[:, h0], inc=sk)
                emit("gpsimd", "dma_start", wqkp[:, h0, 1], wqp_in[:, h0], inc=sk)
                kq_load[h0] = val(sk)
            emit("gpsimd", "dma_start", wv8[:, :, :], wv_in, inc="sg")
            d_wg = val("sg")
            emit("gpsimd", "memset", mbm[:], 1.0, inc="gp")
            emit("gpsimd", "affine_select", out=mbm[:], in_=mbm[:], compare_op=OP.is_ge,
                 fill=0.0, base=-384, pattern=[[1, 896]], channel_multiplier=-1,
                 wait=[("gp", val("gp"))])
            emit("gpsimd", "memset", idb[:], 0.0, inc="gp")
            emit("gpsimd", "affine_select", out=idb[:], in_=idb[:], compare_op=OP.not_equal,
                 fill=1.0, base=0, pattern=[[-1, 128]], channel_multiplier=1,
                 wait=[("gp", val("gp"))])
            emit("gpsimd", "memset", idf[:], 0.0, inc="gp")
            emit("gpsimd", "affine_select", out=idf[:], in_=idf[:], compare_op=OP.not_equal,
                 fill=1.0, base=0, pattern=[[-1, 128]], channel_multiplier=1,
                 wait=[("gp", val("gp"))])
            emit("gpsimd", "memset", esh[:], 0.0, inc="gp")
            emit("gpsimd", "affine_select", out=esh[:], in_=esh[:], compare_op=OP.not_equal,
                 fill=1.0, base=64, pattern=[[-1, 128]], channel_multiplier=1,
                 wait=[("gp", val("gp"))])
            emit("gpsimd", "memset", ones2[:], 0.0, inc="gp")
            emit("gpsimd", "memset", ones2[0:64, 0:1], 1.0,
                 wait=[("gp", val("gp"))], inc="gp")
            emit("gpsimd", "memset", ones2[64:128, 1:2], 1.0,
                 wait=[("gp", val("gp"))])
            emit("gpsimd", "memset", onesr[:], 1.0)
            emit("gpsimd", "memset", onesr2[:], 1.0)
            emit("gpsimd", "memset", magicc[:], MAGIC)
            emit("gpsimd", "memset", onesb[:], 1.0)
            emit("gpsimd", "memset", onesc[:], 1.0)
            emit("gpsimd", "memset", v_sb[:], 1.0, inc="gp")
            emit("gpsimd", "memset", kq2[:], 0.0, inc="gp")
            emit("gpsimd", "memset", wsums[:], 0.0, inc="gp")
            gp_consts = val("gp")

            ve_p1a = {}
            for j in range(NT):
                bi = j % 2
                sx = "sdx%d" % bi
                w = [("ve", ve_p1a.get(j - 2, 0))]
                dv = emit("sync", "dma_start", x0t[bi][:], x_in[j * 128:(j + 1) * 128, :],
                          wait=w, inc=sx)
                emit("scalar", "square", xsq[bi][:], x0t[bi][:],
                     wait=[(sx, dv), ("ve", ve_p1a.get(j - 2, 0))], inc="ac")
                a1 = val("ac")
                emit("vector", "tensor_reduce", s1c[:, j:j + 1], x0t[bi][:], AX, OP.add,
                     wait=[(sx, dv)])
                emit("vector", "tensor_reduce", s2c[:, j:j + 1], xsq[bi][:], AX, OP.add,
                     wait=[("ac", a1)], inc="ve")
                ve_p1a[j] = val("ve")

            def newton_rsqrt(dst, src, tmp_a, tmp_b):
                emit("scalar", "sqrt", tmp_a[:], src[:],
                     wait=[("ve", val("ve"))], inc="ac")
                a0 = val("ac")
                emit("vector", "reciprocal", dst[:], tmp_a[:], wait=[("ac", a0)])
                for _ in range(2):
                    emit("vector", "tensor_mul", tmp_b[:], dst[:], dst[:])
                    emit("vector", "tensor_mul", tmp_b[:], tmp_b[:], src[:])
                    emit("vector", "tensor_scalar", tmp_b[:], tmp_b[:], -0.5, 1.5,
                         OP.mult, OP.add)
                    emit("vector", "tensor_mul", dst[:], dst[:], tmp_b[:])

            waitonly("vector", ("ve", ve_p1a[NT - 1]))
            emit("vector", "tensor_scalar", stat_m[:], s1c[:], 1.0 / D, None, OP.mult)
            emit("vector", "tensor_scalar", stat_t1[:], s2c[:], 1.0 / D, None, OP.mult)
            emit("vector", "tensor_mul", stat_t2[:], stat_m[:], stat_m[:])
            emit("vector", "tensor_sub", stat_t1[:], stat_t1[:], stat_t2[:])
            emit("vector", "tensor_scalar", stat_t2[:], stat_t1[:], EPS, None, OP.add,
                 inc="ve")
            newton_rsqrt(stat_iv, stat_t2, stat_t3, s1c)
            emit("vector", "scalar_tensor_tensor", stat_b1[:], stat_m[:], -1.0,
                 stat_iv[:], OP.mult, OP.mult)
            emit("vector", "scalar_tensor_tensor", stat_t2[:], stat_t2[:], EPS,
                 stat_t1[:], OP.mult, OP.add, inc="ve")
            newton_rsqrt(stat_R, stat_t2, stat_t3, s1c)
            emit("vector", "memset", rowb[:], 0.0, inc="ve")
            ve_stats = val("ve")

            phase_mark(2)
            waitonly("sync", ("ve", ve_stats), ("sg", d_wg), ("gp", gp_consts),
                     ("sd", d_w0))
            pe_tp = {}
            ve_cp = {}
            for j in range(NT):
                bi = j % 2
                ve_prev_y = cnt.get("ve_y_%d" % (j - 2), 0)
                sx = "sdx%d" % bi
                dv = emit("sync", "dma_start", x0t[bi][:], x_in[j * 128:(j + 1) * 128, :],
                          wait=[("ve", ve_prev_y)], inc=sx)
                wv_ = [(sx, dv), ("ve", ve_stats)]
                if (j - 2, 7) in pe_tp:
                    wv_.append(("pe", pe_tp[(j - 2, 7)]))
                emit("vector", "tensor_copy", stgm[:, bi:bi + 1], stat_m[:, j:j + 1],
                     wait=wv_)
                emit("vector", "tensor_copy", stgr[:, bi:bi + 1], stat_R[:, j:j + 1])
                emit("vector", "scalar_tensor_tensor", ytmp[:], x0t[bi][:],
                     stgm[:, bi:bi + 1], lnw[:], OP.subtract, OP.mult)
                emit("vector", "scalar_tensor_tensor", ytk[bi][:], ytmp[:],
                     stgr[:, bi:bi + 1], lnb[:], OP.mult, OP.add, inc="ve")
                cnt["ve_y_%d" % j] = val("ve")
                for c in range(8):
                    pv8 = pp4 if c % 2 == 0 else pp5
                    w = [("ve", cnt["ve_y_%d" % j])]
                    key = (j, c - 2) if c >= 2 else ((j - 1, c + 6) if j >= 1 else None)
                    if key is not None and key in ve_cp:
                        w.append(("ve", ve_cp[key]))
                    emit("tensor", "matmul", pv8[:, 0:128],
                         ytk[bi][:, c * 128:(c + 1) * 128], idb[:],
                         start=True, stop=True, wait=w, inc="pe")
                    pe_tp[(j, c)] = val("pe")
                    emit("vector", "tensor_copy", y_fm[:, c, j * 128:(j + 1) * 128],
                         pv8[:, 0:128], wait=[("pe", pe_tp[(j, c)])], inc="ve")
                    ve_cp[(j, c)] = val("ve")

            if dbg and os.environ.get("KERNEL_DBG_EARLY", "") == "2" and not mute["on"]:
                emit("sync", "dma_start", dbg_y, y_fm[:, :, 1536:2048],
                     wait=[("pe", val("pe")), ("ve", val("ve")), ("ac", val("ac"))],
                     inc="so")
                emit("sync", "dma_start", dbg_y2, y_fm[:, :, 1536:2048], inc="so")
                waitonly("sync", ("so", val("so")))
            phase_mark(3)
            ve_all_cp = val("ve")
            waitonly("tensor", ("ve", ve_all_cp), ("sg", d_wg), ("gp", gp_consts))
            ve_vd = {}
            for kt in range(NT):
                ppv = pp6 if kt % 2 == 0 else pp7
                w = [("ve", ve_vd.get(kt - 2, 0))]
                for ch in range(8):
                    emit("tensor", "matmul", ppv[:, 0:512],
                         y_fm[:, ch, kt * 128:(kt + 1) * 128], wv8[:, ch, 0:512],
                         start=(ch == 0), stop=(ch == 7), wait=(w if ch == 0 else ()),
                         inc=("pe" if ch == 7 else None))
                pv_ = val("pe")
                if os.environ.get("KERNEL_NOVEVAC", "") == "1":
                    emit("vector", "tensor_copy", v_sb[:, kt, 0:512], ppv[:, 0:512],
                         wait=[("pe", pv_)], inc="ve")
                else:
                    src = ppv[:, 0:512].rearrange("p (h e) -> p h e", h=8)
                    dst = v_sb[:, kt, 0:520].rearrange("p (h e) -> p h e", e=65)[:, :, 0:64]
                    emit("vector", "tensor_copy", dst, src, wait=[("pe", pv_)], inc="ve")
                ve_vd[kt] = val("ve")

            NOWQ = os.environ.get("KERNEL_NOWQ", "") == "1"
            wq_chunks = ([("w1", (j, h)) for j in range(8) for h in range(2)]
                         + [("w2", j) for j in range(32)])
            wq = {"li": 0, "ci": 0, "ldv": [], "use": [], "st": [], "mode": "red",
                  "gate": [("pe", val("pe")), ("ve", val("ve"))]}

            def wq_step(n):
                if NOWQ:
                    return
                for _ in range(n):
                    while wq["li"] < len(wq_chunks) and wq["li"] < wq["ci"] + 2:
                        i = wq["li"]
                        kind, j = wq_chunks[i]
                        if kind == "w1":
                            jj, hh = j
                            width = 2048
                            src = w1T_in[jj * 128:(jj + 1) * 128,
                                         hh * 2048:(hh + 1) * 2048]
                        else:
                            width = D
                            src = w2T_in[j * 128:(j + 1) * 128, :]
                        w = [("ve", wq["use"][i - 2])] if i >= 2 else list(wq["gate"])
                        if wq["mode"] == "quant" and i >= 2:
                            w.append(("sst%d" % (i % 2), wq["st"][i - 2]))
                        dv = emit("sync", "dma_start", wbuf[i % 2][:, 0:width], src,
                                  wait=w, inc="swq%d" % (i % 2))
                        wq["ldv"].append(dv)
                        wq["li"] += 1
                    i = wq["ci"]
                    if i >= len(wq_chunks):
                        return
                    wq["ci"] += 1
                    kind, j = wq_chunks[i]
                    width = 2048 if kind == "w1" else D
                    buf = wbuf[i % 2]
                    dv = wq["ldv"][i]
                    if wq["mode"] == "red":
                        ci_col = (2 * j[0] + j[1]) if kind == "w1" else 16 + j
                        emit("vector", "tensor_reduce", wsums[:, ci_col:ci_col + 1],
                             buf[:, 0:width], AX, OP.add,
                             apply_absolute_value=True,
                             wait=[("swq%d" % (i % 2), dv)] +
                                  ([("gp", gp_consts)] if i == 0 else []),
                             inc="ve")
                        wq["use"].append(val("ve"))
                    else:
                        swcol = sw_bc[:, 0:1] if kind == "w1" else sw_bc[:, 1:2]
                        emit("scalar", "activation", buf[:, 0:width], buf[:, 0:width],
                             AF.Identity, bias=magicc[:], scale=swcol,
                             wait=[("swq%d" % (i % 2), dv), ("ve", wq["sw_ready"])] +
                                  ([("sst%d" % (i % 2), wq["st"][i - 2])] if i >= 2 else []),
                             inc="ac")
                        av = val("ac")
                        emit("vector", "tensor_scalar", buf[:, 0:width], buf[:, 0:width],
                             MAGIC, 1.0, OP.subtract, OP.min, wait=[("ac", av)])
                        emit("vector", "tensor_scalar", wqo[i % 2][:, 0:width],
                             buf[:, 0:width], -1.0, None, OP.max, inc="ve")
                        wq["use"].append(val("ve"))
                        qvv = val("ve")
                        if kind == "w1":
                            jj, hh = j
                            hdst = w1q_h[jj * 128:(jj + 1) * 128,
                                         hh * 2048:(hh + 1) * 2048]
                        else:
                            hdst = w2q_h[j * 128:(j + 1) * 128, :]
                        emit("sync", "dma_start", hdst, wqo[i % 2][:, 0:width],
                             wait=[("ve", qvv)], inc="sst%d" % (i % 2))
                        wq["st"].append(val("sst%d" % (i % 2)))

            def wq_meanmath(extra_pe_wait):
                if NOWQ:
                    return
                emit("vector", "tensor_reduce", wcol[:, 0:1], wsums[:, 0:16], AX, OP.add)
                emit("vector", "tensor_reduce", wcol[:, 1:2], wsums[:, 16:48], AX,
                     OP.add, inc="ve")
                vv = val("ve")
                emit("tensor", "matmul", pp4[0:2, 0:1], wcol[:, 0:2], onesc[:],
                     start=True, stop=True,
                     wait=[("ve", vv)] + extra_pe_wait, inc="pe")
                pv_ = val("pe")
                emit("vector", "tensor_copy", wsums[0:2, 48:49], pp4[0:2, 0:1],
                     wait=[("pe", pv_)], inc="ve")
                vv = val("ve")
                emit("tensor", "matmul", pp5[0:1, 0:128], wsums[:, 48:49], idf[:],
                     start=True, stop=True, wait=[("ve", vv)], inc="pe")
                pv_ = val("pe")
                emit("vector", "tensor_copy", rowb[0:1, 10:12], pp5[0:1, 0:2],
                     wait=[("pe", pv_)])
                emit("vector", "tensor_scalar", rowb[0:1, 10:12], rowb[0:1, 10:12],
                     1.0 / (D * DH), 1e-5, OP.mult, OP.max)
                emit("vector", "reciprocal", rowb[0:1, 12:14], rowb[0:1, 10:12],
                     inc="ve")
                vv = val("ve")
                for i, src_off in enumerate((12, 10)):
                    for k in range(2):
                        emit("tensor", "matmul", pp5[:, 4 + i * 2 + k:5 + i * 2 + k],
                             onesr[0:1, :], rowb[0:1, src_off + k:src_off + k + 1],
                             start=True, stop=True,
                             wait=([("ve", vv)] if (i == 0 and k == 0) else ()),
                             inc=("pe" if (i == 1 and k == 1) else None))
                pv_ = val("pe")
                emit("vector", "tensor_copy", sw_bc[:, 0:2], pp5[:, 4:6],
                     wait=[("pe", pv_)])
                emit("vector", "tensor_copy", swr_bc[:, 0:2], pp5[:, 6:8], inc="ve")
                wq["sw_ready"] = val("ve")

            if dbg and os.environ.get("KERNEL_DBG_EARLY", "") == "1" and not mute["on"]:
                emit("sync", "dma_start", dbg_y, y_fm[:, :, 1536:2048],
                     wait=[("pe", val("pe")), ("ve", val("ve")), ("ac", val("ac"))],
                     inc="so")
                waitonly("sync", ("so", val("so")))
            exp_done = []
            av_done = []
            gidx = [0]
            last_fin = [0]
            ve_ppb_rd = [val("ve")]
            ve_accs = [val("ve")]
            ve_ppb_mul = [val("ve")]
            rec_ac = [0]
            rl_rd = [0]
            ktc = [0]
            s_done_l = {}
            pend_fin = [None]

            for hp in range(HP):
                phase_mark(4 + hp)
                last_exp = exp_done[-1] if exp_done else 0
                ve_kq = {}
                for wi_, (which, dst) in enumerate((("k", k_hp), ("q", q_hp))):
                    for ntl in range(8):
                        ppv = pp6 if ntl % 2 == 0 else pp7
                        w = [("ac", last_exp), ("sgk%d" % (hp % 2), kq_load[hp])]
                        key = (which, ntl - 2)
                        if key in ve_kq:
                            w.append(("ve", ve_kq[key]))
                        if which == "q" and ntl < 2:
                            w.append(("ve", ve_kq[("k", 6 + ntl)]))
                        if hp == 0 and which == "k" and ntl < 2:
                            w.append(("ve", ve_vd[NT - 2 + ntl]))
                        for ch in range(8):
                            emit("tensor", "matmul", ppv[:, 0:512],
                                 wqkp[:, hp % 2, wi_, ch, 0:128],
                                 y_fm[:, ch, ntl * 512:(ntl + 1) * 512],
                                 start=(ch == 0), stop=(ch == 7),
                                 wait=(w if ch == 0 else ()),
                                 inc=("pe" if ch == 7 else None))
                        pv_ = val("pe")
                        emit("vector", "tensor_copy", dst[:, ntl * 512:(ntl + 1) * 512],
                             ppv[:, 0:512], wait=[("pe", pv_)], inc="ve")
                        ve_kq[(which, ntl)] = val("ve")
                if hp + 2 < HP:
                    pvkq = val("pe")
                    sk = "sgk%d" % (hp % 2)
                    emit("gpsimd", "dma_start", wqkp[:, hp % 2, 0], wkp_in[:, hp + 2],
                         wait=[("pe", pvkq)], inc=sk)
                    emit("gpsimd", "dma_start", wqkp[:, hp % 2, 1], wqp_in[:, hp + 2],
                         inc=sk)
                    kq_load[hp + 2] = val(sk)
                if hp == HP - 1:
                    cnt["pe_last_kq"] = val("pe")

                pe_norm = {0: 0, 1: 0}
                vr0 = max(ve_ppb_rd[0], last_fin[0])
                ve_red = {0: vr0, 1: vr0}
                nidx = 0
                for which, srcb, coli in (("k", k_hp, 0), ("q", q_hp, 1)):
                    for ntl in range(8):
                        par = nidx % 2
                        col0 = 512 * par
                        wv_ = [("ve", ve_kq[(which, ntl)])]
                        if pe_norm[par]:
                            wv_.append(("pe", pe_norm[par]))
                        emit("scalar", "square", sqs_nb[par][:],
                             srcb[:, ntl * 512:(ntl + 1) * 512], wait=wv_, inc="ac")
                        av = val("ac")
                        emit("tensor", "matmul", pp_b[0:2, col0:col0 + 512],
                             ones2[:], sqs_nb[par][:], start=True, stop=True,
                             wait=[("ac", av), ("ve", ve_red[par])], inc="pe")
                        pe_norm[par] = val("pe")
                        if nidx in (0, 8):
                            ve_red[par] = emit(
                                "vector", "tensor_reduce", kq2[0:2, coli:coli + 1],
                                pp_b[0:2, col0:col0 + 512], AX, OP.max,
                                wait=[("pe", pe_norm[par]), ("gp", gp_consts)],
                                inc="ve")
                        else:
                            ve_red[par] = emit(
                                "vector", "tensor_reduce", stat_t3[0:2, 0:1],
                                pp_b[0:2, col0:col0 + 512], AX, OP.max,
                                wait=[("pe", pe_norm[par])])
                            emit("vector", "tensor_max", kq2[0:2, coli:coli + 1],
                                 kq2[0:2, coli:coli + 1], stat_t3[0:2, 0:1], inc="ve")
                        nidx += 1
                ve_ppb_rd[0] = val("ve")
                vkq = val("ve")
                emit("tensor", "matmul", pp4[0:1, 0:128], kq2[:, 0:1], idf[:],
                     start=True, stop=True,
                     wait=[("ve", vkq), ("ac", last_exp), ("gp", gp_consts)])
                emit("tensor", "matmul", pp4[0:1, 128:256], kq2[:, 1:2], idf[:],
                     start=True, stop=True, inc="pe")
                pv_ = val("pe")
                emit("vector", "tensor_copy", rowb[0:1, 0:2], pp4[0:1, 0:2],
                     wait=[("pe", pv_)])
                emit("vector", "tensor_copy", rowb[0:1, 2:4], pp4[0:1, 128:130])
                emit("vector", "tensor_mul", rowb[0:1, 4:6], rowb[0:1, 0:2],
                     rowb[0:1, 2:4], inc="ve")
                vv = val("ve")
                emit("scalar", "sqrt", rowb[0:1, 6:8], rowb[0:1, 4:6],
                     wait=[("ve", vv)], inc="ac")
                av = val("ac")
                emit("vector", "tensor_scalar", rowb[0:1, 6:8], rowb[0:1, 6:8],
                     -0.125, None, OP.mult, wait=[("ac", av)], inc="ve")
                vv = val("ve")
                for h in range(2):
                    emit("tensor", "matmul", pp4[:, 256 + h:257 + h], onesr[0:1, :],
                         rowb[0:1, 6 + h:7 + h], start=True, stop=True,
                         wait=([("ve", vv)] if h == 0 else ()),
                         inc=("pe" if h == 1 else None))
                pv_ = val("pe")
                emit("vector", "tensor_copy", negc[:, 0:2], pp4[:, 256:258],
                     wait=[("pe", pv_), ("ac", last_exp)], inc="ve")
                ve_negc = val("ve")

                for qb in range(8):
                    nkt = 4 * qb + 4
                    q0 = qb * 512
                    gbase = gidx[0]

                    def S_emit(kt, q0=q0, gbase=gbase):
                        gi = gbase + kt
                        sel = gi % 2
                        sA = pp4 if sel == 0 else pp5
                        sB = pp6 if sel == 0 else pp7
                        w = [("ve", ve_negc)]
                        if gi >= 2:
                            w.append(("ac", exp_done[gi - 2]))
                        emit("tensor", "matmul", sA[:, 0:512],
                             k_hp[0:64, kt * 128:(kt + 1) * 128],
                             q_hp[0:64, q0:q0 + 512],
                             start=True, stop=True, tile_position=(0, 0), wait=w)
                        emit("tensor", "matmul", sB[:, 0:512],
                             k_hp[64:128, kt * 128:(kt + 1) * 128],
                             q_hp[64:128, q0:q0 + 512],
                             start=True, stop=True, tile_position=(64, 0),
                             inc="pe")
                        s_done_l[gi] = val("pe")

                    def tail_emit(kt, q0=q0, gbase=gbase, nkt=nkt, hp=hp, qb=qb):
                        gi = gbase + kt
                        sel = gi % 2
                        sA = pp4 if sel == 0 else pp5
                        sB = pp6 if sel == 0 else pp7
                        we = [("pe", s_done_l[gi])]
                        if gi >= 2:
                            we.append(("pe", av_done[gi - 2]))
                        emit("scalar", "activation", ptile[:, 0, sel, :],
                             sA[:, 0:512], AF.Exp, bias=negc[:, 0:1], scale=0.125,
                             wait=we)
                        emit("scalar", "activation", ptile[:, 1, sel, :],
                             sB[:, 0:512], AF.Exp, bias=negc[:, 1:2], scale=0.125,
                             inc="ac", inck=2)
                        exp_done.append(val("ac"))
                        avw = [("ac", exp_done[gi])]
                        if kt >= 4 * qb:
                            dj = kt - 4 * qb
                            msl = mbm[:, 384 - 128 * dj:896 - 128 * dj]
                            emit("vector", "tensor_mul", ptile[:, 0, sel, :],
                                 ptile[:, 0, sel, :], msl,
                                 wait=[("ac", exp_done[gi])])
                            emit("vector", "tensor_mul", ptile[:, 1, sel, :],
                                 ptile[:, 1, sel, :], msl, inc="ve")
                            avw = [("ve", val("ve"))]
                        avw0 = list(avw)
                        if kt == 0:
                            avw0.append(("ve", ve_accs[0]))
                        emit("tensor", "matmul", pp_a[0:65, 0:512],
                             v_sb[:, kt, 130 * hp:130 * hp + 65],
                             ptile[:, 0, sel, :],
                             start=(kt == 0), stop=(kt == nkt - 1), wait=avw0)
                        emit("tensor", "matmul", pp_a[0:65, 512:1024],
                             v_sb[:, kt, 130 * hp + 65:130 * hp + 130],
                             ptile[:, 1, sel, :],
                             start=(kt == 0), stop=(kt == nkt - 1), inc="pe")
                        av_done.append(val("pe"))
                        ktc[0] += 1
                        if ktc[0] % 6 == 0:
                            wq_step(1)

                    for kt in range(nkt):
                        S_emit(kt)
                        if kt == 1 and pend_fin[0] is not None:
                            pend_fin[0]()
                            pend_fin[0] = None
                        if kt >= 1:
                            tail_emit(kt - 1)
                    if pend_fin[0] is not None:
                        pend_fin[0]()
                        pend_fin[0] = None
                    tail_emit(nkt - 1)
                    gidx[0] += nkt
                    pv_ = av_done[-1]
                    emit("vector", "tensor_copy", accs[:, 0, :], pp_a[0:65, 0:512],
                         wait=[("pe", pv_)])
                    emit("vector", "tensor_copy", accs[:, 1, :], pp_a[0:65, 512:1024],
                         inc="ve")
                    ve_accs[0] = val("ve")
                    emit("vector", "reciprocal", rlA64, accs[64:65, 0, :],
                         wait=[("pe", rl_rd[0])])
                    emit("vector", "reciprocal", rlB64, accs[64:65, 1, :],
                         inc="ve")
                    rec_ac[0] = val("ve")

                    def fin_pe(hp=hp, q0=q0, va=ve_accs[0], ra=rec_ac[0]):
                        emit("tensor", "matmul", pp_b[0:64, 0:512],
                             onesr2[64:65, 0:64], rlA64, start=True, stop=True,
                             wait=[("ve", max(va, ra)),
                                   ("ve", ve_ppb_mul[0]), ("gp", gp_consts)])
                        emit("tensor", "matmul", pp_b[0:64, 512:1024],
                             onesr2[64:65, 0:64], rlB64, start=True, stop=True,
                             inc="pe")
                        pv2 = val("pe")
                        rl_rd[0] = pv2
                        emit("vector", "tensor_mul", attn[0:64, hp, q0:q0 + 512],
                             accs[0:64, 0, :], pp_b[0:64, 0:512],
                             wait=[("pe", pv2)])
                        emit("vector", "tensor_mul", atmp[0:64, :],
                             accs[0:64, 1, :], pp_b[0:64, 512:1024], inc="ve")
                        vv2 = val("ve")
                        ve_ppb_mul[0] = vv2
                        emit("tensor", "matmul", pp_b[:, 0:512], esh[0:64, 0:128],
                             atmp[0:64, :], start=True, stop=True,
                             wait=[("ve", vv2)], inc="pe")
                        pv3 = val("pe")
                        emit("vector", "tensor_copy",
                             attn[64:128, hp, q0:q0 + 512],
                             pp_b[64:128, 0:512], wait=[("pe", pv3)], inc="ve")
                        last_fin[0] = val("ve")

                    pend_fin[0] = fin_pe

                if pend_fin[0] is not None:
                    pend_fin[0]()
                    pend_fin[0] = None
                if hp == 1:
                    wq_step(48)
                    wq_meanmath([("ac", exp_done[-1]), ("ve", last_fin[0])])
                    wq.update(li=0, ci=0, ldv=[], use=[], st=[], mode="quant",
                              gate=[("ve", val("ve"))])
            wq_step(48)

            if dbg and os.environ.get("KERNEL_DBG_EARLY", "") == "3" and not mute["on"]:
                emit("sync", "dma_start", dbg_y, y_fm[:, :, 1536:2048],
                     wait=[("pe", val("pe")), ("ve", val("ve")), ("ac", val("ac"))],
                     inc="so")
                emit("sync", "dma_start", dbg_kq[:, 0, :], k_hp[:], inc="so")
                emit("sync", "dma_start", dbg_kq[:, 1, :], q_hp[:], inc="so")
                emit("sync", "dma_start", dbg_at, attn[:], inc="so")
                waitonly("sync", ("so", val("so")))
            phase_mark(8)
            wq_done_ve = val("ve")
            wq_done_ac = val("ac")
            wq_done_so = (val("sst0"), val("sst1"))
            if dbg and not mute["on"]:
                emit("sync", "dma_start", dbg_y, y_fm[:, :, 1536:2048],
                     wait=[("pe", val("pe")), ("ve", val("ve")), ("ac", val("ac"))],
                     inc="so")
                emit("sync", "dma_start", dbg_kq[:, 0, :], k_hp[:], inc="so")
                emit("sync", "dma_start", dbg_kq[:, 1, :], q_hp[:], inc="so")
                emit("sync", "dma_start", dbg_at, attn[:], inc="so")
                waitonly("sync", ("so", val("so")))
            emit("gpsimd", "dma_start", wo8, wo_in,
                 wait=[("pe", cnt["pe_last_kq"]), ("ve", last_fin[0])], inc="sg")
            d_wo8 = val("sg")
            ve_x1 = {}
            so_x1 = {}
            ve_add = {}
            x25 = [x0t[0], x0t[1], xsq[0], xsq[1]]
            s25 = ["sdx0", "sdx1", "swq0", "swq1"]
            for j in range(NT):
                bi = j % 2
                bi4 = j % 4
                ppw = pp_a if bi == 0 else pp_b
                w = [("ve", last_fin[0])]
                if j >= 2:
                    w.append(("ve", ve_x1[j - 2]))
                if j == 0:
                    w.append(("sg", d_wo8))
                for half in range(2):
                    for hpj in range(HP):
                        emit("tensor", "matmul", ppw[:, half * 512:(half + 1) * 512],
                             attn[:, hpj, j * 128:(j + 1) * 128],
                             wo8[:, hpj, half * 512:(half + 1) * 512],
                             start=(hpj == 0), stop=(hpj == 3),
                             wait=(w if (half == 0 and hpj == 0) else ()),
                             inc=("pe" if (half == 1 and hpj == 3) else None))
                pv_ = val("pe")
                wd = [("ve", ve_x1.get(j - 4, 0))]
                if j in (2, 3):
                    wd.append(("ve", wq_done_ve))
                    wd.append(("ac", wq_done_ac))
                    wd.append(("sst0", wq_done_so[0]))
                    wd.append(("sst1", wq_done_so[1]))
                sx = s25[bi4]
                dv = emit("sync", "dma_start", x25[bi4][:],
                          x_in[j * 128:(j + 1) * 128, :], wait=wd, inc=sx)
                emit("vector", "tensor_copy", stgiv[:, bi:bi + 1], stat_iv[:, j:j + 1],
                     wait=[("ac", cnt.get("ac_n25_%d" % (j - 2), 0))])
                emit("vector", "tensor_copy", stgb1[:, bi:bi + 1], stat_b1[:, j:j + 1],
                     inc="ve")
                stg_ve = val("ve")
                emit("scalar", "activation", n25b[bi][:], x25[bi4][:], AF.Identity,
                     bias=stgb1[:, bi:bi + 1], scale=stgiv[:, bi:bi + 1],
                     wait=[(sx, dv), ("ve", ve_add.get(j - 2, 0)), ("ve", stg_ve),
                           ("sg", d_wo8), ("pe", cnt["pe_last_kq"])], inc="ac")
                cnt["ac_n25_%d" % j] = val("ac")
                av = val("ac")
                emit("vector", "tensor_add", u25b[bi][:], x25[bi4][:], n25b[bi][:],
                     wait=[("ac", av)], inc="ve")
                ve_add[j] = val("ve")
                emit("vector", "scalar_tensor_tensor", x1p[bi][:], u25b[bi][:], 0.5,
                     ppw[:, 0:1024], OP.mult, OP.add,
                     wait=[("pe", pv_), ("sci%d" % bi, so_x1.get(j - 2, 0)),
                           ("sst0", wq_done_so[0]), ("sst1", wq_done_so[1]),
                           ("ve", wq_done_ve)], inc="ve")
                ve_x1[j] = val("ve")
                cr = ((j % 16) // 4) * 1024 + (j // 16) * 512 + (j % 4) * 128
                emit("sync", "dma_start", cin[cr:cr + 128, :], x1p[bi][:],
                     wait=[("ve", ve_x1[j])], inc="sci%d" % bi)
                if j == 15:
                    emit("sync", "dma_start", ag_in[0:1, :], x1p[bi][127:128, :],
                         inc="sci%d" % bi)
                if j == 16:
                    emit("sync", "dma_start", ag_in[1:2, :], x1p[bi][0:1, :],
                         inc="sci%d" % bi)
                so_x1[j] = val("sci%d" % bi)

            phase_mark(9)
            emit("gpsimd", "collective_compute", "AllGather", OP.bypass,
                 replica_groups=GROUPS, ins=[ag_in], outs=[ag_out],
                 wait=[("sci1", so_x1[15]), ("sci0", so_x1[16])], inc="co")
            for k in range(4):
                emit("gpsimd", "collective_compute", "ReduceScatter", OP.add,
                     replica_groups=GROUPS,
                     ins=[cin[k * 1024:(k + 1) * 1024, :]],
                     outs=[rs_pad[1 + k * 512:1 + (k + 1) * 512, :]],
                     wait=[("sci0", so_x1[18 + 4 * k]),
                           ("sci1", so_x1[19 + 4 * k])],
                     inc="co")
            dv = emit("sync", "dma_start", agsb[:], ag_out,
                      wait=[("co", 1), ("ve", wq_done_ve), ("ac", wq_done_ac)],
                      inc="sd")
            emit("tensor", "matmul", pp4[0:2, 0:512], wselb[:], agsb[:, 0:512],
                 start=True, stop=True,
                 wait=[("sd", dv), ("ac", exp_done[-1]), ("ve", last_fin[0])])
            emit("tensor", "matmul", pp5[0:2, 0:512], wselb[:], agsb[:, 512:1024],
                 start=True, stop=True, inc="pe")
            pv_ = val("pe")
            emit("vector", "tensor_copy", x1halo[:, 0:512], pp4[0:2, 0:512],
                 wait=[("pe", pv_), ("ac", wq_done_ac)])
            emit("vector", "tensor_copy", x1halo[:, 512:1024], pp5[0:2, 0:512],
                 inc="ve")
            vv = val("ve")
            emit("sync", "dma_start", rs_pad[0:1, :], x1halo[0:1, :],
                 wait=[("ve", vv)], inc="so")
            emit("sync", "dma_start", rs_pad[T + 1:T + 2, :], x1halo[1:2, :],
                 inc="so")
            so_halo = val("so")
            co_base = val("co") - 4

            finalize_debug(x0t[0][:])
            if not mute.get("done"):
                bv_endA = {s_: val(s_) for s_ in ("sd", "sg", "so", "pe", "ve", "ac", "gp", "sdx0", "sdx1", "swq0", "swq1", "sst0", "sst1", "sci0", "sci1", "sgk0", "sgk1")}
                for eng in ENGS:
                    waitonly(eng, *[(s_, v) for s_, v in bv_endA.items() if v > 0])
            run_block()

        with ExitStack() as fctx:
            ef = fctx.enter_context
            w1q = ef(nc.sbuf_tensor("w1q", [128, 8, DH], BF16))
            w2q = ef(nc.sbuf_tensor("w2q", [128, 32, D], BF16))
            diag = ef(nc.sbuf_tensor("diag", [128, 32, 4, 64], BF16))
            win = [ef(nc.sbuf_tensor("win%d" % i, [128, D], F32)) for i in range(2)]
            x1o = win
            n3 = ef(nc.sbuf_tensor("n3", [128, D], F32))
            sq3 = ef(nc.sbuf_tensor("sq3", [128, D], F32))
            t3 = sq3
            x2b = [n3, sq3]
            xqt = ef(nc.sbuf_tensor("xqt", [128, D], BF16))
            xqf = ef(nc.sbuf_tensor("xqf", [128, 8, 256], BF16))
            h1f = ef(nc.sbuf_tensor("h1f", [128, 32, 256], BF16))
            h3 = ef(nc.sbuf_tensor("h3", [128, 32, 254], BF16))
            habs = ef(nc.sbuf_tensor("habs", [128, 254], F32))
            hacc = ef(nc.sbuf_tensor("hacc", [128, 254], F32))
            h3q = ef(nc.sbuf_tensor("h3q", [128, 2, 254], BF16))
            scol = ef(nc.sbuf_tensor("scol", [128, 16], F32))
            c2c = ef(nc.sbuf_tensor("c2c", [128, 6], F32))
            stiv = ef(nc.sbuf_tensor("stiv", [128, 2], F32))
            stb2 = ef(nc.sbuf_tensor("stb2", [128, 2], F32))
            sts1 = ef(nc.sbuf_tensor("sts1", [128, 2], F32))
            e2c = ef(nc.sbuf_tensor("e2c", [128, 2], F32))
            xqtf = xqt.ap().bitcast(F32)
            bc1 = xqtf[:, 0:256]
            bc2 = xqtf[:, 256:512]
            r1row = n3.ap()[0:1, 0:256]
            s2row = n3.ap()[0:1, 256:512]

            d1 = emit("gpsimd", "dma_start", w1q[:],
                      w1q_h.rearrange("(c p) n -> p c n", p=128), inc="sgk0")
            d2 = emit("gpsimd", "dma_start", w2q[:],
                      w2q_h.rearrange("(c p) n -> p c n", p=128), inc="sgk1")
            d3 = emit("gpsimd", "dma_start", diag[:], diag_in, inc="sg")
            emit("vector", "memset", scol[:], 0.0)
            emit("vector", "memset", c2c[:], 0.0, inc="ve")
            ve_diag = val("ve")
            waitonly("tensor", ("ve", ve_diag))

            prev = {"epi_ve": 0, "outs": 0, "mm2_pe": 0, "xq_pe": 0}
            for ci, (c0, n) in enumerate(FFN_CHUNKS):
                phase_mark(10 + ci)
                Wn = n + 2
                ta = min(128, Wn)
                tb = Wn - ta
                na = min(128, n)
                nb = n - na
                wd = [("ve", prev["epi_ve"]), ("so", prev["outs"]),
                      ("co", co_base + [0, 0, 1, 1, 2, 2, 3, 3, 3][ci] + 1)]
                if ci in (0, 8):
                    wd.append(("so", so_halo))
                dva = emit("sync", "dma_start", win[0][0:ta, :],
                           rs_pad[c0:c0 + ta, :], wait=wd, inc="sd")
                if tb > 0:
                    emit("sync", "dma_start", win[1][0:tb, :],
                         rs_pad[c0 + ta:c0 + Wn, :], inc="sd")
                dva = dvb = val("sd")
                submark(0 + 1000 * ci)
                tiles = [(0, ta, dva)] + ([(1, tb, dvb)] if tb > 0 else [])
                ve_sq_prev = 0
                for wi, rows, dv in tiles:
                    o = 5 * wi
                    wt = win[wi]
                    emit("scalar", "square", sq3[0:rows, :], wt[0:rows, :],
                         wait=[("sd", dv), ("ve", ve_sq_prev),
                               ("so", prev["outs"])], inc="ac")
                    av = val("ac")
                    emit("vector", "tensor_reduce", scol[0:rows, o:o + 1],
                         wt[0:rows, :], AX, OP.add, wait=[("sd", dv)])
                    emit("vector", "tensor_reduce", scol[0:rows, o + 1:o + 2],
                         sq3[0:rows, :], AX, OP.add, wait=[("ac", av)], inc="ve")
                    ve_sq_prev = val("ve")
                    emit("vector", "tensor_scalar", scol[0:rows, o:o + 1],
                         scol[0:rows, o:o + 1], 1.0 / D, None, OP.mult)
                    emit("vector", "tensor_scalar", scol[0:rows, o + 1:o + 2],
                         scol[0:rows, o + 1:o + 2], 1.0 / D, None, OP.mult)
                    emit("vector", "tensor_mul", scol[0:rows, o + 2:o + 3],
                         scol[0:rows, o:o + 1], scol[0:rows, o:o + 1])
                    emit("vector", "tensor_sub", scol[0:rows, o + 1:o + 2],
                         scol[0:rows, o + 1:o + 2], scol[0:rows, o + 2:o + 3])
                    emit("vector", "tensor_scalar", scol[0:rows, o + 2:o + 3],
                         scol[0:rows, o + 1:o + 2], EPS, None, OP.add, inc="ve")
                    vv = val("ve")
                    emit("scalar", "sqrt", scol[0:rows, o + 3:o + 4],
                         scol[0:rows, o + 2:o + 3], wait=[("ve", vv)], inc="ac")
                    av = val("ac")
                    emit("vector", "reciprocal", stiv[0:rows, wi:wi + 1],
                         scol[0:rows, o + 3:o + 4], wait=[("ac", av)])
                    for _ in range(2):
                        emit("vector", "tensor_mul", scol[0:rows, o + 3:o + 4],
                             stiv[0:rows, wi:wi + 1], stiv[0:rows, wi:wi + 1])
                        emit("vector", "tensor_mul", scol[0:rows, o + 3:o + 4],
                             scol[0:rows, o + 3:o + 4], scol[0:rows, o + 2:o + 3])
                        emit("vector", "tensor_scalar", scol[0:rows, o + 3:o + 4],
                             scol[0:rows, o + 3:o + 4], -0.5, 1.5, OP.mult, OP.add)
                        emit("vector", "tensor_mul", stiv[0:rows, wi:wi + 1],
                             stiv[0:rows, wi:wi + 1], scol[0:rows, o + 3:o + 4])
                    emit("vector", "scalar_tensor_tensor", stb2[0:rows, wi:wi + 1],
                         scol[0:rows, o:o + 1], -1.0, stiv[0:rows, wi:wi + 1],
                         OP.mult, OP.mult, inc="ve")

                submark(5 + 1000 * ci)
                ve_stats3 = val("ve")
                ve_nprev = 0
                for wi, rows, dv in tiles:
                    o = 5 * wi
                    wt = win[wi]
                    emit("scalar", "activation", n3[0:rows, :], wt[0:rows, :],
                         AF.Identity, bias=stb2[0:rows, wi:wi + 1],
                         scale=stiv[0:rows, wi:wi + 1],
                         wait=[("ve", ve_stats3), ("ve", ve_nprev)], inc="ac")
                    av = val("ac")
                    emit("vector", "tensor_reduce", scol[0:rows, 10 + wi:11 + wi],
                         n3[0:rows, :], AX, OP.max, apply_absolute_value=True,
                         wait=[("ac", av)])
                    emit("vector", "tensor_scalar", scol[0:rows, 10 + wi:11 + wi],
                         scol[0:rows, 10 + wi:11 + wi], 1e-5, None, OP.max)
                    emit("vector", "reciprocal", sts1[0:rows, wi:wi + 1],
                         scol[0:rows, 10 + wi:11 + wi])
                    emit("vector", "tensor_scalar", sts1[0:rows, wi:wi + 1],
                         sts1[0:rows, wi:wi + 1], 127.0, None, OP.mult)
                    submark(6 + 1000 * ci)
                    emit("vector", "scalar_tensor_tensor", scol[0:rows, 14 + wi:15 + wi],
                         scol[0:rows, 10 + wi:11 + wi], 1.0 / 127.0,
                         swr_bc[0:rows, 0:1], OP.mult, OP.mult)
                    submark(7 + 1000 * ci)
                    emit("vector", "tensor_scalar", t3[0:rows, :], n3[0:rows, :],
                         sts1[0:rows, wi:wi + 1], MAGIC, OP.mult, OP.add)
                    submark(8 + 1000 * ci)
                    emit("vector", "tensor_scalar", xqt[0:rows, :], t3[0:rows, :],
                         MAGIC, None, OP.subtract,
                         wait=[("pe", prev["xq_pe"])], inc="ve")
                    submark(9 + 1000 * ci)
                    ve_nprev = val("ve")
                    submark(10 + 1000 * ci)
                    for c in range(8):
                        pv8 = pp4 if c % 2 == 0 else pp5
                        w = [("ve", ve_nprev)]
                        key = "xqcp_%d" % (c % 2)
                        if key in cnt:
                            w.append(("ve", cnt[key]))
                        if ci > 0 and wi == 0 and c < 2:
                            w.append(("pe", prev["mm2_pe"]))
                        emit("tensor", "matmul", pv8[:, 0:rows],
                             xqt[0:rows, c * 128:(c + 1) * 128], idb[0:rows, 0:rows],
                             start=True, stop=True, wait=w, inc="pe")
                        pv_ = val("pe")
                        emit("vector", "tensor_copy",
                             xqf[:, c, wi * 128:wi * 128 + rows], pv8[:, 0:rows],
                             wait=[("pe", pv_)], inc="ve")
                        cnt[key] = val("ve")
                    prev["xq_pe"] = val("pe")
                submark(20 + 1000 * ci)
                ve_xq = val("ve")
                ac_n3 = val("ac")
                dxa = emit("sync", "dma_start", x1o[0][0:na, :],
                           rs_pad[c0 + 1:c0 + 1 + na, :],
                           wait=[("ac", ac_n3), ("ve", ve_xq)], inc="sd")
                if nb > 0:
                    emit("sync", "dma_start", x1o[1][0:nb, :],
                         rs_pad[c0 + 1 + na:c0 + 1 + n, :], inc="sd")
                dxa = dxb = val("sd")

                emit("tensor", "matmul", pp6[0:1, 0:128], scol[:, 14:15], idf[:],
                     start=True, stop=True, wait=[("ve", ve_xq)])
                emit("tensor", "matmul", pp6[0:1, 128:256], scol[:, 15:16], idf[:],
                     start=True, stop=True, inc="pe")
                pv_ = val("pe")
                emit("vector", "tensor_copy", r1row[0:1, 0:256], pp6[0:1, 0:256],
                     wait=[("pe", pv_)], inc="ve")
                vv = val("ve")
                emit("tensor", "matmul", pp_b[:, 0:Wn], onesr[0:1, :],
                     r1row[0:1, 0:Wn], start=True, stop=True, wait=[("ve", vv)],
                     inc="pe")
                pv_ = val("pe")
                emit("vector", "tensor_copy", bc1[:, 0:Wn], pp_b[:, 0:Wn],
                     wait=[("pe", pv_)], inc="ve")
                ve_bc1 = val("ve")

                submark(30 + 1000 * ci)
                e1 = {}
                s3 = {}
                a3 = {}
                for cc in range(32):
                    ppm = pp4 if cc % 2 == 0 else pp5
                    w = [("ve", ve_bc1)]
                    if ci == 0 and cc == 0:
                        w.append(("sgk0", d1))
                    if cc < 2:
                        w.append(("ve", cnt.get("xqcp_%d" % (cc % 2), 0)))
                    if cc >= 2:
                        w.append(("ve", e1[cc - 2]))
                    for ch in range(8):
                        emit("tensor", "matmul", ppm[:, 0:Wn],
                             w1q[:, ch, cc * 128:(cc + 1) * 128], xqf[:, ch, 0:Wn],
                             start=(ch == 0), stop=(ch == 7),
                             wait=(w if ch == 0 else ()),
                             inc=("pe" if ch == 7 else None))
                    pv_ = val("pe")
                    emit("vector", "tensor_mul", h1f[:, cc, 0:Wn], ppm[:, 0:Wn],
                         bc1[:, 0:Wn], wait=[("pe", pv_)], inc="ve")
                    e1[cc] = val("ve")
                submark(40 + 1000 * ci)
                for cc in range(32):
                    ppc = pp6 if cc % 2 == 0 else pp7
                    w = [("ve", e1[cc])]
                    if ci == 0 and cc == 0:
                        w.append(("sg", d3))
                    if cc >= 2:
                        w.append(("ac", s3[cc - 2]))
                    for tap in range(3):
                        emit("tensor", "matmul", ppc[0:64, 0:n],
                             diag[0:64, cc, tap, :], h1f[0:64, cc, tap:tap + n],
                             start=(tap == 0), stop=False,
                             tile_position=(0, 0), wait=(w if tap == 0 else ()))
                    emit("tensor", "matmul", ppc[0:64, 0:n],
                         diag[0:64, cc, 3, :], onesb[0:64, 0:n],
                         start=False, stop=True, tile_position=(0, 0))
                    for tap in range(3):
                        emit("tensor", "ldweights", diag[64:128, cc, tap, :],
                             tile_position=(64, 64))
                        emit("tensor", "matmul", ppc[64:128, 0:n],
                             diag[64:128, cc, tap, :], h1f[64:128, cc, tap:tap + n],
                             start=(tap == 0), stop=False,
                             tile_position=(64, 64))
                    emit("tensor", "ldweights", diag[64:128, cc, 3, :],
                         tile_position=(64, 64))
                    emit("tensor", "matmul", ppc[64:128, 0:n],
                         diag[64:128, cc, 3, :], onesb[64:128, 0:n],
                         start=False, stop=True,
                         tile_position=(64, 64), inc="pe")
                    cvv = val("pe")
                    emit("scalar", "activation", h3[:, cc, 0:n], ppc[:, 0:n], AF.Silu,
                         bias=0.0, scale=1.0, wait=[("pe", cvv)])
                    emit("scalar", "activation", habs[:, 0:n], h3[:, cc, 0:n], AF.Abs,
                         wait=([("ve", a3[cc - 1])] if cc >= 1 else ()), inc="ac")
                    s3[cc] = val("ac")
                    if cc == 0:
                        emit("vector", "tensor_copy", hacc[:, 0:n], habs[:, 0:n],
                             wait=[("ac", s3[cc])], inc="ve")
                    else:
                        emit("vector", "tensor_max", hacc[:, 0:n], hacc[:, 0:n],
                             habs[:, 0:n], wait=[("ac", s3[cc])], inc="ve")
                    a3[cc] = val("ve")

                submark(50 + 1000 * ci)
                emit("tensor", "matmul", pp_a[0:na, 0:128], hacc[:, 0:na], idf[:],
                     start=True, stop=True,
                     wait=[("ve", a3[31]), ("pe", prev["mm2_pe"])],
                     inc=(None if nb > 0 else "pe"))
                if nb > 0:
                    emit("tensor", "matmul", pp_a[0:nb, 512:640], hacc[:, na:n],
                         idf[:], start=True, stop=True, inc="pe")
                pv_ = val("pe")
                emit("vector", "tensor_reduce", c2c[0:na, 0:1], pp_a[0:na, 0:128],
                     AX, OP.max, wait=[("pe", pv_)])
                if nb > 0:
                    emit("vector", "tensor_reduce", c2c[0:nb, 1:2],
                         pp_a[0:nb, 512:640], AX, OP.max)
                emit("vector", "tensor_scalar", c2c[:, 0:2], c2c[:, 0:2], 1e-5, None,
                     OP.max)
                emit("vector", "reciprocal", c2c[:, 2:4], c2c[:, 0:2])
                emit("vector", "tensor_scalar", c2c[:, 2:4], c2c[:, 2:4], 127.0, None,
                     OP.mult)
                emit("vector", "scalar_tensor_tensor", e2c[:, 0:1], c2c[:, 0:1],
                     1.0 / 127.0, swr_bc[:, 1:2], OP.mult, OP.mult)
                emit("vector", "scalar_tensor_tensor", e2c[:, 1:2], c2c[:, 1:2],
                     1.0 / 127.0, swr_bc[:, 1:2], OP.mult, OP.mult, inc="ve")
                vv = val("ve")
                emit("tensor", "matmul", pp6[0:1, 0:128], c2c[:, 2:3], idf[:],
                     start=True, stop=True, wait=[("ve", vv)],
                     inc=(None if nb > 0 else "pe"))
                if nb > 0:
                    emit("tensor", "matmul", pp6[0:1, 128:256], c2c[:, 3:4], idf[:],
                         start=True, stop=True, inc="pe")
                pv_ = val("pe")
                emit("vector", "tensor_copy", s2row[0:1, 0:256], pp6[0:1, 0:256],
                     wait=[("pe", pv_)], inc="ve")
                vv = val("ve")
                emit("tensor", "matmul", pp_b[:, 256:256 + n], onesr[0:1, :],
                     s2row[0:1, 0:n], start=True, stop=True, wait=[("ve", vv)],
                     inc="pe")
                pv_ = val("pe")
                emit("vector", "tensor_copy", bc2[:, 0:n], pp_b[:, 256:256 + n],
                     wait=[("pe", pv_)], inc="ve")
                ve_bc2 = val("ve")

                submark(60 + 1000 * ci)
                qv = {}
                for cc in range(32):
                    w = [("ve", ve_bc2)]
                    if cc >= 2:
                        w.append(("pe", qv[cc - 2]))
                    emit("vector", "tensor_mul", habs[:, 0:n], h3[:, cc, 0:n],
                         bc2[:, 0:n], wait=w, inc="ve")
                    vv0 = val("ve")
                    emit("scalar", "activation", hacc[:, 0:n], habs[:, 0:n],
                         AF.Identity, bias=magicc[:], scale=1.0, wait=[("ve", vv0)],
                         inc="ac")
                    av = val("ac")
                    emit("vector", "tensor_scalar", h3q[:, cc % 2, 0:n], hacc[:, 0:n],
                         MAGIC, None, OP.subtract, wait=[("ac", av)], inc="ve")
                    vvq = val("ve")
                    mtl = [(0, na)] + ([(na, nb)] if nb else [])
                    for mi, (ms, mw) in enumerate(mtl):
                        ppo = pp_a if mi == 0 else pp_b
                        for half in range(2):
                            wq2 = [("ve", vvq)] if (mi == 0 and half == 0) else []
                            if ci == 0 and cc == 0 and mi == 0 and half == 0:
                                wq2.append(("sgk1", d2))
                            emit("tensor", "matmul",
                                 ppo[0:mw, half * 512:(half + 1) * 512],
                                 h3q[:, cc % 2, ms:ms + mw],
                                 w2q[:, cc, half * 512:(half + 1) * 512],
                                 start=(cc == 0), stop=(cc == 31),
                                 wait=wq2,
                                 inc=("pe" if (mi == len(mtl) - 1 and half == 1) else None))
                    qv[cc] = val("pe")
                prev["mm2_pe"] = qv[31]

                submark(70 + 1000 * ci)
                for mi, (ms, mw) in enumerate([(0, na)] + ([(na, nb)] if nb else [])):
                    ppo = pp_a if mi == 0 else pp_b
                    ecol = e2c[0:mw, mi:mi + 1]
                    xo = x1o[mi]
                    dv = dxa if mi == 0 else dxb
                    emit("vector", "scalar_tensor_tensor", x2b[mi][0:mw, 0:512],
                         ppo[0:mw, 0:512], ecol, xo[0:mw, 0:512], OP.mult, OP.add,
                         wait=[("pe", prev["mm2_pe"]), ("sd", dv)])
                    emit("vector", "scalar_tensor_tensor", x2b[mi][0:mw, 512:1024],
                         ppo[0:mw, 512:1024], ecol, xo[0:mw, 512:1024],
                         OP.mult, OP.add, inc="ve")
                    vv = val("ve")
                    emit("sync", "dma_start", out[c0 + ms:c0 + ms + mw, :],
                         x2b[mi][0:mw, :], wait=[("ve", vv)], inc="so")
                prev["epi_ve"] = val("ve")
                prev["outs"] = val("so")

            phase_mark(98)
            if dbg and not mute["on"]:
                emit("sync", "dma_start", dbg_cin, cin, inc="so")
                emit("sync", "dma_start", dbg_rs, rs_pad, inc="so")
            finalize_debug(rs_pad[1:129, :])
            waitonly("sync", ("so", val("so")))
            run_block()
    return nc


def kernel(L0_attn_x, L0_attn_ln_w, L0_attn_ln_b, L0_attn_wq, L0_attn_wk,
           L0_attn_wv, L0_attn_wo, L0_ffn_w1, L0_ffn_conv_w, L0_ffn_conv_b,
           L0_ffn_w2):
    x = np.ascontiguousarray(np.asarray(L0_attn_x, dtype=np.float32))
    lnwv = np.asarray(L0_attn_ln_w, np.float32)
    lnbv = np.asarray(L0_attn_ln_b, np.float32)
    wqa = np.asarray(L0_attn_wq, np.float32)
    wka = np.asarray(L0_attn_wk, np.float32)
    wva = np.asarray(L0_attn_wv, np.float32)
    woa = np.asarray(L0_attn_wo, np.float32)
    w1 = np.asarray(L0_ffn_w1, np.float32)
    cw = np.asarray(L0_ffn_conv_w, np.float32)
    cb = np.asarray(L0_ffn_conv_b, np.float32)
    w2 = np.asarray(L0_ffn_w2, np.float32)

    lnw_bc = np.ascontiguousarray(np.broadcast_to(lnwv, (128, D)))
    lnb_bc = np.ascontiguousarray(np.broadcast_to(lnbv, (128, D)))
    w1T = np.ascontiguousarray(w1.T)
    w2T = np.ascontiguousarray(w2.T)
    cwr = cw[:, 0, :].reshape(32, 128, 3).transpose(1, 0, 2)
    cbr = cb.reshape(32, 128).T[:, :, None]
    cw4 = np.concatenate([cwr, cbr], axis=2)
    diag_host = np.zeros((128, 32, 4, 64), np.float32)
    for p in range(128):
        diag_host[p, :, :, p % 64] = cw4[p]

    import os
    _upto = int(os.environ.get("KERNEL_UPTO", "99"))
    _ksub = int(os.environ.get("KERNEL_KSUB", "9999"))
    _dbg = os.environ.get("KERNEL_DBG", "") == "1"
    if _NC_CACHE[0] is None:
        _NC_CACHE[0] = build_program(_upto, _ksub, _dbg)
    nc = _NC_CACHE[0]
    kernel._DBG_RES = [None]

    def pack(w8):
        wp = w8.reshape(4, 2, D, E).transpose(0, 2, 1, 3).reshape(4, D, 128)
        return np.ascontiguousarray(wp.reshape(4, 8, 128, 128).transpose(2, 0, 1, 3))

    in_maps = []
    for c in range(8):
        b, r = c // 2, c % 2
        hsl = slice(8 * r, 8 * r + 8)
        wv8 = wva[hsl].transpose(1, 0, 2).reshape(D, 512)
        wv_sb = np.ascontiguousarray(wv8.reshape(8, 128, 512).transpose(1, 0, 2))
        wo8 = woa[512 * r:512 * (r + 1), :]
        wo_sb = np.ascontiguousarray(wo8.reshape(4, 128, D).transpose(1, 0, 2))
        if r == 0:
            wsel = np.array([[0, 0], [0, 1], [0, 0], [0, 1]], np.float32)
        else:
            wsel = np.array([[1, 0], [0, 0], [1, 0], [0, 0]], np.float32)
        in_maps.append({
            "x": np.ascontiguousarray(x[b]),
            "wqp": pack(wqa[hsl]), "wkp": pack(wka[hsl]),
            "wv8": wv_sb, "wo8": wo_sb,
            "lnw_bc": lnw_bc, "lnb_bc": lnb_bc,
            "w1T": w1T, "w2T": w2T,
            "diag_in": diag_host,
            "wsel": wsel,
        })
    res = run_bass_kernel_spmd(nc, in_maps, core_ids=list(range(8)))
    import kernel as _k
    _k._DBG_RES = res
    y = np.empty((4, S, D), np.float32)
    for c in range(8):
        b, r = c // 2, c % 2
        y[b, r * T:(r + 1) * T, :] = res.results[c]["out"]
    return y
